# version 23
# baseline (speedup 1.0000x reference)
import os
import numpy as np
import ml_dtypes
from contextlib import ExitStack

import concourse.bass as bass
import concourse.mybir as mybir
from concourse.bass import IndirectOffsetOnAxis
from concourse.bass_utils import run_bass_kernel_spmd

F32 = mybir.dt.float32
BF16 = mybir.dt.bfloat16
U32 = mybir.dt.uint32
AF = mybir.ActivationFunctionType
ALU = mybir.AluOpType
AXX = mybir.AxisListType.X

B, S, H, NH, HD = 2, 4096, 2048, 16, 128
FH = 5632
TTK = 512
NT = S // TTK
NKC = H // 128
NHC = 4
TOK2 = 1024
CSC = float(1.0 / np.sqrt(np.float32(HD)))
EPS_LN = 1e-5
NW = 2048 * 5632

ENGS = ("tensor", "vector", "scalar", "sync", "gpsimd")


class MK:

    def __init__(self, nc, sems):
        self.nc = nc
        self.sems = sems
        self.cnt = {k: 0 for k in sems}
        self.plan = {e: [] for e in ENGS}
        self.last_w = {}
        self.readers = {}
        self.hw = {e: {} for e in ENGS}
        self.used = {}
        self.own = {"tensor": "T", "vector": "V", "scalar": "S",
                    "sync": None, "gpsimd": None}

    def op(self, q, fn, reads=(), writes=(), sem=None, inc=None, waits=()):
        deps = {}

        def add(s, v):
            if deps.get(s, 0) < v:
                deps[s] = v
        for b in reads:
            lw = self.last_w.get(b)
            if lw is not None:
                add(*lw)
        for b in writes:
            lw = self.last_w.get(b)
            if lw is not None:
                add(*lw)
            for s, v in self.readers.get(b, {}).items():
                add(s, v)
        for s, v in waits:
            add(s, v)
        wl = []
        for s, v in deps.items():
            if v <= self.hw[q].get(s, 0):
                continue
            wl.append((s, v))
            self.hw[q][s] = v
        if sem is None:
            sem = {"tensor": "T", "vector": "V", "scalar": "S",
                   "sync": "dS", "gpsimd": "dG"}[q]
            if inc is None:
                inc = 16 if q in ("sync", "gpsimd") else 1
        if inc is None:
            inc = 1
        self.cnt[sem] += inc
        val = self.cnt[sem]
        rec = {"q": q, "fn": fn, "wl": tuple(wl), "sem": sem, "val": val,
               "inc": inc}
        self.plan[q].append(rec)
        for s, v in wl:
            self.used.setdefault(s, set()).add(v)

        mark = (sem, val)
        for b in writes:
            self.last_w[b] = mark
            self.readers.pop(b, None)
        for b in reads:
            d = self.readers.setdefault(b, {})
            if d.get(sem, 0) < val:
                d[sem] = val
        return mark

    def setmark(self, buf, sem, val=None):
        if val is None:
            val = self.cnt[sem]
        self.last_w[buf] = (sem, val)
        self.readers.pop(buf, None)

    def wait_only(self, q, waits=()):
        self.plan[q].append({"q": q, "fn": None, "wl": tuple(waits),
                             "sem": None, "val": 0, "inc": 0})
        for s, v in waits:
            self.used.setdefault(s, set()).add(v)

    def finalize_plan(self):
        prunable = {"T", "V", "S"}
        remap = {}
        for s in prunable:
            used = self.used.get(s, set())
            kept = 0
            m = {0: 0}
            recs = []
            for qn in ENGS:
                for r in self.plan[qn]:
                    if r["sem"] == s:
                        recs.append(r)
            recs.sort(key=lambda r: r["val"])
            for r in recs:
                if r["val"] in used:
                    kept += 1
                    r["newinc"] = r["val"] - (r["val"] - kept)
                    r["keep"] = True
                else:
                    r["keep"] = False
                m[r["val"]] = kept
            remap[s] = m
        sems = self.sems
        out = {}
        for qn in ENGS:
            fns = []
            for r in self.plan[qn]:
                wl = []
                for s, v in r["wl"]:
                    if s in remap:
                        v = remap[s][v]
                        if v == 0:
                            continue
                    wl.append((s, v))
                fn, s_, inc = r["fn"], r["sem"], r["inc"]
                keep = r.get("keep", True) if s_ in ("T", "V", "S") else True

                def run(eng, wl=tuple(wl), fn=fn, s_=s_, inc=inc, keep=keep):
                    for s, v in wl:
                        eng.wait_ge(sems[s], v)
                    if fn is None:
                        return
                    inst = fn(eng)
                    if keep and s_ is not None:
                        inst.then_inc(sems[s_], inc)
                fns.append(run)
            out[qn] = fns
        self.plan = out


def build_program():
    STOP = int(os.environ.get("MKSTOP", "99"))
    MKTILES = int(os.environ.get("MKTILES", "8"))
    MKATT = int(os.environ.get("MKATT", "1"))
    MKAG = int(os.environ.get("MKAG", "1"))
    MKSUB = int(os.environ.get("MKSUB", "7"))
    MKS2 = int(os.environ.get("MKS2", "31"))
    MKS3 = int(os.environ.get("MKS3", "7"))
    nc = bass.Bass("TRN2", target_bir_lowering=False, debug=False)

    x_t = nc.dram_tensor("x_t", [NT, 128, NKC * 512], F32, kind="ExternalInput").ap()
    xp2_fm = nc.dram_tensor("xp2_fm", [H, TOK2], F32, kind="ExternalInput").ap()
    wq_t = nc.dram_tensor("wq_t", [128, NKC * 512], F32, kind="ExternalInput").ap()
    wk_t = nc.dram_tensor("wk_t", [128, NKC * 512], F32, kind="ExternalInput").ap()
    wv_t = nc.dram_tensor("wv_t", [128, NKC * 512], F32, kind="ExternalInput").ap()
    wo_t = nc.dram_tensor("wo_t", [16, 128, 2048], F32, kind="ExternalInput").ap()
    w1_sl = nc.dram_tensor("w1_sl", [256, FH], F32, kind="ExternalInput").ap()
    w2_sl = nc.dram_tensor("w2_sl", [256, FH], F32, kind="ExternalInput").ap()
    w3_sl = nc.dram_tensor("w3_sl", [768, H], F32, kind="ExternalInput").ap()
    rope2 = nc.dram_tensor("rope2", [2, 128, S], BF16, kind="ExternalInput").ap()
    masks = nc.dram_tensor("masks", [4, 128, 512], BF16, kind="ExternalInput").ap()
    gidx = nc.dram_tensor("gidx", [128, 16], U32, kind="ExternalInput").ap()
    out_fm = nc.dram_tensor("out_fm", [H, TOK2], F32, kind="ExternalOutput").ap()

    g_in = nc.dram_tensor("g_in", [1, 4], F32, kind="Internal")
    g_out = nc.dram_tensor("g_out", [1, 4], F32, kind="Internal")
    w1q_in = nc.dram_tensor("w1q_in", [256, FH], BF16, kind="Internal")
    w2q_in = nc.dram_tensor("w2q_in", [256, FH], BF16, kind="Internal")
    w3q_in = nc.dram_tensor("w3q_in", [704, H], BF16, kind="Internal")
    w1q_all = nc.dram_tensor("w1q_all", [H, FH], BF16, kind="Internal")
    w2q_all = nc.dram_tensor("w2q_all", [H, FH], BF16, kind="Internal")
    w3q_all = nc.dram_tensor("w3q_all", [FH, H], BF16, kind="Internal")
    cag_in = [nc.dram_tensor(f"cag_in{k}", [512, 1024], BF16, kind="Internal")
              for k in range(4)]
    cag_out = nc.dram_tensor("cag_out", [8192, 1024], BF16, kind="Internal")
    res1_dram = nc.dram_tensor("res1_dram", [H, TOK2], F32, kind="Internal")

    sem_names = ["T", "V", "S", "dS", "dG", "dA",
                 "cG", "cW1", "cW2", "cW3", "cX0", "cX1", "cX2", "cX3",
                 "dI", "dIg", "dGi", "dGo", "dX0", "dX1", "dW0", "dW1", "dQ",
                 "dV0", "dV1", "dV2", "dV3", "dV4", "dV5", "dV6", "dV7",
                 "dRc1", "dRs1",
                 "dC0", "dC1", "dC2", "dC3", "dGt", "dO0", "dO1",
                 "dP0", "dP1", "dR", "dF0", "dF1", "dW3a", "dW3b",
                 "dL0", "dL1", "dA0", "dA1", "dRc", "dRs", "dQ1", "dW2", "dW3"]

    st = ExitStack()
    sems = {n: st.enter_context(nc.semaphore(n)) for n in sem_names}

    Q = [st.enter_context(nc.psum_tensor(f"Q{i}", [128, 1024], F32))
         for i in range(4)]

    def qb(i):
        return Q[i // 2][:, 512 * (i % 2):512 * (i % 2) + 512]

    def qrow(i):
        return Q[i // 2][0:1, 512 * (i % 2):512 * (i % 2) + 512]

    def bkey(i):
        return f"Qb{i}"

    mk = MK(nc, sems)
    sb = nc.sbuf_tensor

    S0 = ExitStack()
    ones_cb = S0.enter_context(sb("ones_cb", [128, 1], BF16))
    ones_cf = S0.enter_context(sb("ones_cf", [128, 1], F32))
    ones_rb = S0.enter_context(sb("ones_rb", [1, 128], BF16))
    ones_rf = S0.enter_context(sb("ones_rf", [1, 128], F32))
    one1b = S0.enter_context(sb("one1b", [1, 1], BF16))
    zcol = S0.enter_context(sb("zcol", [128, 1], F32))
    epscol = S0.enter_context(sb("epscol", [128, 1], F32))
    thr = S0.enter_context(sb("thr", [128, 8], F32))
    gpart = S0.enter_context(sb("gpart", [128, 4], F32))
    gred = S0.enter_context(sb("gred", [128, 1], F32))
    biascol = S0.enter_context(sb("biascol", [128, 8], F32))
    gidx_sb = S0.enter_context(sb("gidx_sb", [128, 16], U32))
    rows_f = S0.enter_context(sb("rows_f", [1, 2048], F32))
    rows_b = S0.enter_context(sb("rows_b", [1, 2048], BF16))
    negws = S0.enter_context(sb("negws", [1, 1536], BF16))
    rdenb_sb = [S0.enter_context(sb(f"rdenb_sb{i}", [128, 512], BF16))
                for i in range(2)]
    rcolv = S0.enter_context(sb("rcolv", [128, 4], F32))

    R_M = rows_f[:, 0:512]
    R_VAR = rows_f[:, 512:1024]
    R_T1 = rows_f[:, 1024:1536]
    R_G = rows_f[:, 1536:1540]
    R_TT = rows_f[:, 1540:1544]
    R_NT = rows_f[:, 1544:1548]
    R_KAP = rows_f[:, 1548:1552]
    R_QMX = rows_f[:, 1552:1553]
    R_KMX = rows_f[:, 1553:1554]
    R_PRD = rows_f[:, 1554:1555]
    R_SQT = rows_f[:, 1555:1556]
    R_SQP = rows_f[:, 1556:1557]
    R_M2A = rows_f[:, 0:1024]
    R_V2A = rows_f[:, 1024:2048]

    RB_M = rows_b[:, 0:512]
    RB_RST = rows_b[:, 512:1024]
    RB_DEN = rows_b[:, 1024:1536]
    RB_BV = rows_b[:, 1536:1537]
    RB_R2 = rows_b[:, 0:1024]
    RB_MR2 = rows_b[:, 1024:2048]

    def dma(q, out_ap, in_ap, reads=(), writes=(), sem=None, waits=()):
        return mk.op(q, lambda e, o=out_ap, i=in_ap: e.dma_start(out=o, in_=i),
                     reads=reads, writes=writes, sem=sem,
                     inc=16 if sem else None, waits=waits)

    for buf, valv in ((ones_cb, 1.0), (ones_cf, 1.0), (ones_rb, 1.0),
                      (ones_rf, 1.0), (one1b, 1.0), (zcol, 0.0),
                      (epscol, EPS_LN)):
        mk.op("vector", lambda e, b=buf, v=valv: e.memset(b[:, :], v),
              writes=["consts"])
    mk.op("vector", lambda e: e.memset(gpart[:, :], 0.0), writes=["gpart"])
    mk.op("vector", lambda e: e.memset(R_KAP, 0.0), writes=["kap"])
    dma("sync", gidx_sb[:, :], gidx, writes=["gidx_sb"], sem="dI")

    S1b = ExitStack()
    k_res = S1b.enter_context(sb("k_res", [128, NHC * S], BF16))
    v_sb = S1b.enter_context(sb("v_sb", [128, 32 * 512], BF16))
    S1c = ExitStack()
    q_sb = [S1c.enter_context(sb(f"q_sb{i}", [128, NHC * 512], BF16))
            for i in range(2)]
    masks_sb = S1c.enter_context(sb("masks_sb", [128, 4 * 512], BF16))
    dma("sync", masks_sb[:, :].rearrange("p (m s) -> p m s", m=4),
        masks.rearrange("m p s -> p m s"), writes=["masks"], sem="dI")
    probs = [S1c.enter_context(sb(f"probs{i}", [128, 1024], BF16)) for i in range(2)]
    tmpsw = S1c.enter_context(sb("tmpsw", [128, 512], BF16))
    tmpA = S1c.enter_context(sb("tmpA", [128, 512], BF16))
    sqs = tmpsw
    xsq = [S1c.enter_context(sb(f"xsq{i}", [128, 512], BF16)) for i in range(2)]
    ctxev = [S1c.enter_context(sb(f"ctxev{i}", [128, 512], BF16)) for i in range(8)]

    S1 = ExitStack()
    x_bf = [S1.enter_context(sb(f"x_bf{i}", [128, NKC * 512], BF16)) for i in range(2)]
    wq_sb = S1.enter_context(sb("wq_sb", [128, NKC * 512], BF16))
    wk_sb = S1.enter_context(sb("wk_sb", [128, NKC * 512], BF16))
    wv_sb = S1.enter_context(sb("wv_sb", [128, NKC * 512], BF16))
    rc_raw = [S1.enter_context(sb(f"rc_raw{i}", [128, TTK], BF16))
              for i in range(2)]
    rs_raw = [S1.enter_context(sb(f"rs_raw{i}", [128, TTK], BF16))
              for i in range(2)]

    for nm, dst, srcw in (("wq", wq_sb, wq_t), ("wk", wk_sb, wk_t),
                          ("wv", wv_sb, wv_t)):
        dma("gpsimd", dst[:, :], srcw, writes=[nm], sem="dIg")

    dma("sync", rc_raw[0][:, :], rope2[0, :, 0:TTK], writes=["rcraw0"],
        sem="dRc")
    dma("sync", rs_raw[0][:, :], rope2[1, :, 0:TTK], writes=["rsraw0"],
        sem="dRs")
    for _b in ("gidx_sb", "masks"):
        mk.setmark(_b, "dI")
    for _b in ("wq", "wk", "wv", "ropec_t", "ropes_t"):
        mk.setmark(_b, "dIg")

    def emit_xcast(t):
        par = t % 2
        dma("gpsimd", x_bf[par][:, :], x_t[t],
            writes=[f"xbf{par}"], sem=f"dX{par}")

    NWSL = 5
    wsl = [S1.enter_context(sb(f"wsl{i}", [128, 512], F32)) for i in range(NWSL)]
    qA = [S1.enter_context(sb(f"qA{i}", [128, 512], BF16)) for i in range(2)]
    qB = S1.enter_context(sb("qB", [128, 512], BF16))

    qpieces = []
    for a in range(2):
        for cidx in range(11):
            qpieces.append((0, w1_sl[128 * a:128 * a + 128,
                                     512 * cidx:512 * cidx + 512], 512,
                            w1q_in.ap()[128 * a:128 * a + 128,
                                        512 * cidx:512 * cidx + 512], 128))
    for a in range(2):
        for cidx in range(11):
            qpieces.append((1, w2_sl[128 * a:128 * a + 128,
                                     512 * cidx:512 * cidx + 512], 512,
                            w2q_in.ap()[128 * a:128 * a + 128,
                                        512 * cidx:512 * cidx + 512], 128))
    for a in range(6):
        rows = 64 if a == 5 else 128
        for cidx in range(4):
            qpieces.append((2, w3_sl[128 * a:128 * a + 128,
                                     512 * cidx:512 * cidx + 512], 512,
                            w3q_in.ap()[128 * a:128 * a + rows,
                                        512 * cidx:512 * cidx + 512], rows))
    NPIECE = len(qpieces)

    def gamma_piece(i):
        m, srcp, width, _, _ = qpieces[i]
        par = i % NWSL
        dma("sync", wsl[par][:, 0:width], srcp, writes=[f"wsl{par}"],
            sem=f"dV{par}")
        mk.op("vector",
              lambda e, par=par, w=width: e.reduce_sum(
                  gred[:, 0:1], wsl[par][:, 0:w], axis=AXX,
                  apply_absolute_value=True),
              reads=[f"wsl{par}"], writes=["gred"])
        mk.op("vector",
              lambda e, m=m: e.tensor_add(gpart[:, m:m + 1], gpart[:, m:m + 1],
                                          gred[:, 0:1]),
              reads=["gred", "gpart"], writes=["gpart"])

    def emit_ar():
        mk.op("tensor", lambda e: e.matmul(Q[0][0:1, 0:4], ones_cf[:, :],
                                           gpart[:, :], start=True, stop=True),
              reads=["gpart", "consts"], writes=["Qb0"])
        mk.op("vector", lambda e: e.tensor_copy(R_G, Q[0][0:1, 0:4]),
              reads=["Qb0"], writes=["R_G"])
        dma("sync", g_in.ap(), R_G, reads=["R_G"], writes=["g_in"], sem="dGi")
        mk.op("gpsimd",
              lambda e: e.collective_compute(
                  "AllReduce", ALU.add, replica_groups=[list(range(8))],
                  ins=[g_in.ap().opt()], outs=[g_out.ap().opt()]),
              reads=["g_in"], writes=["g_outb"], sem="cG", inc=1)
        dma("gpsimd", R_G, g_out.ap(), reads=["g_outb", "R_G"],
            writes=["R_G"], sem="dGo")
        mk.op("gpsimd", lambda e: e.tensor_scalar(R_TT, R_G, 0.5 / NW, 0.5e-8,
                                                  ALU.mult, ALU.add),
              reads=["R_G"], writes=["R_TT"])
        mk.op("gpsimd", lambda e: e.tensor_scalar_mul(R_NT, R_TT, -1.0),
              reads=["R_TT"], writes=["R_NT"])

    def emit_thr_bcast():
        mk.op("tensor", lambda e: e.matmul(Q[0][:, 0:4], ones_rf[:, :], R_TT,
                                           start=True, stop=True),
              reads=["R_TT", "consts"], writes=["Qb0"])
        mk.op("tensor", lambda e: e.matmul(Q[0][:, 4:8], ones_rf[:, :], R_NT,
                                           start=True, stop=True),
              reads=["R_NT", "consts"], writes=["Qb0"])
        mk.op("vector", lambda e: e.tensor_copy(thr[:, 0:8], Q[0][:, 0:8]),
              reads=["Qb0"], writes=["gthr"])

    def quant_piece(i):
        m, srcp, width, dst, rows = qpieces[i]
        par = i % NWSL
        qp = i % 2
        dma("sync", wsl[par][:, 0:width], srcp, writes=[f"wsl{par}"],
            sem=f"dV{par}")
        mk.op("vector",
              lambda e, par=par, qp=qp, w=width, m=m: e.tensor_scalar(
                  qA[qp][:, 0:w], wsl[par][:, 0:w], thr[:, m:m + 1], 2.0,
                  ALU.is_gt, ALU.mult),
              reads=[f"wsl{par}", "gthr"], writes=[f"qA{qp}"])
        mk.op("vector",
              lambda e, par=par, w=width, m=m: e.tensor_scalar(
                  qB[:, 0:w], wsl[par][:, 0:w], thr[:, 4 + m:5 + m], 2.0,
                  ALU.is_gt, ALU.mult),
              reads=[f"wsl{par}", "gthr"], writes=["qB"])
        mk.op("vector",
              lambda e, qp=qp, w=width: e.scalar_tensor_tensor(
                  qA[qp][:, 0:w], qA[qp][:, 0:w], -2.0, qB[:, 0:w],
                  ALU.add, ALU.add),
              reads=[f"qA{qp}", "qB"], writes=[f"qA{qp}"])
        dma("sync", dst, qA[qp][0:rows, 0:width],
            reads=[f"qA{qp}"], writes=[f"wq_in{m}"],
            sem=("dQ" if qp == 0 else "dQ1"))

    emit_xcast(0)
    emit_xcast(1)
    qfree = []

    bg = {}

    def bg_add(t, h, fn):
        bg.setdefault((t, h), []).append(fn)

    bg_add(2, 0, emit_thr_bcast)
    gam_q = list(range(NPIECE))
    qnt_q = list(range(NPIECE))


    for j, (nm, wsb) in enumerate((("wq", wq_sb), ("wk", wk_sb), ("wv", wv_sb))):
        def wsum_mm(e, wsb=wsb):
            last = None
            for k in range(NKC):
                last = e.matmul(Q[1][0:1, 0:512], ones_cb[:, :],
                                wsb[:, 512 * k:512 * k + 512],
                                start=(k == 0), stop=(k == NKC - 1))
            return last
        mk.op("tensor", wsum_mm, reads=[nm, "consts"], writes=["Qb2"])
        mk.op("vector",
              lambda e, j=j: e.tensor_scalar_mul(negws[:, 512 * j:512 * j + 512],
                                                 Q[1][0:1, 0:512], -1.0),
              reads=["Qb2"], writes=[f"negws{j}"])

    def quiesce():
        allw = [(s, mk.cnt[s]) for s in sem_names if mk.cnt[s] > 0]
        for qn in ENGS:
            mk.wait_only(qn, waits=allw)

    def finish(nc=nc):
        mk.finalize_plan()
        with nc.Block() as blk:
            @blk.tensor
            def _(eng):
                for f in mk.plan["tensor"]:
                    f(eng)

            @blk.vector
            def _(eng):
                for f in mk.plan["vector"]:
                    f(eng)

            @blk.scalar
            def _(eng):
                for f in mk.plan["scalar"]:
                    f(eng)

            @blk.sync
            def _(eng):
                for f in mk.plan["sync"]:
                    f(eng)

            @blk.gpsimd
            def _(eng):
                for f in mk.plan["gpsimd"]:
                    f(eng)
        return nc

    if STOP <= 1:
        quiesce()
        return finish()


    def gen_qkv(t):
        par = t % 2
        xb = x_bf[par]
        xkey = f"xbf{par}"
        qsb = q_sb[par]

        if t + 1 < NT:
            nxp = (t + 1) % 2
            dma("sync", rc_raw[nxp][:, :], rope2[0, :, TTK * (t + 1):TTK * (t + 2)],
                writes=[f"rcraw{nxp}"], sem=("dRc" if nxp == 0 else "dRc1"))
            dma("sync", rs_raw[nxp][:, :], rope2[1, :, TTK * (t + 1):TTK * (t + 2)],
                writes=[f"rsraw{nxp}"], sem=("dRs" if nxp == 0 else "dRs1"))
        def sum_mm(e, xb=xb):
            last = None
            for k in range(NKC):
                last = e.matmul(Q[0][0:1, 0:512], ones_cb[:, :],
                                xb[:, 512 * k:512 * k + 512],
                                start=(k == 0), stop=(k == NKC - 1))
            return last
        mk.op("tensor", sum_mm, reads=[xkey, "consts"], writes=["Qb0"])
        yield
        for k in range(NKC):
            xp = k % 2
            mk.op("vector",
                  lambda e, xb=xb, k=k, xp=xp: e.tensor_mul(
                      xsq[xp][:, :], xb[:, 512 * k:512 * k + 512],
                      xb[:, 512 * k:512 * k + 512]),
                  reads=[xkey], writes=[f"xsq{xp}"])
            mk.op("tensor",
                  lambda e, k=k, xp=xp: e.matmul(
                      Q[0][0:1, 512:1024], ones_cb[:, :], xsq[xp][:, :],
                      start=(k == 0), stop=(k == NKC - 1)),
                  reads=[f"xsq{xp}", "consts"], writes=["Qb1"])
            if k % 4 == 3:
                yield
        for ss in (0, 1):
            def v_main(e, xb=xb, ss=ss):
                last = None
                for k in range(NKC):
                    last = e.matmul(qb(2 + ss),
                                    xb[:, 512 * k + 128 * ss:512 * k + 128 * ss + 128],
                                    wv_sb[:, 512 * k:512 * k + 512],
                                    start=(k == 0), stop=False)
                return last
            mk.op("tensor", v_main, reads=[xkey, "wv"], writes=[bkey(2 + ss)])
            yield
        mk.op("vector", lambda e: e.tensor_scalar_mul(R_M, Q[0][0:1, 0:512], 1.0 / H),
              reads=["Qb0"], writes=["R_M"])
        mk.op("vector", lambda e: e.tensor_scalar_mul(R_VAR, Q[0][0:1, 512:1024], 1.0 / H),
              reads=["Qb1"], writes=["R_VAR"])
        mk.op("vector", lambda e: e.tensor_mul(R_T1, R_M, R_M),
              reads=["R_M"], writes=["R_T1"])
        mk.op("vector", lambda e: e.tensor_sub(R_VAR, R_VAR, R_T1),
              reads=["R_VAR", "R_T1"], writes=["R_VAR"])
        mk.op("scalar", lambda e: e.activation(R_T1, R_VAR, AF.Ln,
                                               bias=epscol[0:1, 0:1], scale=1.0),
              reads=["R_VAR"], writes=["R_T1"])
        mk.op("scalar", lambda e: e.activation(R_VAR, R_T1, AF.Exp,
                                               bias=zcol[0:1, 0:1], scale=-0.5),
              reads=["R_T1"], writes=["R_VAR"])
        mk.op("vector", lambda e: e.tensor_copy(RB_M, R_M),
              reads=["R_M"], writes=["RB_M"])
        mk.op("vector", lambda e: e.tensor_copy(RB_RST, R_VAR),
              reads=["R_VAR"], writes=["RB_RST"])
        yield
        def rcol_mm(e):
            last = None
            for ss in range(4):
                last = e.matmul(Q[0][:, ss:ss + 1],
                                RB_RST[:, 128 * ss:128 * ss + 128],
                                one1b[:, :], start=True, stop=True)
            return last
        mk.op("tensor", rcol_mm, reads=["RB_RST", "consts"], writes=["Qb0"])
        mk.op("vector", lambda e: e.tensor_copy(rcolv[:, :], Q[0][:, 0:4]),
              reads=["Qb0"], writes=["rcolv"])
        mk.op("tensor", lambda e: e.matmul(Q[0][:, 512:1024], ones_rb[:, :],
                                           RB_RST, start=True, stop=True),
              reads=["RB_RST", "consts"], writes=["Qb1"])
        mk.op("vector",
              lambda e, par=par: e.tensor_mul(rc_raw[par][:, :],
                                              rc_raw[par][:, :],
                                              Q[0][:, 512:1024]),
              reads=[f"rcraw{par}", "Qb1"], writes=[f"rcraw{par}"])
        mk.op("vector",
              lambda e, par=par: e.tensor_mul(rs_raw[par][:, :],
                                              rs_raw[par][:, :],
                                              Q[0][:, 512:1024]),
              reads=[f"rsraw{par}", "Qb1"], writes=[f"rsraw{par}"])
        yield
        for ss in (0, 1):
            mk.op("tensor",
                  lambda e, ss=ss: e.matmul(qb(2 + ss),
                                            RB_M[:, 128 * ss:128 * ss + 128],
                                            negws[:, 1024:1536],
                                            start=False, stop=True),
                  reads=["RB_M", "negws2"], writes=[bkey(2 + ss)])
            gs = 4 * t + ss
            mk.op("vector",
                  lambda e, ss=ss, gs=gs: e.tensor_scalar(
                      v_sb[:, 512 * gs:512 * gs + 512], qb(2 + ss),
                      rcolv[:, ss:ss + 1], None, ALU.mult),
                  reads=[bkey(2 + ss), "rcolv"], writes=[f"v_s{gs}"])
            yield
        for ss in (2, 3):
            def v_mm2(e, xb=xb, ss=ss):
                for k in range(NKC):
                    e.matmul(qb(ss),
                             xb[:, 512 * k + 128 * ss:512 * k + 128 * ss + 128],
                             wv_sb[:, 512 * k:512 * k + 512],
                             start=(k == 0), stop=False)
                return e.matmul(qb(ss), RB_M[:, 128 * ss:128 * ss + 128],
                                negws[:, 1024:1536], start=False, stop=True)
            mk.op("tensor", v_mm2, reads=[xkey, "wv", "RB_M", "negws2"],
                  writes=[bkey(ss)])
            gs = 4 * t + ss
            mk.op("vector",
                  lambda e, ss=ss, gs=gs: e.tensor_scalar(
                      v_sb[:, 512 * gs:512 * gs + 512], qb(ss),
                      rcolv[:, ss:ss + 1], None, ALU.mult),
                  reads=[bkey(ss), "rcolv"], writes=[f"v_s{gs}"])
            yield
        for h in range(NHC):
            for iq in range(2):
                is_q = (iq == 0)
                bank = 2 * (h % 2) + iq
                wsb, wkey, nwj = ((wq_sb, "wq", 0) if is_q else (wk_sb, "wk", 1))

                def qk_mm(e, xb=xb, h=h, wsb=wsb, bank=bank, nwj=nwj):
                    for k in range(NKC):
                        e.matmul(qb(bank),
                                 wsb[:, 512 * k + 128 * h:512 * k + 128 * h + 128],
                                 xb[:, 512 * k:512 * k + 512],
                                 start=(k == 0), stop=False)
                    return e.matmul(
                        qb(bank),
                        negws[:, 512 * nwj + 128 * h:512 * nwj + 128 * h + 128],
                        RB_M, start=False, stop=True)
                mk.op("tensor", qk_mm, reads=[xkey, wkey, "RB_M", f"negws{nwj}"],
                      writes=[bkey(bank)])
                mk.op("vector",
                      lambda e, bank=bank: e.tensor_copy(tmpsw[0:64, :],
                                                         qb(bank)[64:128, :]),
                      reads=[bkey(bank)], writes=["tmpsw_a"])
                mk.op("vector",
                      lambda e, bank=bank: e.tensor_copy(tmpsw[64:128, :],
                                                         qb(bank)[0:64, :]),
                      reads=[bkey(bank)], writes=["tmpsw_b"])
                if is_q:
                    dst = qsb[:, 512 * h:512 * h + 512]
                    dkey = f"q_h{h}_p{par}"
                else:
                    dst = k_res[:, S * h + TTK * t: S * h + TTK * t + 512]
                    dkey = f"k_h{h}_t{t}"
                mk.op("vector",
                      lambda e, bank=bank, par=par: e.tensor_mul(
                          tmpA[:, :], qb(bank), rc_raw[par][:, :]),
                      reads=[bkey(bank), f"rcraw{par}"], writes=["tmpA"])
                mk.op("vector",
                      lambda e, par=par: e.tensor_mul(tmpsw[:, :], tmpsw[:, :],
                                                      rs_raw[par][:, :]),
                      reads=["tmpsw_a", "tmpsw_b", f"rsraw{par}"],
                      writes=["tmpsw_a", "tmpsw_b"])
                mk.op("vector",
                      lambda e, dst=dst: e.tensor_add(dst, tmpA[:, :],
                                                      tmpsw[:, :]),
                      reads=["tmpsw_a", "tmpsw_b", "tmpA"],
                      writes=[dkey])
                yield
            cb = 2 * (h % 2)
            mk.op("vector",
                  lambda e, h=h, qsb=qsb: e.tensor_mul(
                      sqs[:, :], qsb[:, 512 * h:512 * h + 512],
                      qsb[:, 512 * h:512 * h + 512]),
                  reads=[f"q_h{h}_p{par}"], writes=["tmpsw_a", "tmpsw_b"])
            mk.op("tensor", lambda e, cb=cb: e.matmul(
                      qrow(cb), ones_cb[:, :], sqs[:, :], start=True, stop=True),
                  reads=["tmpsw_a", "tmpsw_b", "consts"], writes=[bkey(cb)])
            mk.op("vector", lambda e, cb=cb: e.reduce_max(R_QMX, qrow(cb),
                                                          axis=AXX),
                  reads=[bkey(cb)], writes=["R_QMX"])
            mk.op("vector",
                  lambda e, h=h, t=t: e.tensor_mul(
                      sqs[:, :], k_res[:, S * h + TTK * t:S * h + TTK * t + 512],
                      k_res[:, S * h + TTK * t:S * h + TTK * t + 512]),
                  reads=[f"k_h{h}_t{t}"], writes=["tmpsw_a", "tmpsw_b"])
            mk.op("tensor", lambda e, cb=cb: e.matmul(
                      qrow(cb), ones_cb[:, :], sqs[:, :], start=True, stop=True),
                  reads=["tmpsw_a", "tmpsw_b", "consts"], writes=[bkey(cb)])
            mk.op("vector", lambda e, cb=cb: e.reduce_max(R_KMX, qrow(cb),
                                                          axis=AXX),
                  reads=[bkey(cb)], writes=["R_KMX"])
            mk.op("vector",
                  lambda e, h=h: e.tensor_max(R_KAP[:, h:h + 1],
                                              R_KAP[:, h:h + 1], R_KMX),
                  reads=["R_KMX", "kap"], writes=["kap"])
            mk.op("vector",
                  lambda e, h=h: e.tensor_mul(R_PRD, R_QMX, R_KAP[:, h:h + 1]),
                  reads=["R_QMX", "kap"], writes=["R_PRD"])
            mk.op("scalar", lambda e: e.activation(R_SQP, R_PRD, AF.Ln,
                                                   bias=epscol[0:1, 0:1],
                                                   scale=1.0),
                  reads=["R_PRD"], writes=["R_SQP"])
            mk.op("scalar", lambda e: e.activation(R_SQT, R_SQP, AF.Exp,
                                                   bias=zcol[0:1, 0:1],
                                                   scale=0.5),
                  reads=["R_SQP"], writes=["R_SQT"])
            mk.op("vector", lambda e: e.tensor_scalar_mul(RB_BV, R_SQT, -CSC),
                  reads=["R_SQT"], writes=["RB_BV"])
            mk.op("tensor", lambda e, cb=cb: e.matmul(
                      Q[cb // 2][:, 512 * (cb % 2):512 * (cb % 2) + 1],
                      ones_rb[:, :], RB_BV, start=True, stop=True),
                  reads=["RB_BV", "consts"], writes=[bkey(cb)])
            bcn = 4 * par + h
            mk.op("vector",
                  lambda e, bcn=bcn, cb=cb: e.tensor_copy(
                      biascol[:, bcn:bcn + 1],
                      Q[cb // 2][:, 512 * (cb % 2):512 * (cb % 2) + 1]),
                  reads=[bkey(cb)], writes=[f"bias_h{h}_p{par}"])
            yield

    def gen_att(ta):
        par = ta % 2
        qsb = q_sb[par]
        for h in range(NHC):
            kcn = 4 * ta + 4
            npair = kcn // 2

            def emit_scores(j, hh=h):
                for half in range(2):
                    kc = 2 * j + half
                    mk.op("tensor",
                          lambda e, half=half, hh=hh, kc=kc: e.matmul(
                              Q[2][:, 512 * half:512 * half + 512],
                              k_res[:, S * hh + 128 * kc:S * hh + 128 * kc + 128],
                              qsb[:, 512 * hh:512 * hh + 512],
                              start=True, stop=True),
                          reads=[f"k_h{hh}_t{kc // 4}", f"q_h{hh}_p{par}"],
                          writes=[bkey(4 + half)])
            if h == 0:
                emit_scores(0)
            for j in range(npair):
                pb = probs[j % 2]
                pkey = f"probs{j % 2}"
                bcn = 4 * par + h
                for half in range(2):
                    mk.op("scalar",
                          lambda e, pb=pb, half=half, bcn=bcn: e.activation(
                              pb[:, 512 * half:512 * half + 512],
                              Q[2][:, 512 * half:512 * half + 512], AF.Exp,
                              bias=biascol[:, bcn:bcn + 1], scale=CSC),
                          reads=[bkey(4 + half), f"bias_h{h}_p{par}"],
                          writes=[pkey])
                for half in range(2):
                    kc = 2 * j + half
                    a = kc - 4 * ta
                    if a >= 0:
                        mk.op("vector",
                              lambda e, pb=pb, half=half, a=a: e.tensor_mul(
                                  pb[:, 512 * half:512 * half + 512],
                                  pb[:, 512 * half:512 * half + 512],
                                  masks_sb[:, 512 * a:512 * a + 512]),
                              reads=[pkey, "masks"], writes=[pkey])
                if j + 1 < npair:
                    emit_scores(j + 1)
                for half in range(2):
                    kc = 2 * j + half
                    mk.op("tensor",
                          lambda e, pb=pb, half=half, st0=(kc == 0),
                          sp=(kc == kcn - 1): e.matmul(
                              qrow(7), ones_cb[:, :],
                              pb[:, 512 * half:512 * half + 512],
                              start=st0, stop=sp),
                          reads=[pkey, "consts"], writes=["Qb7"])
                for half in range(2):
                    kc = 2 * j + half
                    mk.op("tensor",
                          lambda e, pb=pb, half=half, kc=kc, h=h,
                          st0=(kc == 0), sp=(kc == kcn - 1): e.matmul(
                              qb(6),
                              v_sb[:, 512 * kc + 128 * h:512 * kc + 128 * h + 128],
                              pb[:, 512 * half:512 * half + 512],
                              start=st0, stop=sp),
                          reads=[pkey, f"v_s{kc}"], writes=["Qb6"])
                yield
            if h + 1 < NHC:
                emit_scores(0, h + 1)
            mk.op("scalar", lambda e: e.activation(
                      R_T1, qrow(7), AF.Ln, bias=zcol[0:1, 0:1], scale=1.0),
                  reads=["Qb7"], writes=["R_T1"])
            mk.op("scalar", lambda e: e.activation(
                      RB_DEN, R_T1, AF.Exp, bias=zcol[0:1, 0:1], scale=-1.0),
                  reads=["R_T1"], writes=["RB_DEN"])
            mk.op("tensor", lambda e: e.matmul(
                      qb(7), ones_rb[:, :], RB_DEN, start=True, stop=True),
                  reads=["RB_DEN", "consts"], writes=["Qb7"])
            mk.op("vector", lambda e: e.tensor_copy(rdenb_sb[0][:, :], qb(7)),
                  reads=["Qb7"], writes=["rdenb0"])
            cp = (ta % 2) * 4 + h
            cw = []
            if ta >= 2:
                cw.append((f"dC{(ta - 2) // 2}", 128))
            mk.op("vector",
                  lambda e, cp=cp: e.tensor_mul(
                      ctxev[cp][:, :], qb(6), rdenb_sb[0][:, :]),
                  reads=["Qb6", "rdenb0"], writes=[f"ctxev{cp}"],
                  waits=cw)
            dma("sync",
                cag_in[ta // 2].ap()[128 * h:128 * h + 128,
                                     512 * (ta % 2):512 * (ta % 2) + 512],
                ctxev[cp][:, :],
                reads=[f"ctxev{cp}"], writes=[f"cagin{ta // 2}"],
                sem=f"dC{ta // 2}")
            for fn in bg.get((ta, h), ()):
                fn()
            yield

    def drain(g):
        if g is None:
            return
        for _ in g:
            pass

    for t in range(min(NT, MKTILES)):
        gq = gen_qkv(t)
        ga = gen_att(t - 1) if (t >= 1 and MKATT) else None
        rnd = 0
        while True:
            rnd += 1
            qdone = adone = False
            try:
                next(gq)
            except StopIteration:
                qdone = True
            if ga is not None:
                try:
                    next(ga)
                except StopIteration:
                    adone = True
            else:
                adone = True
            if t <= 2 and gam_q:
                gamma_piece(gam_q.pop(0))
            elif ((t == 3 and rnd >= 9) or t in (4, 5)) and qnt_q:
                quant_piece(qnt_q.pop(0))
            if qdone:
                if not adone and ga is not None:
                    drain(ga)
                break
        ta = t - 1
        if t + 2 < min(NT, MKTILES):
            emit_xcast(t + 2)
        if ta >= 1 and ta % 2 == 1 and MKATT and MKAG:
            kq = ta // 2
            mk.op("gpsimd",
                  lambda e, kq=kq: e.collective_compute(
                      "AllGather", ALU.bypass,
                      replica_groups=[[0, 1, 2, 3], [4, 5, 6, 7]],
                      ins=[cag_in[kq].ap().opt()],
                      outs=[cag_out.ap()[2048 * kq:2048 * kq + 2048, :]]),
                  reads=[f"cagin{kq}"], writes=[f"cagout{kq}"],
                  sem=f"cX{kq}", inc=1)
        if ta == 1:
            while gam_q:
                gamma_piece(gam_q.pop(0))
            emit_ar()
        if t == 5:
            while qnt_q:
                quant_piece(qnt_q.pop(0))
            mk.op("gpsimd",
                  lambda e: e.collective_compute(
                      "AllGather", ALU.bypass, replica_groups=[list(range(8))],
                      ins=[w1q_in.ap().opt()], outs=[w1q_all.ap().opt()]),
                  reads=["wq_in0"], writes=["w1q_all"], sem="cW1", inc=1,
                  waits=[("dQ", mk.cnt["dQ"]), ("dQ1", mk.cnt["dQ1"])])
        if t == 6:
            mk.op("gpsimd",
                  lambda e: e.collective_compute(
                      "AllGather", ALU.bypass, replica_groups=[list(range(8))],
                      ins=[w2q_in.ap().opt()], outs=[w2q_all.ap().opt()]),
                  reads=["wq_in1"], writes=["w2q_all"], sem="cW2", inc=1,
                  waits=[("dQ", mk.cnt["dQ"]), ("dQ1", mk.cnt["dQ1"])])
        if t == 7:
            mk.op("gpsimd",
                  lambda e: e.collective_compute(
                      "AllGather", ALU.bypass, replica_groups=[list(range(8))],
                      ins=[w3q_in.ap().opt()], outs=[w3q_all.ap().opt()]),
                  reads=["wq_in2"], writes=["w3q_all"], sem="cW3", inc=1,
                  waits=[("dQ", mk.cnt["dQ"]), ("dQ1", mk.cnt["dQ1"])])

    qkv_done = [("T", mk.cnt["T"]), ("V", mk.cnt["V"]), ("S", mk.cnt["S"]),
                ("dQ", mk.cnt["dQ"]), ("dQ1", mk.cnt["dQ1"])]
    S1.close()
    Sr = ExitStack()
    res1 = Sr.enter_context(sb("res1", [128, NKC * 1024], F32))
    Sb1a = ExitStack()
    wo_sb = [Sb1a.enter_context(sb(f"wo_sb{i}", [128, 2048], BF16))
             for i in range(2)]
    xp2_sb = [Sb1a.enter_context(sb(f"xp2_sb{i}", [128, 1024], F32))
              for i in range(2)]
    r1bf = [probs[0][:, :], probs[1][:, :]]
    r1sq = [q_sb[0][:, 0:1024], q_sb[0][:, 1024:2048]]

    def emit_wo(m):
        slot = m % 2
        dma("gpsimd", wo_sb[slot][:, :], wo_t[m],
            writes=[f"wo{slot}"], waits=qkv_done, sem=f"dO{slot}")

    def emit_xp2(m):
        par = m % 2
        dma("sync", xp2_sb[par][:, :], xp2_fm[128 * m:128 * m + 128, :],
            writes=[f"xp2{par}"], waits=qkv_done, sem=f"dP{par}")

    emit_wo(0)
    emit_wo(1)
    emit_xp2(0)
    emit_xp2(1)

    if MKATT and min(NT, MKTILES) == NT:
        drain(gen_att(NT - 1))
        mk.op("gpsimd",
              lambda e: e.collective_compute(
                  "AllGather", ALU.bypass,
                  replica_groups=[[0, 1, 2, 3], [4, 5, 6, 7]],
                  ins=[cag_in[3].ap().opt()],
                  outs=[cag_out.ap()[6144:8192, :]]),
              reads=["cagin3"], writes=["cagout3"], sem="cX3", inc=1)

    ph1_done = [("T", mk.cnt["T"]), ("V", mk.cnt["V"]), ("S", mk.cnt["S"]),
                ("dS", mk.cnt["dS"])]

    ctx_fm = k_res
    h2 = v_sb

    for k in range(NKC):
        mk.op("gpsimd",
              lambda e, k=k: e.indirect_dma_start(
                  out=ctx_fm[:, 1024 * k:1024 * k + 1024], out_offset=None,
                  in_=cag_out.ap(),
                  in_offset=IndirectOffsetOnAxis(gidx_sb[:, k:k + 1], axis=0)),
              reads=["cagout0", "cagout1", "cagout2", "cagout3", "gidx_sb"],
              writes=[f"ctxfm{k}"], waits=ph1_done, sem="dGt", inc=16)

    if os.environ.get("MKDBG"):
        dbg_ctx = nc.dram_tensor("dbg_ctx", [H, TOK2], BF16, kind="Internal")
        for k in range(NKC):
            dma("sync", dbg_ctx.ap()[128 * k:128 * k + 128, :],
                ctx_fm[:, 1024 * k:1024 * k + 1024],
                reads=[f"ctxfm{kk}" for kk in range(NKC)],
                writes=[f"dbgctx{k}"], sem="dS")

    if STOP <= 3:
        quiesce()
        return finish()

    for m in range(NKC):
        par = m % 2
        slot = m % 2
        bk0 = 2 * par
        for n in range(2):
            def ao_mm(e, n=n, slot=slot, bk0=bk0):
                last = None
                for k in range(NKC):
                    last = e.matmul(
                        qb(bk0 + n), wo_sb[slot][:, 128 * k:128 * k + 128],
                        ctx_fm[:, 1024 * k + 512 * n:1024 * k + 512 * n + 512],
                        start=(k == 0), stop=(k == NKC - 1))
                return last
            mk.op("tensor", ao_mm,
                  reads=[f"wo{slot}"] + [f"ctxfm{k}" for k in range(NKC)],
                  writes=[bkey(bk0 + n)])
        if m + 2 < NKC:
            emit_wo(m + 2)
        mk.op("vector",
              lambda e, m=m, par=par, bk0=bk0: e.tensor_add(
                  res1[:, 1024 * m:1024 * m + 1024], Q[bk0 // 2][:, :],
                  xp2_sb[par][:, :]),
              reads=[bkey(bk0), bkey(bk0 + 1), f"xp2{par}"],
              writes=[f"res1_{m}"])
        if m + 2 < NKC:
            emit_xp2(m + 2)
        mk.op("vector",
              lambda e, m=m, par=par: e.tensor_copy(
                  r1bf[par][:, :], res1[:, 1024 * m:1024 * m + 1024]),
              reads=[f"res1_{m}"], writes=[f"r1bf{par}"])
        mk.op("vector",
              lambda e, par=par: e.tensor_mul(r1sq[par][:, :], r1bf[par][:, :],
                                              r1bf[par][:, :]),
              reads=[f"r1bf{par}"], writes=[f"r1sq{par}"])
        for n in range(2):
            mk.op("tensor",
                  lambda e, par=par, n=n, m=m: e.matmul(
                      qrow(4 + n), ones_cb[:, :],
                      r1bf[par][:, 512 * n:512 * n + 512],
                      start=(m == 0), stop=(m == NKC - 1)),
                  reads=[f"r1bf{par}", "consts"], writes=[bkey(4 + n)])
            mk.op("tensor",
                  lambda e, par=par, n=n, m=m: e.matmul(
                      qrow(6 + n), ones_cb[:, :],
                      r1sq[par][:, 512 * n:512 * n + 512],
                      start=(m == 0), stop=(m == NKC - 1)),
                  reads=[f"r1sq{par}", "consts"], writes=[bkey(6 + n)])

    mk.op("vector", lambda e: e.tensor_scalar_mul(R_M2A, Q[2][0:1, :], 1.0 / H),
          reads=["Qb4", "Qb5"], writes=["R_M2"])
    mk.op("vector", lambda e: e.tensor_scalar_mul(R_V2A, Q[3][0:1, :], 1.0 / H),
          reads=["Qb6", "Qb7"], writes=["R_V2"])
    mk.op("vector", lambda e: e.tensor_mul(Q[2][0:1, :], R_M2A, R_M2A),
          reads=["R_M2"], writes=["Qb4", "Qb5"])
    mk.op("vector", lambda e: e.tensor_sub(R_V2A, R_V2A, Q[2][0:1, :]),
          reads=["R_V2", "Qb4"], writes=["R_V2"])
    mk.op("scalar", lambda e: e.activation(Q[2][0:1, :], R_V2A, AF.Ln,
                                           bias=epscol[0:1, 0:1], scale=1.0),
          reads=["R_V2"], writes=["Qb4", "Qb5"])
    mk.op("scalar", lambda e: e.activation(R_V2A, Q[2][0:1, :], AF.Exp,
                                           bias=zcol[0:1, 0:1], scale=-0.5),
          reads=["Qb4"], writes=["R_V2"])
    mk.op("vector", lambda e: e.tensor_copy(RB_R2, R_V2A),
          reads=["R_V2"], writes=["RB_R2"])
    mk.op("vector", lambda e: e.tensor_mul(R_M2A, R_M2A, R_V2A),
          reads=["R_M2", "R_V2"], writes=["R_M2"])
    mk.op("vector", lambda e: e.tensor_copy(RB_MR2, R_M2A),
          reads=["R_M2"], writes=["RB_MR2"])
    for n in range(2):
        mk.op("tensor", lambda e, n=n: e.matmul(
            qb(0 + n), ones_rb[:, :], RB_R2[:, 512 * n:512 * n + 512],
            start=True, stop=True),
              reads=["RB_R2", "consts"], writes=[bkey(n)])
        mk.op("tensor", lambda e, n=n: e.matmul(
            qb(2 + n), ones_rb[:, :], RB_MR2[:, 512 * n:512 * n + 512],
            start=True, stop=True),
              reads=["RB_MR2", "consts"], writes=[bkey(2 + n)])

    SLn2 = ExitStack()
    r2b_sb = SLn2.enter_context(sb("r2b_sb", [128, 1024], F32))
    mr2b_sb = SLn2.enter_context(sb("mr2b_sb", [128, 1024], F32))
    h2tmp = SLn2.enter_context(sb("h2tmp", [128, 1024], F32))
    mk.op("vector", lambda e: e.tensor_copy(r2b_sb[:, :], Q[0][:, :]),
          reads=["Qb0", "Qb1"], writes=["r2b"])
    mk.op("vector", lambda e: e.tensor_copy(mr2b_sb[:, :], Q[1][:, :]),
          reads=["Qb2", "Qb3"], writes=["mr2b"])
    for m in range(NKC):
        mk.op("vector",
              lambda e, m=m: e.tensor_mul(h2tmp[:, :],
                                          res1[:, 1024 * m:1024 * m + 1024],
                                          r2b_sb[:, :]),
              reads=[f"res1_{m}", "r2b"], writes=["h2tmp"])
        mk.op("vector",
              lambda e, m=m: e.tensor_sub(h2[:, 1024 * m:1024 * m + 1024],
                                          h2tmp[:, :], mr2b_sb[:, :]),
              reads=["h2tmp", "mr2b"], writes=[f"h2_{m}"])
        dma("sync", res1_dram.ap()[128 * m:128 * m + 128, :],
            res1[:, 1024 * m:1024 * m + 1024],
            reads=[f"res1_{m}"], writes=[f"r1d{m}"], sem="dR")

    for m in range(NKC):
        mk.setmark(f"r1d{m}", "dR")

    b1free = [("dR", mk.cnt["dR"]), ("T", mk.cnt["T"]), ("V", mk.cnt["V"])]

    SLn2.close()
    Sb1a.close()
    Sr.close()
    S1c.close()

    if STOP <= 4:
        quiesce()
        return finish()

    Shid = ExitStack()
    hidden = Shid.enter_context(sb("hidden", [128, 44 * 1024], BF16))
    w12 = [ctx_fm[:, 0:8192], ctx_fm[:, 8192:16384]]

    NJ = 11
    h2keys = [f"h2_{k}" for k in range(NKC)]
    for j in range(NJ):
        dma("sync", w12[0][:, :].rearrange("p (k c) -> p k c", k=NKC),
            w1q_all.ap().rearrange("(k p) h -> p k h", p=128)[
                :, :, 512 * j:512 * j + 512],
            writes=["w12b0"], sem="dF0",
            waits=([("cW1", 1)] + (b1free if j == 0 else [])))
        dma("sync", w12[1][:, :].rearrange("p (k c) -> p k c", k=NKC),
            w2q_all.ap().rearrange("(k p) h -> p k h", p=128)[
                :, :, 512 * j:512 * j + 512],
            writes=["w12b1"], sem="dF1",
            waits=([("cW2", 1)] + (b1free if j == 0 else [])))
        for cc in range(4):
            c = 4 * j + cc
            b1 = (cc % 2) * 4
            for n in range(2):
                def x1_mm(e, cc=cc, n=n, b1=b1):
                    last = None
                    for k in range(NKC):
                        last = e.matmul(
                            qb(b1 + n),
                            w12[0][:, 512 * k + 128 * cc:512 * k + 128 * cc + 128],
                            h2[:, 1024 * k + 512 * n:1024 * k + 512 * n + 512],
                            start=(k == 0), stop=(k == NKC - 1))
                    return last
                mk.op("tensor", x1_mm, reads=["w12b0"] + h2keys,
                      writes=[bkey(b1 + n)])
            mk.op("scalar",
                  lambda e, c=c, b1=b1: e.activation(
                      hidden[:, 1024 * c:1024 * c + 1024], Q[b1 // 2][:, :],
                      AF.Sigmoid, bias=zcol[:, 0:1], scale=0.5),
                  reads=[bkey(b1), bkey(b1 + 1)], writes=[f"hid_{c}"])
        for cc in range(4):
            c = 4 * j + cc
            b1 = (cc % 2) * 4
            for n in range(2):
                def x2_mm(e, cc=cc, n=n, b1=b1):
                    last = None
                    for k in range(NKC):
                        last = e.matmul(
                            qb(b1 + 2 + n),
                            w12[1][:, 512 * k + 128 * cc:512 * k + 128 * cc + 128],
                            h2[:, 1024 * k + 512 * n:1024 * k + 512 * n + 512],
                            start=(k == 0), stop=(k == NKC - 1))
                    return last
                mk.op("tensor", x2_mm, reads=["w12b1"] + h2keys,
                      writes=[bkey(b1 + 2 + n)])
            mk.op("vector",
                  lambda e, c=c, b1=b1: e.tensor_mul(
                      hidden[:, 1024 * c:1024 * c + 1024],
                      hidden[:, 1024 * c:1024 * c + 1024],
                      Q[b1 // 2 + 1][:, :]),
                  reads=[bkey(b1 + 2), bkey(b1 + 3), f"hid_{c}"],
                  writes=[f"hid_{c}"])

    w3free = [("T", mk.cnt["T"]), ("V", mk.cnt["V"]), ("S", mk.cnt["S"])]

    if STOP <= 5:
        quiesce()
        return finish()

    Sw3 = ExitStack()
    w3s = [v_sb[:, 0:44 * 256], k_res[:, 0:44 * 256]]
    r1s = [Sw3.enter_context(sb(f"r1s{i}", [128, 1024], F32)) for i in range(2)]
    outsb = [Sw3.enter_context(sb(f"outs{i}", [128, 1024], F32)) for i in range(2)]
    hidkeys = [f"hid_{k}" for k in range(44)]

    for mp in range(8):
        par = mp % 2
        for piece in range(2):
            dma("sync",
                w3s[par][:, :].rearrange("p (k c) -> p k c", k=44)[
                    :, 22 * piece:22 * piece + 22, :],
                w3q_all.ap().rearrange("(k p) d -> p k d", p=128)[
                    :, 22 * piece:22 * piece + 22, 256 * mp:256 * mp + 256],
                writes=[f"w3p{piece}_{par}"], sem=("dW3a" if par == 0 else "dW3b"),
                waits=([("cW3", 1)] + (w3free if mp < 2 else [])))
        mk.setmark(f"w3p0_{par}", "dW3a" if par == 0 else "dW3b")
        for mi in range(2):
            m = 2 * mp + mi
            b1 = (m % 2) * 4
            dma("sync", r1s[par][:, :], res1_dram.ap()[128 * m:128 * m + 128, :],
                reads=[f"r1d{m}"], writes=[f"r1s{par}"], sem=f"dL{par}",
                waits=(w3free if m < 2 else []))
            for n in range(2):
                def x3_mm(e, par=par, mi=mi, n=n, b1=b1):
                    last = None
                    for k in range(44):
                        last = e.matmul(
                            qb(b1 + n),
                            w3s[par][:, 256 * k + 128 * mi:256 * k + 128 * mi + 128],
                            hidden[:, 1024 * k + 512 * n:1024 * k + 512 * n + 512],
                            start=(k == 0), stop=(k == 43))
                    return last
                mk.op("tensor", x3_mm,
                      reads=[f"w3p0_{par}", f"w3p1_{par}"] + hidkeys,
                      writes=[bkey(b1 + n)])
            mk.op("vector",
                  lambda e, par=par, b1=b1: e.scalar_tensor_tensor(
                      outsb[par][:, :], Q[b1 // 2][:, :], 0.25, r1s[par][:, :],
                      ALU.mult, ALU.add),
                  reads=[bkey(b1), bkey(b1 + 1), f"r1s{par}"],
                  writes=[f"outs{par}"])
            dma("scalar", out_fm[128 * m:128 * m + 128, :], outsb[par][:, :],
                reads=[f"outs{par}"], writes=[f"outd{m}"], sem=f"dA{par}")

    mk.wait_only("scalar", waits=[("dA0", mk.cnt["dA0"]), ("dA1", mk.cnt["dA1"])])

    Sw3.close()
    Shid.close()
    S1b.close()
    S0.close()

    finish()
    st.close()
    return nc


_NC_CACHE = None


def _rope_tables():
    inv_freq = (1.0 / (np.float32(10000.0) **
                       (np.arange(0, HD, 2, dtype=np.float32) / np.float32(HD)))
                ).astype(np.float32)
    t = np.arange(S, dtype=np.float32)
    freqs = np.einsum("i,j->ij", t, inv_freq).astype(np.float32)
    cos = np.cos(freqs).astype(np.float32).T
    sin = np.sin(freqs).astype(np.float32).T
    ropec = np.concatenate([cos, cos], axis=0)
    ropes = np.concatenate([-sin, sin], axis=0)
    return np.ascontiguousarray(ropec), np.ascontiguousarray(ropes)


def make_in_maps(L0_attn_hidden_states, L0_attn_Wq, L0_attn_Wk, L0_attn_Wv,
                 L0_attn_Wo, L0_ffn_w1, L0_ffn_w2, L0_ffn_w3):
    x = np.asarray(L0_attn_hidden_states, dtype=np.float32)
    Wq = np.asarray(L0_attn_Wq, dtype=np.float32)
    Wk = np.asarray(L0_attn_Wk, dtype=np.float32)
    Wv = np.asarray(L0_attn_Wv, dtype=np.float32)
    Wo = np.asarray(L0_attn_Wo, dtype=np.float32)
    w1 = np.asarray(L0_ffn_w1, dtype=np.float32)
    w2 = np.asarray(L0_ffn_w2, dtype=np.float32)
    w3 = np.asarray(L0_ffn_w3, dtype=np.float32)

    ropec, ropes = _rope_tables()
    rope2b = np.ascontiguousarray(
        np.stack([ropec, ropes]).astype(ml_dtypes.bfloat16))

    masks = np.zeros((4, 128, 512), dtype=np.float32)
    qf = np.arange(512)
    kp = np.arange(128)
    for a in range(4):
        masks[a] = (qf[None, :] >= (128 * a + kp)[:, None]).astype(np.float32)
    masks_bf = masks.astype(ml_dtypes.bfloat16)

    WqT = np.ascontiguousarray(Wq.T)
    WkT = np.ascontiguousarray(Wk.T)
    WvT = np.ascontiguousarray(Wv.T)
    wo_tiled = np.ascontiguousarray(
        Wo.T.reshape(16, 128, 16, 128).transpose(2, 1, 0, 3).reshape(16, 128, 2048))
    w1T = np.ascontiguousarray(w1.T)
    w2T = np.ascontiguousarray(w2.T)
    w3T = np.ascontiguousarray(w3.T)

    xT = [np.ascontiguousarray(x[b].T) for b in range(B)]
    x_tl = [np.ascontiguousarray(
        x[b].reshape(NT, TTK, NKC, 128).transpose(0, 3, 2, 1).reshape(
            NT, 128, NKC * TTK)) for b in range(B)]

    def wtile(WT, g):
        return np.ascontiguousarray(
            WT[:, 512 * g:512 * g + 512].reshape(NKC, 128, 512)
            .transpose(1, 0, 2).reshape(128, NKC * 512))

    in_maps = []
    for c in range(8):
        b, g = c // 4, c % 4
        w3sl = np.zeros((768, H), dtype=np.float32)
        w3sl[:704] = w3T[704 * c:704 * c + 704]
        gidx = np.empty((128, 16), dtype=np.uint32)
        for k in range(16):
            gidx[:, k] = 2048 * g + 128 * k + np.arange(128)
        in_maps.append({
            "x_t": x_tl[b],
            "xp2_fm": np.ascontiguousarray(xT[b][:, 1024 * g:1024 * g + 1024]),
            "wq_t": wtile(WqT, g),
            "wk_t": wtile(WkT, g),
            "wv_t": wtile(WvT, g),
            "wo_t": wo_tiled,
            "w1_sl": np.ascontiguousarray(w1T[256 * c:256 * c + 256]),
            "w2_sl": np.ascontiguousarray(w2T[256 * c:256 * c + 256]),
            "w3_sl": w3sl,
            "rope2": rope2b,
            "masks": masks_bf,
            "gidx": gidx,
        })
    return in_maps


def kernel(L0_attn_hidden_states, L0_attn_Wq, L0_attn_Wk, L0_attn_Wv,
           L0_attn_Wo, L0_ffn_w1, L0_ffn_w2, L0_ffn_w3):
    global _NC_CACHE
    in_maps = make_in_maps(L0_attn_hidden_states, L0_attn_Wq, L0_attn_Wk,
                           L0_attn_Wv, L0_attn_Wo, L0_ffn_w1, L0_ffn_w2,
                           L0_ffn_w3)
    if _NC_CACHE is None:
        _NC_CACHE = build_program()
    nc = _NC_CACHE
    res = run_bass_kernel_spmd(nc, in_maps, core_ids=list(range(8)))

    out = np.empty((B, S, H), dtype=np.float32)
    for c in range(8):
        b, g = c // 4, c % 4
        out[b, 1024 * g:1024 * g + 1024, :] = res.results[c]["out_fm"].T
    return out


# version 24
# speedup vs baseline: 1.0310x; 1.0310x over previous
import os
import numpy as np
import ml_dtypes
from contextlib import ExitStack

import concourse.bass as bass
import concourse.mybir as mybir
from concourse.bass import IndirectOffsetOnAxis
from concourse.bass_utils import run_bass_kernel_spmd

F32 = mybir.dt.float32
BF16 = mybir.dt.bfloat16
U32 = mybir.dt.uint32
AF = mybir.ActivationFunctionType
ALU = mybir.AluOpType
AXX = mybir.AxisListType.X

B, S, H, NH, HD = 2, 4096, 2048, 16, 128
FH = 5632
TTK = 512
NT = S // TTK
NKC = H // 128
NHC = 4
TOK2 = 1024
CSC = float(1.0 / np.sqrt(np.float32(HD)))
EPS_LN = 1e-5
NW = 2048 * 5632

ENGS = ("tensor", "vector", "scalar", "sync", "gpsimd")


class MK:

    def __init__(self, nc, sems):
        self.nc = nc
        self.sems = sems
        self.cnt = {k: 0 for k in sems}
        self.plan = {e: [] for e in ENGS}
        self.last_w = {}
        self.readers = {}
        self.hw = {e: {} for e in ENGS}
        self.used = {}
        self.own = {"tensor": "T", "vector": "V", "scalar": "S",
                    "sync": None, "gpsimd": None}

    def op(self, q, fn, reads=(), writes=(), sem=None, inc=None, waits=()):
        deps = {}

        def add(s, v):
            if deps.get(s, 0) < v:
                deps[s] = v
        for b in reads:
            lw = self.last_w.get(b)
            if lw is not None:
                add(*lw)
        for b in writes:
            lw = self.last_w.get(b)
            if lw is not None:
                add(*lw)
            for s, v in self.readers.get(b, {}).items():
                add(s, v)
        for s, v in waits:
            add(s, v)
        wl = []
        for s, v in deps.items():
            if v <= self.hw[q].get(s, 0):
                continue
            wl.append((s, v))
            self.hw[q][s] = v
        if sem is None:
            sem = {"tensor": "T", "vector": "V", "scalar": "S",
                   "sync": "dS", "gpsimd": "dG"}[q]
            if inc is None:
                inc = 16 if q in ("sync", "gpsimd") else 1
        if inc is None:
            inc = 1
        self.cnt[sem] += inc
        val = self.cnt[sem]
        rec = {"q": q, "fn": fn, "wl": tuple(wl), "sem": sem, "val": val,
               "inc": inc}
        self.plan[q].append(rec)
        for s, v in wl:
            self.used.setdefault(s, set()).add(v)

        mark = (sem, val)
        for b in writes:
            self.last_w[b] = mark
            self.readers.pop(b, None)
        for b in reads:
            d = self.readers.setdefault(b, {})
            if d.get(sem, 0) < val:
                d[sem] = val
        return mark

    def setmark(self, buf, sem, val=None):
        if val is None:
            val = self.cnt[sem]
        self.last_w[buf] = (sem, val)
        self.readers.pop(buf, None)

    def wait_only(self, q, waits=()):
        self.plan[q].append({"q": q, "fn": None, "wl": tuple(waits),
                             "sem": None, "val": 0, "inc": 0})
        for s, v in waits:
            self.used.setdefault(s, set()).add(v)

    def finalize_plan(self):
        prunable = {"T", "V", "S"}
        remap = {}
        for s in prunable:
            used = self.used.get(s, set())
            kept = 0
            m = {0: 0}
            recs = []
            for qn in ENGS:
                for r in self.plan[qn]:
                    if r["sem"] == s:
                        recs.append(r)
            recs.sort(key=lambda r: r["val"])
            for r in recs:
                if r["val"] in used:
                    kept += 1
                    r["newinc"] = r["val"] - (r["val"] - kept)
                    r["keep"] = True
                else:
                    r["keep"] = False
                m[r["val"]] = kept
            remap[s] = m
        sems = self.sems
        out = {}
        for qn in ENGS:
            fns = []
            for r in self.plan[qn]:
                wl = []
                for s, v in r["wl"]:
                    if s in remap:
                        v = remap[s][v]
                        if v == 0:
                            continue
                    wl.append((s, v))
                fn, s_, inc = r["fn"], r["sem"], r["inc"]
                keep = r.get("keep", True) if s_ in ("T", "V", "S") else True

                def run(eng, wl=tuple(wl), fn=fn, s_=s_, inc=inc, keep=keep):
                    for s, v in wl:
                        eng.wait_ge(sems[s], v)
                    if fn is None:
                        return
                    inst = fn(eng)
                    if keep and s_ is not None:
                        inst.then_inc(sems[s_], inc)
                fns.append(run)
            out[qn] = fns
        self.plan = out


def build_program():
    STOP = int(os.environ.get("MKSTOP", "99"))
    MKTILES = int(os.environ.get("MKTILES", "8"))
    MKATT = int(os.environ.get("MKATT", "1"))
    MKAG = int(os.environ.get("MKAG", "1"))
    MKSUB = int(os.environ.get("MKSUB", "7"))
    MKS2 = int(os.environ.get("MKS2", "31"))
    MKS3 = int(os.environ.get("MKS3", "7"))
    nc = bass.Bass("TRN2", target_bir_lowering=False, debug=False)

    x_t = nc.dram_tensor("x_t", [NT, 128, NKC * 512], F32, kind="ExternalInput").ap()
    xp2_fm = nc.dram_tensor("xp2_fm", [H, TOK2], F32, kind="ExternalInput").ap()
    wq_t = nc.dram_tensor("wq_t", [128, NKC * 512], F32, kind="ExternalInput").ap()
    wk_t = nc.dram_tensor("wk_t", [128, NKC * 512], F32, kind="ExternalInput").ap()
    wv_t = nc.dram_tensor("wv_t", [128, NKC * 512], F32, kind="ExternalInput").ap()
    wo_t = nc.dram_tensor("wo_t", [16, 128, 2048], F32, kind="ExternalInput").ap()
    w1_sl = nc.dram_tensor("w1_sl", [256, FH], F32, kind="ExternalInput").ap()
    w2_sl = nc.dram_tensor("w2_sl", [256, FH], F32, kind="ExternalInput").ap()
    w3_sl = nc.dram_tensor("w3_sl", [768, H], F32, kind="ExternalInput").ap()
    rope2 = nc.dram_tensor("rope2", [2, 128, S], BF16, kind="ExternalInput").ap()
    masks = nc.dram_tensor("masks", [4, 128, 512], BF16, kind="ExternalInput").ap()
    gidx = nc.dram_tensor("gidx", [128, 16], U32, kind="ExternalInput").ap()
    out_fm = nc.dram_tensor("out_fm", [H, TOK2], F32, kind="ExternalOutput").ap()

    g_in = nc.dram_tensor("g_in", [1, 4], F32, kind="Internal")
    g_out = nc.dram_tensor("g_out", [1, 4], F32, kind="Internal")
    w1q_in = nc.dram_tensor("w1q_in", [256, FH], BF16, kind="Internal")
    w2q_in = nc.dram_tensor("w2q_in", [256, FH], BF16, kind="Internal")
    w3q_in = nc.dram_tensor("w3q_in", [704, H], BF16, kind="Internal")
    w1q_all = nc.dram_tensor("w1q_all", [H, FH], BF16, kind="Internal")
    w2q_all = nc.dram_tensor("w2q_all", [H, FH], BF16, kind="Internal")
    w3q_all = nc.dram_tensor("w3q_all", [FH, H], BF16, kind="Internal")
    cag_in = [nc.dram_tensor(f"cag_in{k}", [512, 1024], BF16, kind="Internal")
              for k in range(4)]
    cag_out = nc.dram_tensor("cag_out", [8192, 1024], BF16, kind="Internal")
    res1_dram = nc.dram_tensor("res1_dram", [H, TOK2], F32, kind="Internal")

    sem_names = ["T", "V", "S", "dS", "dG", "dA",
                 "cG", "cW1", "cW2", "cW3", "cX0", "cX1", "cX2", "cX3",
                 "dI", "dIg", "dGi", "dGo", "dX0", "dX1", "dW0", "dW1", "dQ",
                 "dV0", "dV1", "dV2", "dV3", "dV4", "dV5", "dV6", "dV7",
                 "dRc1", "dRs1",
                 "dC0", "dC1", "dC2", "dC3", "dGt", "dO0", "dO1",
                 "dP0", "dP1", "dR", "dF0", "dF1", "dW3a", "dW3b",
                 "dL0", "dL1", "dA0", "dA1", "dRc", "dRs", "dQ1", "dW2", "dW3"]

    st = ExitStack()
    sems = {n: st.enter_context(nc.semaphore(n)) for n in sem_names}

    Q = [st.enter_context(nc.psum_tensor(f"Q{i}", [128, 1024], F32))
         for i in range(4)]

    def qb(i):
        return Q[i // 2][:, 512 * (i % 2):512 * (i % 2) + 512]

    def qrow(i):
        return Q[i // 2][0:1, 512 * (i % 2):512 * (i % 2) + 512]

    def bkey(i):
        return f"Qb{i}"

    mk = MK(nc, sems)
    sb = nc.sbuf_tensor

    S0 = ExitStack()
    ones_cb = S0.enter_context(sb("ones_cb", [128, 1], BF16))
    ones_cf = S0.enter_context(sb("ones_cf", [128, 1], F32))
    ones_rb = S0.enter_context(sb("ones_rb", [1, 128], BF16))
    ones_rf = S0.enter_context(sb("ones_rf", [1, 128], F32))
    one1b = S0.enter_context(sb("one1b", [1, 1], BF16))
    zcol = S0.enter_context(sb("zcol", [128, 1], F32))
    epscol = S0.enter_context(sb("epscol", [128, 1], F32))
    thr = S0.enter_context(sb("thr", [128, 8], F32))
    gpart = S0.enter_context(sb("gpart", [128, 4], F32))
    gred = S0.enter_context(sb("gred", [128, 1], F32))
    biascol = S0.enter_context(sb("biascol", [128, 8], F32))
    gidx_sb = S0.enter_context(sb("gidx_sb", [128, 16], U32))
    rows_f = S0.enter_context(sb("rows_f", [1, 2048], F32))
    rows_b = S0.enter_context(sb("rows_b", [1, 2048], BF16))
    negws = S0.enter_context(sb("negws", [1, 1536], BF16))
    rdenb_sb = [S0.enter_context(sb(f"rdenb_sb{i}", [128, 512], BF16))
                for i in range(2)]
    rcolv = S0.enter_context(sb("rcolv", [128, 4], F32))

    R_M = rows_f[:, 0:512]
    R_VAR = rows_f[:, 512:1024]
    R_T1 = rows_f[:, 1024:1536]
    R_G = rows_f[:, 1536:1540]
    R_TT = rows_f[:, 1540:1544]
    R_NT = rows_f[:, 1544:1548]
    R_KAP = rows_f[:, 1548:1552]
    R_QMX = rows_f[:, 1552:1553]
    R_KMX = rows_f[:, 1553:1554]
    R_PRD = rows_f[:, 1554:1555]
    R_SQT = rows_f[:, 1555:1556]
    R_SQP = rows_f[:, 1556:1557]
    R_M2A = rows_f[:, 0:1024]
    R_V2A = rows_f[:, 1024:2048]

    RB_M = rows_b[:, 0:512]
    RB_RST = rows_b[:, 512:1024]
    RB_DEN = rows_b[:, 1024:1536]
    RB_BV = rows_b[:, 1536:1537]
    RB_R2 = rows_b[:, 0:1024]
    RB_MR2 = rows_b[:, 1024:2048]

    def dma(q, out_ap, in_ap, reads=(), writes=(), sem=None, waits=()):
        return mk.op(q, lambda e, o=out_ap, i=in_ap: e.dma_start(out=o, in_=i),
                     reads=reads, writes=writes, sem=sem,
                     inc=16 if sem else None, waits=waits)

    for buf, valv in ((ones_cb, 1.0), (ones_cf, 1.0), (ones_rb, 1.0),
                      (ones_rf, 1.0), (one1b, 1.0), (zcol, 0.0),
                      (epscol, EPS_LN)):
        mk.op("vector", lambda e, b=buf, v=valv: e.memset(b[:, :], v),
              writes=["consts"])
    mk.op("vector", lambda e: e.memset(gpart[:, :], 0.0), writes=["gpart"])
    mk.op("vector", lambda e: e.memset(R_KAP, 0.0), writes=["kap"])
    dma("sync", gidx_sb[:, :], gidx, writes=["gidx_sb"], sem="dI")

    S1b = ExitStack()
    k_res = S1b.enter_context(sb("k_res", [128, NHC * S], BF16))
    v_sb = S1b.enter_context(sb("v_sb", [128, 32 * 512], BF16))
    S1c = ExitStack()
    q_sb = [S1c.enter_context(sb(f"q_sb{i}", [128, NHC * 512], BF16))
            for i in range(2)]
    masks_sb = S1c.enter_context(sb("masks_sb", [128, 4 * 512], BF16))
    dma("sync", masks_sb[:, :].rearrange("p (m s) -> p m s", m=4),
        masks.rearrange("m p s -> p m s"), writes=["masks"], sem="dI")
    probs = [S1c.enter_context(sb(f"probs{i}", [128, 1024], BF16)) for i in range(2)]
    tmpsw = S1c.enter_context(sb("tmpsw", [128, 512], BF16))
    tmpA = S1c.enter_context(sb("tmpA", [128, 512], BF16))
    sqs = tmpsw
    xsq = [S1c.enter_context(sb(f"xsq{i}", [128, 512], BF16)) for i in range(2)]
    ctxev = [S1c.enter_context(sb(f"ctxev{i}", [128, 512], BF16)) for i in range(8)]

    S1 = ExitStack()
    x_bf = [S1.enter_context(sb(f"x_bf{i}", [128, NKC * 512], BF16)) for i in range(2)]
    wq_sb = S1.enter_context(sb("wq_sb", [128, NKC * 512], BF16))
    wk_sb = S1.enter_context(sb("wk_sb", [128, NKC * 512], BF16))
    wv_sb = S1.enter_context(sb("wv_sb", [128, NKC * 512], BF16))
    rc_raw = [S1.enter_context(sb(f"rc_raw{i}", [128, TTK], BF16))
              for i in range(2)]
    rs_raw = [S1.enter_context(sb(f"rs_raw{i}", [128, TTK], BF16))
              for i in range(2)]

    for nm, dst, srcw in (("wq", wq_sb, wq_t), ("wk", wk_sb, wk_t),
                          ("wv", wv_sb, wv_t)):
        dma("gpsimd", dst[:, :], srcw, writes=[nm], sem="dIg")

    dma("sync", rc_raw[0][:, :], rope2[0, :, 0:TTK], writes=["rcraw0"],
        sem="dRc")
    dma("sync", rs_raw[0][:, :], rope2[1, :, 0:TTK], writes=["rsraw0"],
        sem="dRs")
    for _b in ("gidx_sb", "masks"):
        mk.setmark(_b, "dI")
    for _b in ("wq", "wk", "wv", "ropec_t", "ropes_t"):
        mk.setmark(_b, "dIg")

    def emit_xcast(t):
        par = t % 2
        dma("gpsimd", x_bf[par][:, :], x_t[t],
            writes=[f"xbf{par}"], sem=f"dX{par}")

    NWSL = 5
    wsl = [S1.enter_context(sb(f"wsl{i}", [128, 512], F32)) for i in range(NWSL)]
    qA = [S1.enter_context(sb(f"qA{i}", [128, 512], BF16)) for i in range(2)]
    qB = S1.enter_context(sb("qB", [128, 512], BF16))

    qpieces = []
    for a in range(2):
        for cidx in range(11):
            qpieces.append((0, w1_sl[128 * a:128 * a + 128,
                                     512 * cidx:512 * cidx + 512], 512,
                            w1q_in.ap()[128 * a:128 * a + 128,
                                        512 * cidx:512 * cidx + 512], 128))
    for a in range(2):
        for cidx in range(11):
            qpieces.append((1, w2_sl[128 * a:128 * a + 128,
                                     512 * cidx:512 * cidx + 512], 512,
                            w2q_in.ap()[128 * a:128 * a + 128,
                                        512 * cidx:512 * cidx + 512], 128))
    for a in range(6):
        rows = 64 if a == 5 else 128
        for cidx in range(4):
            qpieces.append((2, w3_sl[128 * a:128 * a + 128,
                                     512 * cidx:512 * cidx + 512], 512,
                            w3q_in.ap()[128 * a:128 * a + rows,
                                        512 * cidx:512 * cidx + 512], rows))
    NPIECE = len(qpieces)

    def gamma_piece(i):
        m, srcp, width, _, _ = qpieces[i]
        par = i % NWSL
        dma("sync", wsl[par][:, 0:width], srcp, writes=[f"wsl{par}"],
            sem=f"dV{par}")
        mk.op("vector",
              lambda e, par=par, w=width: e.reduce_sum(
                  gred[:, 0:1], wsl[par][:, 0:w], axis=AXX,
                  apply_absolute_value=True),
              reads=[f"wsl{par}"], writes=["gred"])
        mk.op("vector",
              lambda e, m=m: e.tensor_add(gpart[:, m:m + 1], gpart[:, m:m + 1],
                                          gred[:, 0:1]),
              reads=["gred", "gpart"], writes=["gpart"])

    def emit_ar():
        mk.op("tensor", lambda e: e.matmul(Q[0][0:1, 0:4], ones_cf[:, :],
                                           gpart[:, :], start=True, stop=True),
              reads=["gpart", "consts"], writes=["Qb0"])
        mk.op("vector", lambda e: e.tensor_copy(R_G, Q[0][0:1, 0:4]),
              reads=["Qb0"], writes=["R_G"])
        dma("sync", g_in.ap(), R_G, reads=["R_G"], writes=["g_in"], sem="dGi")
        mk.op("gpsimd",
              lambda e: e.collective_compute(
                  "AllReduce", ALU.add, replica_groups=[list(range(8))],
                  ins=[g_in.ap().opt()], outs=[g_out.ap().opt()]),
              reads=["g_in"], writes=["g_outb"], sem="cG", inc=1)
        dma("gpsimd", R_G, g_out.ap(), reads=["g_outb", "R_G"],
            writes=["R_G"], sem="dGo")
        mk.op("gpsimd", lambda e: e.tensor_scalar(R_TT, R_G, 0.5 / NW, 0.5e-8,
                                                  ALU.mult, ALU.add),
              reads=["R_G"], writes=["R_TT"])
        mk.op("gpsimd", lambda e: e.tensor_scalar_mul(R_NT, R_TT, -1.0),
              reads=["R_TT"], writes=["R_NT"])

    def emit_thr_bcast():
        mk.op("tensor", lambda e: e.matmul(Q[0][:, 0:4], ones_rf[:, :], R_TT,
                                           start=True, stop=True),
              reads=["R_TT", "consts"], writes=["Qb0"])
        mk.op("tensor", lambda e: e.matmul(Q[0][:, 4:8], ones_rf[:, :], R_NT,
                                           start=True, stop=True),
              reads=["R_NT", "consts"], writes=["Qb0"])
        mk.op("vector", lambda e: e.tensor_copy(thr[:, 0:8], Q[0][:, 0:8]),
              reads=["Qb0"], writes=["gthr"])

    def quant_piece(i):
        m, srcp, width, dst, rows = qpieces[i]
        par = i % NWSL
        qp = i % 2
        dma("sync", wsl[par][:, 0:width], srcp, writes=[f"wsl{par}"],
            sem=f"dV{par}")
        mk.op("vector",
              lambda e, par=par, qp=qp, w=width, m=m: e.tensor_scalar(
                  qA[qp][:, 0:w], wsl[par][:, 0:w], thr[:, m:m + 1], 2.0,
                  ALU.is_gt, ALU.mult),
              reads=[f"wsl{par}", "gthr"], writes=[f"qA{qp}"])
        mk.op("vector",
              lambda e, par=par, w=width, m=m: e.tensor_scalar(
                  qB[:, 0:w], wsl[par][:, 0:w], thr[:, 4 + m:5 + m], 2.0,
                  ALU.is_gt, ALU.mult),
              reads=[f"wsl{par}", "gthr"], writes=["qB"])
        mk.op("vector",
              lambda e, qp=qp, w=width: e.scalar_tensor_tensor(
                  qA[qp][:, 0:w], qA[qp][:, 0:w], -2.0, qB[:, 0:w],
                  ALU.add, ALU.add),
              reads=[f"qA{qp}", "qB"], writes=[f"qA{qp}"])
        dma("sync", dst, qA[qp][0:rows, 0:width],
            reads=[f"qA{qp}"], writes=[f"wq_in{m}"],
            sem=("dQ" if qp == 0 else "dQ1"))

    emit_xcast(0)
    emit_xcast(1)
    qfree = []

    bg = {}

    def bg_add(t, h, fn):
        bg.setdefault((t, h), []).append(fn)

    bg_add(2, 0, emit_thr_bcast)
    gam_q = list(range(NPIECE))
    qnt_q = list(range(NPIECE))


    for j, (nm, wsb) in enumerate((("wq", wq_sb), ("wk", wk_sb), ("wv", wv_sb))):
        def wsum_mm(e, wsb=wsb):
            last = None
            for k in range(NKC):
                last = e.matmul(Q[1][0:1, 0:512], ones_cb[:, :],
                                wsb[:, 512 * k:512 * k + 512],
                                start=(k == 0), stop=(k == NKC - 1))
            return last
        mk.op("tensor", wsum_mm, reads=[nm, "consts"], writes=["Qb2"])
        mk.op("vector",
              lambda e, j=j: e.tensor_scalar_mul(negws[:, 512 * j:512 * j + 512],
                                                 Q[1][0:1, 0:512], -1.0),
              reads=["Qb2"], writes=[f"negws{j}"])

    def quiesce():
        allw = [(s, mk.cnt[s]) for s in sem_names if mk.cnt[s] > 0]
        for qn in ENGS:
            mk.wait_only(qn, waits=allw)

    def finish(nc=nc):
        mk.finalize_plan()
        with nc.Block() as blk:
            @blk.tensor
            def _(eng):
                for f in mk.plan["tensor"]:
                    f(eng)

            @blk.vector
            def _(eng):
                for f in mk.plan["vector"]:
                    f(eng)

            @blk.scalar
            def _(eng):
                for f in mk.plan["scalar"]:
                    f(eng)

            @blk.sync
            def _(eng):
                for f in mk.plan["sync"]:
                    f(eng)

            @blk.gpsimd
            def _(eng):
                for f in mk.plan["gpsimd"]:
                    f(eng)
        return nc

    if STOP <= 1:
        quiesce()
        return finish()


    def gen_qkv(t):
        par = t % 2
        xb = x_bf[par]
        xkey = f"xbf{par}"
        qsb = q_sb[par]

        if t + 1 < NT:
            nxp = (t + 1) % 2
            dma("sync", rc_raw[nxp][:, :], rope2[0, :, TTK * (t + 1):TTK * (t + 2)],
                writes=[f"rcraw{nxp}"], sem=("dRc" if nxp == 0 else "dRc1"))
            dma("sync", rs_raw[nxp][:, :], rope2[1, :, TTK * (t + 1):TTK * (t + 2)],
                writes=[f"rsraw{nxp}"], sem=("dRs" if nxp == 0 else "dRs1"))
        def sum_mm(e, xb=xb):
            last = None
            for k in range(NKC):
                last = e.matmul(Q[0][0:1, 0:512], ones_cb[:, :],
                                xb[:, 512 * k:512 * k + 512],
                                start=(k == 0), stop=(k == NKC - 1))
            return last
        mk.op("tensor", sum_mm, reads=[xkey, "consts"], writes=["Qb0"])
        yield
        for k in range(NKC):
            xp = k % 2
            mk.op("vector",
                  lambda e, xb=xb, k=k, xp=xp: e.tensor_mul(
                      xsq[xp][:, :], xb[:, 512 * k:512 * k + 512],
                      xb[:, 512 * k:512 * k + 512]),
                  reads=[xkey], writes=[f"xsq{xp}"])
            mk.op("tensor",
                  lambda e, k=k, xp=xp: e.matmul(
                      Q[0][0:1, 512:1024], ones_cb[:, :], xsq[xp][:, :],
                      start=(k == 0), stop=(k == NKC - 1)),
                  reads=[f"xsq{xp}", "consts"], writes=["Qb1"])
            if k % 4 == 3:
                yield
        for ss in (0, 1):
            def v_main(e, xb=xb, ss=ss):
                last = None
                for k in range(NKC):
                    last = e.matmul(qb(2 + ss),
                                    xb[:, 512 * k + 128 * ss:512 * k + 128 * ss + 128],
                                    wv_sb[:, 512 * k:512 * k + 512],
                                    start=(k == 0), stop=False)
                return last
            mk.op("tensor", v_main, reads=[xkey, "wv"], writes=[bkey(2 + ss)])
            yield
        mk.op("vector", lambda e: e.tensor_scalar_mul(R_M, Q[0][0:1, 0:512], 1.0 / H),
              reads=["Qb0"], writes=["R_M"])
        mk.op("vector", lambda e: e.tensor_scalar_mul(R_VAR, Q[0][0:1, 512:1024], 1.0 / H),
              reads=["Qb1"], writes=["R_VAR"])
        mk.op("vector", lambda e: e.tensor_mul(R_T1, R_M, R_M),
              reads=["R_M"], writes=["R_T1"])
        mk.op("vector", lambda e: e.tensor_sub(R_VAR, R_VAR, R_T1),
              reads=["R_VAR", "R_T1"], writes=["R_VAR"])
        mk.op("scalar", lambda e: e.activation(R_T1, R_VAR, AF.Ln,
                                               bias=epscol[0:1, 0:1], scale=1.0),
              reads=["R_VAR"], writes=["R_T1"])
        mk.op("scalar", lambda e: e.activation(R_VAR, R_T1, AF.Exp,
                                               bias=zcol[0:1, 0:1], scale=-0.5),
              reads=["R_T1"], writes=["R_VAR"])
        mk.op("vector", lambda e: e.tensor_copy(RB_M, R_M),
              reads=["R_M"], writes=["RB_M"])
        mk.op("vector", lambda e: e.tensor_copy(RB_RST, R_VAR),
              reads=["R_VAR"], writes=["RB_RST"])
        yield
        def rcol_mm(e):
            last = None
            for ss in range(4):
                last = e.matmul(Q[0][:, ss:ss + 1],
                                RB_RST[:, 128 * ss:128 * ss + 128],
                                one1b[:, :], start=True, stop=True)
            return last
        mk.op("tensor", rcol_mm, reads=["RB_RST", "consts"], writes=["Qb0"])
        mk.op("vector", lambda e: e.tensor_copy(rcolv[:, :], Q[0][:, 0:4]),
              reads=["Qb0"], writes=["rcolv"])
        mk.op("tensor", lambda e: e.matmul(Q[0][:, 512:1024], ones_rb[:, :],
                                           RB_RST, start=True, stop=True),
              reads=["RB_RST", "consts"], writes=["Qb1"])
        mk.op("vector",
              lambda e, par=par: e.tensor_mul(rc_raw[par][:, :],
                                              rc_raw[par][:, :],
                                              Q[0][:, 512:1024]),
              reads=[f"rcraw{par}", "Qb1"], writes=[f"rcraw{par}"])
        mk.op("vector",
              lambda e, par=par: e.tensor_mul(rs_raw[par][:, :],
                                              rs_raw[par][:, :],
                                              Q[0][:, 512:1024]),
              reads=[f"rsraw{par}", "Qb1"], writes=[f"rsraw{par}"])
        yield
        for ss in (0, 1):
            mk.op("tensor",
                  lambda e, ss=ss: e.matmul(qb(2 + ss),
                                            RB_M[:, 128 * ss:128 * ss + 128],
                                            negws[:, 1024:1536],
                                            start=False, stop=True),
                  reads=["RB_M", "negws2"], writes=[bkey(2 + ss)])
            gs = 4 * t + ss
            mk.op("vector",
                  lambda e, ss=ss, gs=gs: e.tensor_scalar(
                      v_sb[:, 512 * gs:512 * gs + 512], qb(2 + ss),
                      rcolv[:, ss:ss + 1], None, ALU.mult),
                  reads=[bkey(2 + ss), "rcolv"], writes=[f"v_s{gs}"])
            yield
        for ss in (2, 3):
            def v_mm2(e, xb=xb, ss=ss):
                for k in range(NKC):
                    e.matmul(qb(ss),
                             xb[:, 512 * k + 128 * ss:512 * k + 128 * ss + 128],
                             wv_sb[:, 512 * k:512 * k + 512],
                             start=(k == 0), stop=False)
                return e.matmul(qb(ss), RB_M[:, 128 * ss:128 * ss + 128],
                                negws[:, 1024:1536], start=False, stop=True)
            mk.op("tensor", v_mm2, reads=[xkey, "wv", "RB_M", "negws2"],
                  writes=[bkey(ss)])
            gs = 4 * t + ss
            mk.op("vector",
                  lambda e, ss=ss, gs=gs: e.tensor_scalar(
                      v_sb[:, 512 * gs:512 * gs + 512], qb(ss),
                      rcolv[:, ss:ss + 1], None, ALU.mult),
                  reads=[bkey(ss), "rcolv"], writes=[f"v_s{gs}"])
            yield
        for h in range(NHC):
            for iq in range(2):
                is_q = (iq == 0)
                bank = 2 * (h % 2) + iq
                wsb, wkey, nwj = ((wq_sb, "wq", 0) if is_q else (wk_sb, "wk", 1))

                def qk_mm(e, xb=xb, h=h, wsb=wsb, bank=bank, nwj=nwj):
                    for k in range(NKC):
                        e.matmul(qb(bank),
                                 wsb[:, 512 * k + 128 * h:512 * k + 128 * h + 128],
                                 xb[:, 512 * k:512 * k + 512],
                                 start=(k == 0), stop=False)
                    return e.matmul(
                        qb(bank),
                        negws[:, 512 * nwj + 128 * h:512 * nwj + 128 * h + 128],
                        RB_M, start=False, stop=True)
                mk.op("tensor", qk_mm, reads=[xkey, wkey, "RB_M", f"negws{nwj}"],
                      writes=[bkey(bank)])
                mk.op("vector",
                      lambda e, bank=bank: e.tensor_copy(tmpsw[0:64, :],
                                                         qb(bank)[64:128, :]),
                      reads=[bkey(bank)], writes=["tmpsw_a"])
                mk.op("vector",
                      lambda e, bank=bank: e.tensor_copy(tmpsw[64:128, :],
                                                         qb(bank)[0:64, :]),
                      reads=[bkey(bank)], writes=["tmpsw_b"])
                if is_q:
                    dst = qsb[:, 512 * h:512 * h + 512]
                    dkey = f"q_h{h}_p{par}"
                else:
                    dst = k_res[:, S * h + TTK * t: S * h + TTK * t + 512]
                    dkey = f"k_h{h}_t{t}"
                mk.op("vector",
                      lambda e, bank=bank, par=par: e.tensor_mul(
                          tmpA[:, :], qb(bank), rc_raw[par][:, :]),
                      reads=[bkey(bank), f"rcraw{par}"], writes=["tmpA"])
                mk.op("vector",
                      lambda e, par=par: e.tensor_mul(tmpsw[:, :], tmpsw[:, :],
                                                      rs_raw[par][:, :]),
                      reads=["tmpsw_a", "tmpsw_b", f"rsraw{par}"],
                      writes=["tmpsw_a", "tmpsw_b"])
                mk.op("vector",
                      lambda e, dst=dst: e.tensor_add(dst, tmpA[:, :],
                                                      tmpsw[:, :]),
                      reads=["tmpsw_a", "tmpsw_b", "tmpA"],
                      writes=[dkey])
                yield
            cb = 2 * (h % 2)
            mk.op("vector",
                  lambda e, h=h, qsb=qsb: e.tensor_mul(
                      sqs[:, :], qsb[:, 512 * h:512 * h + 512],
                      qsb[:, 512 * h:512 * h + 512]),
                  reads=[f"q_h{h}_p{par}"], writes=["tmpsw_a", "tmpsw_b"])
            mk.op("tensor", lambda e, cb=cb: e.matmul(
                      qrow(cb), ones_cb[:, :], sqs[:, :], start=True, stop=True),
                  reads=["tmpsw_a", "tmpsw_b", "consts"], writes=[bkey(cb)])
            mk.op("vector", lambda e, cb=cb: e.reduce_max(R_QMX, qrow(cb),
                                                          axis=AXX),
                  reads=[bkey(cb)], writes=["R_QMX"])
            mk.op("vector",
                  lambda e, h=h, t=t: e.tensor_mul(
                      sqs[:, :], k_res[:, S * h + TTK * t:S * h + TTK * t + 512],
                      k_res[:, S * h + TTK * t:S * h + TTK * t + 512]),
                  reads=[f"k_h{h}_t{t}"], writes=["tmpsw_a", "tmpsw_b"])
            mk.op("tensor", lambda e, cb=cb: e.matmul(
                      qrow(cb), ones_cb[:, :], sqs[:, :], start=True, stop=True),
                  reads=["tmpsw_a", "tmpsw_b", "consts"], writes=[bkey(cb)])
            mk.op("vector", lambda e, cb=cb: e.reduce_max(R_KMX, qrow(cb),
                                                          axis=AXX),
                  reads=[bkey(cb)], writes=["R_KMX"])
            mk.op("vector",
                  lambda e, h=h: e.tensor_max(R_KAP[:, h:h + 1],
                                              R_KAP[:, h:h + 1], R_KMX),
                  reads=["R_KMX", "kap"], writes=["kap"])
            mk.op("vector",
                  lambda e, h=h: e.tensor_mul(R_PRD, R_QMX, R_KAP[:, h:h + 1]),
                  reads=["R_QMX", "kap"], writes=["R_PRD"])
            mk.op("scalar", lambda e: e.activation(R_SQP, R_PRD, AF.Ln,
                                                   bias=epscol[0:1, 0:1],
                                                   scale=1.0),
                  reads=["R_PRD"], writes=["R_SQP"])
            mk.op("scalar", lambda e: e.activation(R_SQT, R_SQP, AF.Exp,
                                                   bias=zcol[0:1, 0:1],
                                                   scale=0.5),
                  reads=["R_SQP"], writes=["R_SQT"])
            mk.op("vector", lambda e: e.tensor_scalar_mul(RB_BV, R_SQT, -CSC),
                  reads=["R_SQT"], writes=["RB_BV"])
            mk.op("tensor", lambda e, cb=cb: e.matmul(
                      Q[cb // 2][:, 512 * (cb % 2):512 * (cb % 2) + 1],
                      ones_rb[:, :], RB_BV, start=True, stop=True),
                  reads=["RB_BV", "consts"], writes=[bkey(cb)])
            bcn = 4 * par + h
            mk.op("vector",
                  lambda e, bcn=bcn, cb=cb: e.tensor_copy(
                      biascol[:, bcn:bcn + 1],
                      Q[cb // 2][:, 512 * (cb % 2):512 * (cb % 2) + 1]),
                  reads=[bkey(cb)], writes=[f"bias_h{h}_p{par}"])
            yield

    def gen_att(ta):
        par = ta % 2
        qsb = q_sb[par]
        for h in range(NHC):
            kcn = 4 * ta + 4
            npair = kcn // 2

            def emit_scores(j, hh=h):
                for half in range(2):
                    kc = 2 * j + half
                    mk.op("tensor",
                          lambda e, half=half, hh=hh, kc=kc: e.matmul(
                              Q[2][:, 512 * half:512 * half + 512],
                              k_res[:, S * hh + 128 * kc:S * hh + 128 * kc + 128],
                              qsb[:, 512 * hh:512 * hh + 512],
                              start=True, stop=True),
                          reads=[f"k_h{hh}_t{kc // 4}", f"q_h{hh}_p{par}"],
                          writes=[bkey(4 + half)])
            if h == 0:
                emit_scores(0)
            for j in range(npair):
                pb = probs[j % 2]
                pkey = f"probs{j % 2}"
                bcn = 4 * par + h
                for half in range(2):
                    mk.op("scalar",
                          lambda e, pb=pb, half=half, bcn=bcn: e.activation(
                              pb[:, 512 * half:512 * half + 512],
                              Q[2][:, 512 * half:512 * half + 512], AF.Exp,
                              bias=biascol[:, bcn:bcn + 1], scale=CSC),
                          reads=[bkey(4 + half), f"bias_h{h}_p{par}"],
                          writes=[pkey])
                for half in range(2):
                    kc = 2 * j + half
                    a = kc - 4 * ta
                    if a >= 0:
                        mk.op("vector",
                              lambda e, pb=pb, half=half, a=a: e.tensor_mul(
                                  pb[:, 512 * half:512 * half + 512],
                                  pb[:, 512 * half:512 * half + 512],
                                  masks_sb[:, 512 * a:512 * a + 512]),
                              reads=[pkey, "masks"], writes=[pkey])
                if j + 1 < npair:
                    emit_scores(j + 1)
                for half in range(2):
                    kc = 2 * j + half
                    mk.op("tensor",
                          lambda e, pb=pb, half=half, st0=(kc == 0),
                          sp=(kc == kcn - 1): e.matmul(
                              qrow(7), ones_cb[:, :],
                              pb[:, 512 * half:512 * half + 512],
                              start=st0, stop=sp),
                          reads=[pkey, "consts"], writes=["Qb7"])
                    mk.op("tensor",
                          lambda e, pb=pb, half=half, kc=kc, h=h,
                          st0=(kc == 0), sp=(kc == kcn - 1): e.matmul(
                              qb(6),
                              v_sb[:, 512 * kc + 128 * h:512 * kc + 128 * h + 128],
                              pb[:, 512 * half:512 * half + 512],
                              start=st0, stop=sp),
                          reads=[pkey, f"v_s{kc}"], writes=["Qb6"])
                yield
            if h + 1 < NHC:
                emit_scores(0, h + 1)
            mk.op("scalar", lambda e: e.activation(
                      R_T1, qrow(7), AF.Ln, bias=zcol[0:1, 0:1], scale=1.0),
                  reads=["Qb7"], writes=["R_T1"])
            mk.op("scalar", lambda e: e.activation(
                      RB_DEN, R_T1, AF.Exp, bias=zcol[0:1, 0:1], scale=-1.0),
                  reads=["R_T1"], writes=["RB_DEN"])
            mk.op("tensor", lambda e: e.matmul(
                      qb(7), ones_rb[:, :], RB_DEN, start=True, stop=True),
                  reads=["RB_DEN", "consts"], writes=["Qb7"])
            mk.op("vector", lambda e: e.tensor_copy(rdenb_sb[0][:, :], qb(7)),
                  reads=["Qb7"], writes=["rdenb0"])
            cp = (ta % 2) * 4 + h
            cw = []
            if ta >= 2:
                cw.append((f"dC{(ta - 2) // 2}", 128))
            mk.op("vector",
                  lambda e, cp=cp: e.tensor_mul(
                      ctxev[cp][:, :], qb(6), rdenb_sb[0][:, :]),
                  reads=["Qb6", "rdenb0"], writes=[f"ctxev{cp}"],
                  waits=cw)
            dma("sync",
                cag_in[ta // 2].ap()[128 * h:128 * h + 128,
                                     512 * (ta % 2):512 * (ta % 2) + 512],
                ctxev[cp][:, :],
                reads=[f"ctxev{cp}"], writes=[f"cagin{ta // 2}"],
                sem=f"dC{ta // 2}")
            for fn in bg.get((ta, h), ()):
                fn()
            yield

    def drain(g):
        if g is None:
            return
        for _ in g:
            pass

    for t in range(min(NT, MKTILES)):
        gq = gen_qkv(t)
        ga = gen_att(t - 1) if (t >= 1 and MKATT) else None
        rnd = 0
        while True:
            rnd += 1
            qdone = adone = False
            try:
                next(gq)
            except StopIteration:
                qdone = True
            if ga is not None:
                try:
                    next(ga)
                except StopIteration:
                    adone = True
            else:
                adone = True
            if t <= 2 and gam_q:
                gamma_piece(gam_q.pop(0))
            elif ((t == 3 and rnd >= 9) or t in (4, 5)) and qnt_q:
                quant_piece(qnt_q.pop(0))
            if qdone:
                if not adone and ga is not None:
                    drain(ga)
                break
        ta = t - 1
        if t + 2 < min(NT, MKTILES):
            emit_xcast(t + 2)
        if ta >= 1 and ta % 2 == 1 and MKATT and MKAG:
            kq = ta // 2
            mk.op("gpsimd",
                  lambda e, kq=kq: e.collective_compute(
                      "AllGather", ALU.bypass,
                      replica_groups=[[0, 1, 2, 3], [4, 5, 6, 7]],
                      ins=[cag_in[kq].ap().opt()],
                      outs=[cag_out.ap()[2048 * kq:2048 * kq + 2048, :]]),
                  reads=[f"cagin{kq}"], writes=[f"cagout{kq}"],
                  sem=f"cX{kq}", inc=1)
        if ta == 1:
            while gam_q:
                gamma_piece(gam_q.pop(0))
            emit_ar()
        if t == 5:
            while qnt_q:
                quant_piece(qnt_q.pop(0))
            mk.op("gpsimd",
                  lambda e: e.collective_compute(
                      "AllGather", ALU.bypass, replica_groups=[list(range(8))],
                      ins=[w1q_in.ap().opt()], outs=[w1q_all.ap().opt()]),
                  reads=["wq_in0"], writes=["w1q_all"], sem="cW1", inc=1,
                  waits=[("dQ", mk.cnt["dQ"]), ("dQ1", mk.cnt["dQ1"])])
        if t == 6:
            mk.op("gpsimd",
                  lambda e: e.collective_compute(
                      "AllGather", ALU.bypass, replica_groups=[list(range(8))],
                      ins=[w2q_in.ap().opt()], outs=[w2q_all.ap().opt()]),
                  reads=["wq_in1"], writes=["w2q_all"], sem="cW2", inc=1,
                  waits=[("dQ", mk.cnt["dQ"]), ("dQ1", mk.cnt["dQ1"])])
        if t == 7:
            mk.op("gpsimd",
                  lambda e: e.collective_compute(
                      "AllGather", ALU.bypass, replica_groups=[list(range(8))],
                      ins=[w3q_in.ap().opt()], outs=[w3q_all.ap().opt()]),
                  reads=["wq_in2"], writes=["w3q_all"], sem="cW3", inc=1,
                  waits=[("dQ", mk.cnt["dQ"]), ("dQ1", mk.cnt["dQ1"])])

    qkv_done = [("T", mk.cnt["T"]), ("V", mk.cnt["V"]), ("S", mk.cnt["S"]),
                ("dQ", mk.cnt["dQ"]), ("dQ1", mk.cnt["dQ1"])]
    S1.close()
    Sr = ExitStack()
    res1 = Sr.enter_context(sb("res1", [128, NKC * 1024], F32))
    Sb1a = ExitStack()
    wo_sb = [Sb1a.enter_context(sb(f"wo_sb{i}", [128, 2048], BF16))
             for i in range(2)]
    xp2_sb = [Sb1a.enter_context(sb(f"xp2_sb{i}", [128, 1024], F32))
              for i in range(2)]
    r1bf = [probs[0][:, :], probs[1][:, :]]
    r1sq = [q_sb[0][:, 0:1024], q_sb[0][:, 1024:2048]]

    def emit_wo(m):
        slot = m % 2
        dma("gpsimd", wo_sb[slot][:, :], wo_t[m],
            writes=[f"wo{slot}"], waits=qkv_done, sem=f"dO{slot}")

    def emit_xp2(m):
        par = m % 2
        dma("scalar", xp2_sb[par][:, :], xp2_fm[128 * m:128 * m + 128, :],
            writes=[f"xp2{par}"], waits=qkv_done, sem=f"dP{par}")

    emit_wo(0)
    emit_wo(1)
    emit_xp2(0)
    emit_xp2(1)

    if MKATT and min(NT, MKTILES) == NT:
        drain(gen_att(NT - 1))
        mk.op("gpsimd",
              lambda e: e.collective_compute(
                  "AllGather", ALU.bypass,
                  replica_groups=[[0, 1, 2, 3], [4, 5, 6, 7]],
                  ins=[cag_in[3].ap().opt()],
                  outs=[cag_out.ap()[6144:8192, :]]),
              reads=["cagin3"], writes=["cagout3"], sem="cX3", inc=1)

    ph1_done = [("T", mk.cnt["T"]), ("V", mk.cnt["V"]), ("S", mk.cnt["S"]),
                ("dS", mk.cnt["dS"])]

    ctx_fm = k_res
    h2 = v_sb

    for k in range(NKC):
        mk.op("gpsimd",
              lambda e, k=k: e.indirect_dma_start(
                  out=ctx_fm[:, 1024 * k:1024 * k + 1024], out_offset=None,
                  in_=cag_out.ap(),
                  in_offset=IndirectOffsetOnAxis(gidx_sb[:, k:k + 1], axis=0)),
              reads=["cagout0", "cagout1", "cagout2", "cagout3", "gidx_sb"],
              writes=[f"ctxfm{k}"], waits=ph1_done, sem="dGt", inc=16)

    if os.environ.get("MKDBG"):
        dbg_ctx = nc.dram_tensor("dbg_ctx", [H, TOK2], BF16, kind="Internal")
        for k in range(NKC):
            dma("sync", dbg_ctx.ap()[128 * k:128 * k + 128, :],
                ctx_fm[:, 1024 * k:1024 * k + 1024],
                reads=[f"ctxfm{kk}" for kk in range(NKC)],
                writes=[f"dbgctx{k}"], sem="dS")

    if STOP <= 3:
        quiesce()
        return finish()

    for m in range(NKC):
        par = m % 2
        slot = m % 2
        bk0 = 2 * par
        for n in range(2):
            def ao_mm(e, n=n, slot=slot, bk0=bk0):
                last = None
                for k in range(NKC):
                    last = e.matmul(
                        qb(bk0 + n), wo_sb[slot][:, 128 * k:128 * k + 128],
                        ctx_fm[:, 1024 * k + 512 * n:1024 * k + 512 * n + 512],
                        start=(k == 0), stop=(k == NKC - 1))
                return last
            mk.op("tensor", ao_mm,
                  reads=[f"wo{slot}"] + [f"ctxfm{k}" for k in range(NKC)],
                  writes=[bkey(bk0 + n)])
        if m + 2 < NKC:
            emit_wo(m + 2)
        mk.op("vector",
              lambda e, m=m, par=par, bk0=bk0: e.tensor_add(
                  res1[:, 1024 * m:1024 * m + 1024], Q[bk0 // 2][:, :],
                  xp2_sb[par][:, :]),
              reads=[bkey(bk0), bkey(bk0 + 1), f"xp2{par}"],
              writes=[f"res1_{m}"])
        if m + 2 < NKC:
            emit_xp2(m + 2)
        mk.op("vector",
              lambda e, m=m, par=par: e.tensor_copy(
                  r1bf[par][:, :], res1[:, 1024 * m:1024 * m + 1024]),
              reads=[f"res1_{m}"], writes=[f"r1bf{par}"])
        mk.op("vector",
              lambda e, par=par: e.tensor_mul(r1sq[par][:, :], r1bf[par][:, :],
                                              r1bf[par][:, :]),
              reads=[f"r1bf{par}"], writes=[f"r1sq{par}"])
        for n in range(2):
            mk.op("tensor",
                  lambda e, par=par, n=n, m=m: e.matmul(
                      qrow(4 + n), ones_cb[:, :],
                      r1bf[par][:, 512 * n:512 * n + 512],
                      start=(m == 0), stop=(m == NKC - 1)),
                  reads=[f"r1bf{par}", "consts"], writes=[bkey(4 + n)])
            mk.op("tensor",
                  lambda e, par=par, n=n, m=m: e.matmul(
                      qrow(6 + n), ones_cb[:, :],
                      r1sq[par][:, 512 * n:512 * n + 512],
                      start=(m == 0), stop=(m == NKC - 1)),
                  reads=[f"r1sq{par}", "consts"], writes=[bkey(6 + n)])

    mk.op("vector", lambda e: e.tensor_scalar_mul(R_M2A, Q[2][0:1, :], 1.0 / H),
          reads=["Qb4", "Qb5"], writes=["R_M2"])
    mk.op("vector", lambda e: e.tensor_scalar_mul(R_V2A, Q[3][0:1, :], 1.0 / H),
          reads=["Qb6", "Qb7"], writes=["R_V2"])
    mk.op("vector", lambda e: e.tensor_mul(Q[2][0:1, :], R_M2A, R_M2A),
          reads=["R_M2"], writes=["Qb4", "Qb5"])
    mk.op("vector", lambda e: e.tensor_sub(R_V2A, R_V2A, Q[2][0:1, :]),
          reads=["R_V2", "Qb4"], writes=["R_V2"])
    mk.op("scalar", lambda e: e.activation(Q[2][0:1, :], R_V2A, AF.Ln,
                                           bias=epscol[0:1, 0:1], scale=1.0),
          reads=["R_V2"], writes=["Qb4", "Qb5"])
    mk.op("scalar", lambda e: e.activation(R_V2A, Q[2][0:1, :], AF.Exp,
                                           bias=zcol[0:1, 0:1], scale=-0.5),
          reads=["Qb4"], writes=["R_V2"])
    mk.op("vector", lambda e: e.tensor_copy(RB_R2, R_V2A),
          reads=["R_V2"], writes=["RB_R2"])
    mk.op("vector", lambda e: e.tensor_mul(R_M2A, R_M2A, R_V2A),
          reads=["R_M2", "R_V2"], writes=["R_M2"])
    mk.op("vector", lambda e: e.tensor_copy(RB_MR2, R_M2A),
          reads=["R_M2"], writes=["RB_MR2"])
    for n in range(2):
        mk.op("tensor", lambda e, n=n: e.matmul(
            qb(0 + n), ones_rb[:, :], RB_R2[:, 512 * n:512 * n + 512],
            start=True, stop=True),
              reads=["RB_R2", "consts"], writes=[bkey(n)])
        mk.op("tensor", lambda e, n=n: e.matmul(
            qb(2 + n), ones_rb[:, :], RB_MR2[:, 512 * n:512 * n + 512],
            start=True, stop=True),
              reads=["RB_MR2", "consts"], writes=[bkey(2 + n)])

    SLn2 = ExitStack()
    r2b_sb = SLn2.enter_context(sb("r2b_sb", [128, 1024], F32))
    mr2b_sb = SLn2.enter_context(sb("mr2b_sb", [128, 1024], F32))
    h2tmp = SLn2.enter_context(sb("h2tmp", [128, 1024], F32))
    mk.op("vector", lambda e: e.tensor_copy(r2b_sb[:, :], Q[0][:, :]),
          reads=["Qb0", "Qb1"], writes=["r2b"])
    mk.op("vector", lambda e: e.tensor_copy(mr2b_sb[:, :], Q[1][:, :]),
          reads=["Qb2", "Qb3"], writes=["mr2b"])
    for m in range(NKC):
        mk.op("vector",
              lambda e, m=m: e.tensor_mul(h2tmp[:, :],
                                          res1[:, 1024 * m:1024 * m + 1024],
                                          r2b_sb[:, :]),
              reads=[f"res1_{m}", "r2b"], writes=["h2tmp"])
        mk.op("vector",
              lambda e, m=m: e.tensor_sub(h2[:, 1024 * m:1024 * m + 1024],
                                          h2tmp[:, :], mr2b_sb[:, :]),
              reads=["h2tmp", "mr2b"], writes=[f"h2_{m}"])
        dma("scalar", res1_dram.ap()[128 * m:128 * m + 128, :],
            res1[:, 1024 * m:1024 * m + 1024],
            reads=[f"res1_{m}"], writes=[f"r1d{m}"], sem="dR")

    for m in range(NKC):
        mk.setmark(f"r1d{m}", "dR")

    b1free = [("dR", mk.cnt["dR"]), ("T", mk.cnt["T"]), ("V", mk.cnt["V"])]

    SLn2.close()
    Sb1a.close()
    Sr.close()
    S1c.close()

    if STOP <= 4:
        quiesce()
        return finish()

    Shid = ExitStack()
    hidden = Shid.enter_context(sb("hidden", [128, 44 * 1024], BF16))
    w12 = [ctx_fm[:, 0:8192], ctx_fm[:, 8192:16384]]

    NJ = 11
    h2keys = [f"h2_{k}" for k in range(NKC)]
    for j in range(NJ):
        dma("sync", w12[0][:, :].rearrange("p (k c) -> p k c", k=NKC),
            w1q_all.ap().rearrange("(k p) h -> p k h", p=128)[
                :, :, 512 * j:512 * j + 512],
            writes=["w12b0"], sem="dF0",
            waits=([("cW1", 1)] + (b1free if j == 0 else [])))
        dma("sync", w12[1][:, :].rearrange("p (k c) -> p k c", k=NKC),
            w2q_all.ap().rearrange("(k p) h -> p k h", p=128)[
                :, :, 512 * j:512 * j + 512],
            writes=["w12b1"], sem="dF1",
            waits=([("cW2", 1)] + (b1free if j == 0 else [])))
        for cc in range(4):
            c = 4 * j + cc
            b1 = (cc % 2) * 4
            for n in range(2):
                def x1_mm(e, cc=cc, n=n, b1=b1):
                    last = None
                    for k in range(NKC):
                        last = e.matmul(
                            qb(b1 + n),
                            w12[0][:, 512 * k + 128 * cc:512 * k + 128 * cc + 128],
                            h2[:, 1024 * k + 512 * n:1024 * k + 512 * n + 512],
                            start=(k == 0), stop=(k == NKC - 1))
                    return last
                mk.op("tensor", x1_mm, reads=["w12b0"] + h2keys,
                      writes=[bkey(b1 + n)])
            mk.op("scalar",
                  lambda e, c=c, b1=b1: e.activation(
                      hidden[:, 1024 * c:1024 * c + 1024], Q[b1 // 2][:, :],
                      AF.Sigmoid, bias=zcol[:, 0:1], scale=0.5),
                  reads=[bkey(b1), bkey(b1 + 1)], writes=[f"hid_{c}"])
        for cc in range(4):
            c = 4 * j + cc
            b1 = (cc % 2) * 4
            for n in range(2):
                def x2_mm(e, cc=cc, n=n, b1=b1):
                    last = None
                    for k in range(NKC):
                        last = e.matmul(
                            qb(b1 + 2 + n),
                            w12[1][:, 512 * k + 128 * cc:512 * k + 128 * cc + 128],
                            h2[:, 1024 * k + 512 * n:1024 * k + 512 * n + 512],
                            start=(k == 0), stop=(k == NKC - 1))
                    return last
                mk.op("tensor", x2_mm, reads=["w12b1"] + h2keys,
                      writes=[bkey(b1 + 2 + n)])
            mk.op("vector",
                  lambda e, c=c, b1=b1: e.tensor_mul(
                      hidden[:, 1024 * c:1024 * c + 1024],
                      hidden[:, 1024 * c:1024 * c + 1024],
                      Q[b1 // 2 + 1][:, :]),
                  reads=[bkey(b1 + 2), bkey(b1 + 3), f"hid_{c}"],
                  writes=[f"hid_{c}"])

    w3free = [("T", mk.cnt["T"]), ("V", mk.cnt["V"]), ("S", mk.cnt["S"])]

    if STOP <= 5:
        quiesce()
        return finish()

    Sw3 = ExitStack()
    w3s = [v_sb[:, 0:44 * 256], k_res[:, 0:44 * 256]]
    r1s = [Sw3.enter_context(sb(f"r1s{i}", [128, 1024], F32)) for i in range(2)]
    outsb = [Sw3.enter_context(sb(f"outs{i}", [128, 1024], F32)) for i in range(2)]
    hidkeys = [f"hid_{k}" for k in range(44)]

    for mp in range(8):
        par = mp % 2
        for piece in range(2):
            dma("sync",
                w3s[par][:, :].rearrange("p (k c) -> p k c", k=44)[
                    :, 22 * piece:22 * piece + 22, :],
                w3q_all.ap().rearrange("(k p) d -> p k d", p=128)[
                    :, 22 * piece:22 * piece + 22, 256 * mp:256 * mp + 256],
                writes=[f"w3p{piece}_{par}"], sem=("dW3a" if par == 0 else "dW3b"),
                waits=([("cW3", 1)] + (w3free if mp < 2 else [])))
        mk.setmark(f"w3p0_{par}", "dW3a" if par == 0 else "dW3b")
        for mi in range(2):
            m = 2 * mp + mi
            b1 = (m % 2) * 4
            dma("sync", r1s[par][:, :], res1_dram.ap()[128 * m:128 * m + 128, :],
                reads=[f"r1d{m}"], writes=[f"r1s{par}"], sem=f"dL{par}",
                waits=(w3free if m < 2 else []))
            for n in range(2):
                def x3_mm(e, par=par, mi=mi, n=n, b1=b1):
                    last = None
                    for k in range(44):
                        last = e.matmul(
                            qb(b1 + n),
                            w3s[par][:, 256 * k + 128 * mi:256 * k + 128 * mi + 128],
                            hidden[:, 1024 * k + 512 * n:1024 * k + 512 * n + 512],
                            start=(k == 0), stop=(k == 43))
                    return last
                mk.op("tensor", x3_mm,
                      reads=[f"w3p0_{par}", f"w3p1_{par}"] + hidkeys,
                      writes=[bkey(b1 + n)])
            mk.op("vector",
                  lambda e, par=par, b1=b1: e.scalar_tensor_tensor(
                      outsb[par][:, :], Q[b1 // 2][:, :], 0.25, r1s[par][:, :],
                      ALU.mult, ALU.add),
                  reads=[bkey(b1), bkey(b1 + 1), f"r1s{par}"],
                  writes=[f"outs{par}"])
            dma("scalar", out_fm[128 * m:128 * m + 128, :], outsb[par][:, :],
                reads=[f"outs{par}"], writes=[f"outd{m}"], sem=f"dA{par}")

    mk.wait_only("scalar", waits=[("dA0", mk.cnt["dA0"]), ("dA1", mk.cnt["dA1"])])

    Sw3.close()
    Shid.close()
    S1b.close()
    S0.close()

    finish()
    st.close()
    return nc


_NC_CACHE = None


def _rope_tables():
    inv_freq = (1.0 / (np.float32(10000.0) **
                       (np.arange(0, HD, 2, dtype=np.float32) / np.float32(HD)))
                ).astype(np.float32)
    t = np.arange(S, dtype=np.float32)
    freqs = np.einsum("i,j->ij", t, inv_freq).astype(np.float32)
    cos = np.cos(freqs).astype(np.float32).T
    sin = np.sin(freqs).astype(np.float32).T
    ropec = np.concatenate([cos, cos], axis=0)
    ropes = np.concatenate([-sin, sin], axis=0)
    return np.ascontiguousarray(ropec), np.ascontiguousarray(ropes)


def make_in_maps(L0_attn_hidden_states, L0_attn_Wq, L0_attn_Wk, L0_attn_Wv,
                 L0_attn_Wo, L0_ffn_w1, L0_ffn_w2, L0_ffn_w3):
    x = np.asarray(L0_attn_hidden_states, dtype=np.float32)
    Wq = np.asarray(L0_attn_Wq, dtype=np.float32)
    Wk = np.asarray(L0_attn_Wk, dtype=np.float32)
    Wv = np.asarray(L0_attn_Wv, dtype=np.float32)
    Wo = np.asarray(L0_attn_Wo, dtype=np.float32)
    w1 = np.asarray(L0_ffn_w1, dtype=np.float32)
    w2 = np.asarray(L0_ffn_w2, dtype=np.float32)
    w3 = np.asarray(L0_ffn_w3, dtype=np.float32)

    ropec, ropes = _rope_tables()
    rope2b = np.ascontiguousarray(
        np.stack([ropec, ropes]).astype(ml_dtypes.bfloat16))

    masks = np.zeros((4, 128, 512), dtype=np.float32)
    qf = np.arange(512)
    kp = np.arange(128)
    for a in range(4):
        masks[a] = (qf[None, :] >= (128 * a + kp)[:, None]).astype(np.float32)
    masks_bf = masks.astype(ml_dtypes.bfloat16)

    WqT = np.ascontiguousarray(Wq.T)
    WkT = np.ascontiguousarray(Wk.T)
    WvT = np.ascontiguousarray(Wv.T)
    wo_tiled = np.ascontiguousarray(
        Wo.T.reshape(16, 128, 16, 128).transpose(2, 1, 0, 3).reshape(16, 128, 2048))
    w1T = np.ascontiguousarray(w1.T)
    w2T = np.ascontiguousarray(w2.T)
    w3T = np.ascontiguousarray(w3.T)

    xT = [np.ascontiguousarray(x[b].T) for b in range(B)]
    x_tl = [np.ascontiguousarray(
        x[b].reshape(NT, TTK, NKC, 128).transpose(0, 3, 2, 1).reshape(
            NT, 128, NKC * TTK)) for b in range(B)]

    def wtile(WT, g):
        return np.ascontiguousarray(
            WT[:, 512 * g:512 * g + 512].reshape(NKC, 128, 512)
            .transpose(1, 0, 2).reshape(128, NKC * 512))

    in_maps = []
    for c in range(8):
        b, g = c // 4, c % 4
        w3sl = np.zeros((768, H), dtype=np.float32)
        w3sl[:704] = w3T[704 * c:704 * c + 704]
        gidx = np.empty((128, 16), dtype=np.uint32)
        for k in range(16):
            gidx[:, k] = 2048 * g + 128 * k + np.arange(128)
        in_maps.append({
            "x_t": x_tl[b],
            "xp2_fm": np.ascontiguousarray(xT[b][:, 1024 * g:1024 * g + 1024]),
            "wq_t": wtile(WqT, g),
            "wk_t": wtile(WkT, g),
            "wv_t": wtile(WvT, g),
            "wo_t": wo_tiled,
            "w1_sl": np.ascontiguousarray(w1T[256 * c:256 * c + 256]),
            "w2_sl": np.ascontiguousarray(w2T[256 * c:256 * c + 256]),
            "w3_sl": w3sl,
            "rope2": rope2b,
            "masks": masks_bf,
            "gidx": gidx,
        })
    return in_maps


def kernel(L0_attn_hidden_states, L0_attn_Wq, L0_attn_Wk, L0_attn_Wv,
           L0_attn_Wo, L0_ffn_w1, L0_ffn_w2, L0_ffn_w3):
    global _NC_CACHE
    in_maps = make_in_maps(L0_attn_hidden_states, L0_attn_Wq, L0_attn_Wk,
                           L0_attn_Wv, L0_attn_Wo, L0_ffn_w1, L0_ffn_w2,
                           L0_ffn_w3)
    if _NC_CACHE is None:
        _NC_CACHE = build_program()
    nc = _NC_CACHE
    res = run_bass_kernel_spmd(nc, in_maps, core_ids=list(range(8)))

    out = np.empty((B, S, H), dtype=np.float32)
    for c in range(8):
        b, g = c // 4, c % 4
        out[b, 1024 * g:1024 * g + 1024, :] = res.results[c]["out_fm"].T
    return out


# version 25
# speedup vs baseline: 1.1003x; 1.0672x over previous
import os
import numpy as np
import ml_dtypes
from contextlib import ExitStack

import concourse.bass as bass
import concourse.mybir as mybir
from concourse.bass import IndirectOffsetOnAxis
from concourse.bass_utils import run_bass_kernel_spmd

F32 = mybir.dt.float32
BF16 = mybir.dt.bfloat16
U32 = mybir.dt.uint32
AF = mybir.ActivationFunctionType
ALU = mybir.AluOpType
AXX = mybir.AxisListType.X

B, S, H, NH, HD = 2, 4096, 2048, 16, 128
FH = 5632
TTK = 512
NT = S // TTK
NKC = H // 128
NHC = 4
TOK2 = 1024
CSC = float(1.0 / np.sqrt(np.float32(HD)))
EPS_LN = 1e-5
NW = 2048 * 5632

ENGS = ("tensor", "vector", "scalar", "sync", "gpsimd")


class MK:

    def __init__(self, nc, sems):
        self.nc = nc
        self.sems = sems
        self.cnt = {k: 0 for k in sems}
        self.plan = {e: [] for e in ENGS}
        self.last_w = {}
        self.readers = {}
        self.hw = {e: {} for e in ENGS}
        self.used = {}
        self.own = {"tensor": "T", "vector": "V", "scalar": "S",
                    "sync": None, "gpsimd": None}

    def op(self, q, fn, reads=(), writes=(), sem=None, inc=None, waits=()):
        deps = {}

        def add(s, v):
            if deps.get(s, 0) < v:
                deps[s] = v
        for b in reads:
            lw = self.last_w.get(b)
            if lw is not None:
                add(*lw)
        for b in writes:
            lw = self.last_w.get(b)
            if lw is not None:
                add(*lw)
            for s, v in self.readers.get(b, {}).items():
                add(s, v)
        for s, v in waits:
            add(s, v)
        wl = []
        for s, v in deps.items():
            if v <= self.hw[q].get(s, 0):
                continue
            wl.append((s, v))
            self.hw[q][s] = v
        if sem is None:
            sem = {"tensor": "T", "vector": "V", "scalar": "S",
                   "sync": "dS", "gpsimd": "dG"}[q]
            if inc is None:
                inc = 16 if q in ("sync", "gpsimd") else 1
        if inc is None:
            inc = 1
        self.cnt[sem] += inc
        val = self.cnt[sem]
        rec = {"q": q, "fn": fn, "wl": tuple(wl), "sem": sem, "val": val,
               "inc": inc}
        self.plan[q].append(rec)
        for s, v in wl:
            self.used.setdefault(s, set()).add(v)

        mark = (sem, val)
        for b in writes:
            self.last_w[b] = mark
            self.readers.pop(b, None)
        for b in reads:
            d = self.readers.setdefault(b, {})
            if d.get(sem, 0) < val:
                d[sem] = val
        return mark

    def setmark(self, buf, sem, val=None):
        if val is None:
            val = self.cnt[sem]
        self.last_w[buf] = (sem, val)
        self.readers.pop(buf, None)

    def wait_only(self, q, waits=()):
        self.plan[q].append({"q": q, "fn": None, "wl": tuple(waits),
                             "sem": None, "val": 0, "inc": 0})
        for s, v in waits:
            self.used.setdefault(s, set()).add(v)

    def finalize_plan(self):
        prunable = {"T", "V", "S"}
        remap = {}
        for s in prunable:
            used = self.used.get(s, set())
            kept = 0
            m = {0: 0}
            recs = []
            for qn in ENGS:
                for r in self.plan[qn]:
                    if r["sem"] == s:
                        recs.append(r)
            recs.sort(key=lambda r: r["val"])
            for r in recs:
                if r["val"] in used:
                    kept += 1
                    r["newinc"] = r["val"] - (r["val"] - kept)
                    r["keep"] = True
                else:
                    r["keep"] = False
                m[r["val"]] = kept
            remap[s] = m
        sems = self.sems
        out = {}
        for qn in ENGS:
            fns = []
            for r in self.plan[qn]:
                wl = []
                for s, v in r["wl"]:
                    if s in remap:
                        v = remap[s][v]
                        if v == 0:
                            continue
                    wl.append((s, v))
                fn, s_, inc = r["fn"], r["sem"], r["inc"]
                keep = r.get("keep", True) if s_ in ("T", "V", "S") else True

                def run(eng, wl=tuple(wl), fn=fn, s_=s_, inc=inc, keep=keep):
                    for s, v in wl:
                        eng.wait_ge(sems[s], v)
                    if fn is None:
                        return
                    inst = fn(eng)
                    if keep and s_ is not None:
                        inst.then_inc(sems[s_], inc)
                fns.append(run)
            out[qn] = fns
        self.plan = out


def build_program():
    STOP = int(os.environ.get("MKSTOP", "99"))
    MKTILES = int(os.environ.get("MKTILES", "8"))
    MKATT = int(os.environ.get("MKATT", "1"))
    MKAG = int(os.environ.get("MKAG", "1"))
    MKSUB = int(os.environ.get("MKSUB", "7"))
    MKS2 = int(os.environ.get("MKS2", "31"))
    MKS3 = int(os.environ.get("MKS3", "7"))
    nc = bass.Bass("TRN2", target_bir_lowering=False, debug=False)

    x_t = nc.dram_tensor("x_t", [NT, 128, NKC * 512], F32, kind="ExternalInput").ap()
    xp2_fm = nc.dram_tensor("xp2_fm", [H, TOK2], F32, kind="ExternalInput").ap()
    wq_t = nc.dram_tensor("wq_t", [128, NKC * 512], F32, kind="ExternalInput").ap()
    wk_t = nc.dram_tensor("wk_t", [128, NKC * 512], F32, kind="ExternalInput").ap()
    wv_t = nc.dram_tensor("wv_t", [128, NKC * 512], F32, kind="ExternalInput").ap()
    wo_t = nc.dram_tensor("wo_t", [16, 128, 2048], F32, kind="ExternalInput").ap()
    w1_sl = nc.dram_tensor("w1_sl", [256, FH], F32, kind="ExternalInput").ap()
    w2_sl = nc.dram_tensor("w2_sl", [256, FH], F32, kind="ExternalInput").ap()
    w3_sl = nc.dram_tensor("w3_sl", [768, H], F32, kind="ExternalInput").ap()
    rope2 = nc.dram_tensor("rope2", [2, 128, S], BF16, kind="ExternalInput").ap()
    masks = nc.dram_tensor("masks", [4, 128, 512], BF16, kind="ExternalInput").ap()
    gidx = nc.dram_tensor("gidx", [128, 16], U32, kind="ExternalInput").ap()
    out_fm = nc.dram_tensor("out_fm", [H, TOK2], F32, kind="ExternalOutput").ap()

    g_in = nc.dram_tensor("g_in", [1, 4], F32, kind="Internal")
    g_out = nc.dram_tensor("g_out", [1, 4], F32, kind="Internal")
    w1q_in = nc.dram_tensor("w1q_in", [256, FH], BF16, kind="Internal")
    w2q_in = nc.dram_tensor("w2q_in", [256, FH], BF16, kind="Internal")
    w3q_in = nc.dram_tensor("w3q_in", [704, H], BF16, kind="Internal")
    w1q_all = nc.dram_tensor("w1q_all", [H, FH], BF16, kind="Internal")
    w2q_all = nc.dram_tensor("w2q_all", [H, FH], BF16, kind="Internal")
    w3q_all = nc.dram_tensor("w3q_all", [FH, H], BF16, kind="Internal")
    cag_in = [nc.dram_tensor(f"cag_in{k}", [512, 1024], BF16, kind="Internal")
              for k in range(4)]
    cag_out = nc.dram_tensor("cag_out", [8192, 1024], BF16, kind="Internal")
    res1_dram = nc.dram_tensor("res1_dram", [H, TOK2], F32, kind="Internal")

    sem_names = ["T", "V", "S", "dS", "dG", "dA",
                 "cG", "cW1", "cW2", "cW3", "cX0", "cX1", "cX2", "cX3",
                 "dI", "dIg", "dGi", "dGo", "dX0", "dX1", "dW0", "dW1", "dQ",
                 "dV0", "dV1", "dV2", "dV3", "dV4", "dV5", "dV6", "dV7",
                 "dRc1", "dRs1",
                 "dC0", "dC1", "dC2", "dC3", "dGt", "dO0", "dO1",
                 "dP0", "dP1", "dR", "dF0", "dF1", "dW3a", "dW3b",
                 "dL0", "dL1", "dA0", "dA1", "dRc", "dRs", "dQ1", "dW2", "dW3"]

    st = ExitStack()
    sems = {n: st.enter_context(nc.semaphore(n)) for n in sem_names}

    Q = [st.enter_context(nc.psum_tensor(f"Q{i}", [128, 1024], F32))
         for i in range(4)]

    def qb(i):
        return Q[i // 2][:, 512 * (i % 2):512 * (i % 2) + 512]

    def qrow(i):
        return Q[i // 2][0:1, 512 * (i % 2):512 * (i % 2) + 512]

    def bkey(i):
        return f"Qb{i}"

    mk = MK(nc, sems)
    sb = nc.sbuf_tensor

    S0 = ExitStack()
    ones_cb = S0.enter_context(sb("ones_cb", [128, 1], BF16))
    ones_cf = S0.enter_context(sb("ones_cf", [128, 1], F32))
    ones_rb = S0.enter_context(sb("ones_rb", [1, 128], BF16))
    ones_rf = S0.enter_context(sb("ones_rf", [1, 128], F32))
    one1b = S0.enter_context(sb("one1b", [1, 1], BF16))
    zcol = S0.enter_context(sb("zcol", [128, 1], F32))
    epscol = S0.enter_context(sb("epscol", [128, 1], F32))
    thr = S0.enter_context(sb("thr", [128, 8], F32))
    gpart = S0.enter_context(sb("gpart", [128, 4], F32))
    gred = S0.enter_context(sb("gred", [128, 1], F32))
    biascol = S0.enter_context(sb("biascol", [128, 8], F32))
    gidx_sb = S0.enter_context(sb("gidx_sb", [128, 16], U32))
    rows_f = S0.enter_context(sb("rows_f", [1, 2048], F32))
    rows_b = S0.enter_context(sb("rows_b", [1, 2048], BF16))
    negws = S0.enter_context(sb("negws", [1, 1536], BF16))
    rdenb_sb = [S0.enter_context(sb(f"rdenb_sb{i}", [128, 512], BF16))
                for i in range(2)]
    rcolv = S0.enter_context(sb("rcolv", [128, 4], F32))

    R_M = rows_f[:, 0:512]
    R_VAR = rows_f[:, 512:1024]
    R_T1 = rows_f[:, 1024:1536]
    R_G = rows_f[:, 1536:1540]
    R_TT = rows_f[:, 1540:1544]
    R_NT = rows_f[:, 1544:1548]
    R_KAP = rows_f[:, 1548:1552]
    R_QMX = rows_f[:, 1552:1553]
    R_KMX = rows_f[:, 1553:1554]
    R_PRD = rows_f[:, 1554:1555]
    R_SQT = rows_f[:, 1555:1556]
    R_SQP = rows_f[:, 1556:1557]
    R_M2A = rows_f[:, 0:1024]
    R_V2A = rows_f[:, 1024:2048]

    RB_M = rows_b[:, 0:512]
    RB_RST = rows_b[:, 512:1024]
    RB_DEN = rows_b[:, 1024:1536]
    RB_BV = rows_b[:, 1536:1537]
    RB_R2 = rows_b[:, 0:1024]
    RB_MR2 = rows_b[:, 1024:2048]

    def dma(q, out_ap, in_ap, reads=(), writes=(), sem=None, waits=()):
        return mk.op(q, lambda e, o=out_ap, i=in_ap: e.dma_start(out=o, in_=i),
                     reads=reads, writes=writes, sem=sem,
                     inc=16 if sem else None, waits=waits)

    for buf, valv in ((ones_cb, 1.0), (ones_cf, 1.0), (ones_rb, 1.0),
                      (ones_rf, 1.0), (one1b, 1.0), (zcol, 0.0),
                      (epscol, EPS_LN)):
        mk.op("vector", lambda e, b=buf, v=valv: e.memset(b[:, :], v),
              writes=["consts"])
    mk.op("vector", lambda e: e.memset(gpart[:, :], 0.0), writes=["gpart"])
    mk.op("vector", lambda e: e.memset(R_KAP, 0.0), writes=["kap"])
    dma("sync", gidx_sb[:, :], gidx, writes=["gidx_sb"], sem="dI")

    S1b = ExitStack()
    k_res = S1b.enter_context(sb("k_res", [128, NHC * S], BF16))
    v_sb = S1b.enter_context(sb("v_sb", [128, 32 * 512], BF16))
    S1c = ExitStack()
    q_sb = [S1c.enter_context(sb(f"q_sb{i}", [128, NHC * 512], BF16))
            for i in range(2)]
    masks_sb = S1c.enter_context(sb("masks_sb", [128, 4 * 512], BF16))
    dma("sync", masks_sb[:, :].rearrange("p (m s) -> p m s", m=4),
        masks.rearrange("m p s -> p m s"), writes=["masks"], sem="dI")
    probs = [S1c.enter_context(sb(f"probs{i}", [128, 1024], BF16)) for i in range(2)]
    tmpsw = S1c.enter_context(sb("tmpsw", [128, 512], BF16))
    tmpA = S1c.enter_context(sb("tmpA", [128, 512], BF16))
    sqs = tmpsw
    xsq = [S1c.enter_context(sb(f"xsq{i}", [128, 512], BF16)) for i in range(2)]
    ctxev = [S1c.enter_context(sb(f"ctxev{i}", [128, 512], BF16)) for i in range(8)]

    S1 = ExitStack()
    x_bf = [S1.enter_context(sb(f"x_bf{i}", [128, NKC * 512], BF16)) for i in range(2)]
    wq_sb = S1.enter_context(sb("wq_sb", [128, NKC * 512], BF16))
    wk_sb = S1.enter_context(sb("wk_sb", [128, NKC * 512], BF16))
    wv_sb = S1.enter_context(sb("wv_sb", [128, NKC * 512], BF16))
    rc_raw = [S1.enter_context(sb(f"rc_raw{i}", [128, TTK], BF16))
              for i in range(2)]
    rs_raw = [S1.enter_context(sb(f"rs_raw{i}", [128, TTK], BF16))
              for i in range(2)]

    for nm, dst, srcw in (("wq", wq_sb, wq_t), ("wk", wk_sb, wk_t),
                          ("wv", wv_sb, wv_t)):
        dma("gpsimd", dst[:, :], srcw, writes=[nm], sem="dIg")

    dma("sync", rc_raw[0][:, :], rope2[0, :, 0:TTK], writes=["rcraw0"],
        sem="dRc")
    dma("sync", rs_raw[0][:, :], rope2[1, :, 0:TTK], writes=["rsraw0"],
        sem="dRs")
    for _b in ("gidx_sb", "masks"):
        mk.setmark(_b, "dI")
    for _b in ("wq", "wk", "wv", "ropec_t", "ropes_t"):
        mk.setmark(_b, "dIg")

    def emit_xcast(t):
        par = t % 2
        dma("gpsimd", x_bf[par][:, :], x_t[t],
            writes=[f"xbf{par}"], sem=f"dX{par}")

    NWSL = 5
    wsl = [S1.enter_context(sb(f"wsl{i}", [128, 512], F32)) for i in range(NWSL)]
    qA = [S1.enter_context(sb(f"qA{i}", [128, 512], BF16)) for i in range(2)]
    qB = S1.enter_context(sb("qB", [128, 512], BF16))

    qpieces = []
    for a in range(2):
        for cidx in range(11):
            qpieces.append((0, w1_sl[128 * a:128 * a + 128,
                                     512 * cidx:512 * cidx + 512], 512,
                            w1q_in.ap()[128 * a:128 * a + 128,
                                        512 * cidx:512 * cidx + 512], 128))
    for a in range(2):
        for cidx in range(11):
            qpieces.append((1, w2_sl[128 * a:128 * a + 128,
                                     512 * cidx:512 * cidx + 512], 512,
                            w2q_in.ap()[128 * a:128 * a + 128,
                                        512 * cidx:512 * cidx + 512], 128))
    for a in range(6):
        rows = 64 if a == 5 else 128
        for cidx in range(4):
            qpieces.append((2, w3_sl[128 * a:128 * a + 128,
                                     512 * cidx:512 * cidx + 512], 512,
                            w3q_in.ap()[128 * a:128 * a + rows,
                                        512 * cidx:512 * cidx + 512], rows))
    NPIECE = len(qpieces)

    def gamma_piece(i):
        m, srcp, width, _, _ = qpieces[i]
        par = i % NWSL
        dma("sync", wsl[par][:, 0:width], srcp, writes=[f"wsl{par}"],
            sem=f"dV{par}")
        mk.op("vector",
              lambda e, par=par, w=width: e.reduce_sum(
                  gred[:, 0:1], wsl[par][:, 0:w], axis=AXX,
                  apply_absolute_value=True),
              reads=[f"wsl{par}"], writes=["gred"])
        mk.op("vector",
              lambda e, m=m: e.tensor_add(gpart[:, m:m + 1], gpart[:, m:m + 1],
                                          gred[:, 0:1]),
              reads=["gred", "gpart"], writes=["gpart"])

    def emit_ar():
        mk.op("tensor", lambda e: e.matmul(Q[0][0:1, 0:4], ones_cf[:, :],
                                           gpart[:, :], start=True, stop=True),
              reads=["gpart", "consts"], writes=["Qb0"])
        mk.op("vector", lambda e: e.tensor_copy(R_G, Q[0][0:1, 0:4]),
              reads=["Qb0"], writes=["R_G"])
        dma("sync", g_in.ap(), R_G, reads=["R_G"], writes=["g_in"], sem="dGi")
        mk.op("gpsimd",
              lambda e: e.collective_compute(
                  "AllReduce", ALU.add, replica_groups=[list(range(8))],
                  ins=[g_in.ap().opt()], outs=[g_out.ap().opt()]),
              reads=["g_in"], writes=["g_outb"], sem="cG", inc=1)
        dma("gpsimd", R_G, g_out.ap(), reads=["g_outb", "R_G"],
            writes=["R_G"], sem="dGo")
        mk.op("gpsimd", lambda e: e.tensor_scalar(R_TT, R_G, 0.5 / NW, 0.5e-8,
                                                  ALU.mult, ALU.add),
              reads=["R_G"], writes=["R_TT"])
        mk.op("gpsimd", lambda e: e.tensor_scalar_mul(R_NT, R_TT, -1.0),
              reads=["R_TT"], writes=["R_NT"])

    def emit_thr_bcast():
        mk.op("tensor", lambda e: e.matmul(Q[0][:, 0:4], ones_rf[:, :], R_TT,
                                           start=True, stop=True),
              reads=["R_TT", "consts"], writes=["Qb0"])
        mk.op("tensor", lambda e: e.matmul(Q[0][:, 4:8], ones_rf[:, :], R_NT,
                                           start=True, stop=True),
              reads=["R_NT", "consts"], writes=["Qb0"])
        mk.op("vector", lambda e: e.tensor_copy(thr[:, 0:8], Q[0][:, 0:8]),
              reads=["Qb0"], writes=["gthr"])

    def quant_piece(i):
        m, srcp, width, dst, rows = qpieces[i]
        par = i % NWSL
        qp = i % 2
        dma("sync", wsl[par][:, 0:width], srcp, writes=[f"wsl{par}"],
            sem=f"dV{par}")
        mk.op("vector",
              lambda e, par=par, qp=qp, w=width, m=m: e.tensor_scalar(
                  qA[qp][:, 0:w], wsl[par][:, 0:w], thr[:, m:m + 1], 2.0,
                  ALU.is_gt, ALU.mult),
              reads=[f"wsl{par}", "gthr"], writes=[f"qA{qp}"])
        mk.op("vector",
              lambda e, par=par, w=width, m=m: e.tensor_scalar(
                  qB[:, 0:w], wsl[par][:, 0:w], thr[:, 4 + m:5 + m], 2.0,
                  ALU.is_gt, ALU.mult),
              reads=[f"wsl{par}", "gthr"], writes=["qB"])
        mk.op("vector",
              lambda e, qp=qp, w=width: e.scalar_tensor_tensor(
                  qA[qp][:, 0:w], qA[qp][:, 0:w], -2.0, qB[:, 0:w],
                  ALU.add, ALU.add),
              reads=[f"qA{qp}", "qB"], writes=[f"qA{qp}"])
        dma("sync", dst, qA[qp][0:rows, 0:width],
            reads=[f"qA{qp}"], writes=[f"wq_in{m}"],
            sem=("dQ" if qp == 0 else "dQ1"))

    emit_xcast(0)
    emit_xcast(1)
    qfree = []

    bg = {}

    def bg_add(t, h, fn):
        bg.setdefault((t, h), []).append(fn)

    bg_add(2, 0, emit_thr_bcast)
    gam_q = list(range(NPIECE))
    qnt_q = list(range(NPIECE))


    for j, (nm, wsb) in enumerate((("wq", wq_sb), ("wk", wk_sb), ("wv", wv_sb))):
        def wsum_mm(e, wsb=wsb):
            last = None
            for k in range(NKC):
                last = e.matmul(Q[1][0:1, 0:512], ones_cb[:, :],
                                wsb[:, 512 * k:512 * k + 512],
                                start=(k == 0), stop=(k == NKC - 1))
            return last
        mk.op("tensor", wsum_mm, reads=[nm, "consts"], writes=["Qb2"])
        mk.op("vector",
              lambda e, j=j: e.tensor_scalar_mul(negws[:, 512 * j:512 * j + 512],
                                                 Q[1][0:1, 0:512], -1.0),
              reads=["Qb2"], writes=[f"negws{j}"])

    def quiesce():
        allw = [(s, mk.cnt[s]) for s in sem_names if mk.cnt[s] > 0]
        for qn in ENGS:
            mk.wait_only(qn, waits=allw)

    def finish(nc=nc):
        mk.finalize_plan()
        with nc.Block() as blk:
            @blk.tensor
            def _(eng):
                for f in mk.plan["tensor"]:
                    f(eng)

            @blk.vector
            def _(eng):
                for f in mk.plan["vector"]:
                    f(eng)

            @blk.scalar
            def _(eng):
                for f in mk.plan["scalar"]:
                    f(eng)

            @blk.sync
            def _(eng):
                for f in mk.plan["sync"]:
                    f(eng)

            @blk.gpsimd
            def _(eng):
                for f in mk.plan["gpsimd"]:
                    f(eng)
        return nc

    if STOP <= 1:
        quiesce()
        return finish()


    def gen_qkv(t):
        par = t % 2
        xb = x_bf[par]
        xkey = f"xbf{par}"
        qsb = q_sb[par]

        if t + 1 < NT:
            nxp = (t + 1) % 2
            dma("sync", rc_raw[nxp][:, :], rope2[0, :, TTK * (t + 1):TTK * (t + 2)],
                writes=[f"rcraw{nxp}"], sem=("dRc" if nxp == 0 else "dRc1"))
            dma("sync", rs_raw[nxp][:, :], rope2[1, :, TTK * (t + 1):TTK * (t + 2)],
                writes=[f"rsraw{nxp}"], sem=("dRs" if nxp == 0 else "dRs1"))
        def sum_mm(e, xb=xb):
            last = None
            for k in range(NKC):
                last = e.matmul(Q[0][0:1, 0:512], ones_cb[:, :],
                                xb[:, 512 * k:512 * k + 512],
                                start=(k == 0), stop=(k == NKC - 1))
            return last
        mk.op("tensor", sum_mm, reads=[xkey, "consts"], writes=["Qb0"])
        yield
        for k in range(NKC):
            xp = k % 2
            mk.op("vector",
                  lambda e, xb=xb, k=k, xp=xp: e.tensor_mul(
                      xsq[xp][:, :], xb[:, 512 * k:512 * k + 512],
                      xb[:, 512 * k:512 * k + 512]),
                  reads=[xkey], writes=[f"xsq{xp}"])
            mk.op("tensor",
                  lambda e, k=k, xp=xp: e.matmul(
                      Q[0][0:1, 512:1024], ones_cb[:, :], xsq[xp][:, :],
                      start=(k == 0), stop=(k == NKC - 1)),
                  reads=[f"xsq{xp}", "consts"], writes=["Qb1"])
            if k % 4 == 3:
                yield
        for ss in (0, 1):
            def v_main(e, xb=xb, ss=ss):
                last = None
                for k in range(NKC):
                    last = e.matmul(qb(2 + ss),
                                    xb[:, 512 * k + 128 * ss:512 * k + 128 * ss + 128],
                                    wv_sb[:, 512 * k:512 * k + 512],
                                    start=(k == 0), stop=False)
                return last
            mk.op("tensor", v_main, reads=[xkey, "wv"], writes=[bkey(2 + ss)])
            yield
        mk.op("vector", lambda e: e.tensor_scalar_mul(R_M, Q[0][0:1, 0:512], 1.0 / H),
              reads=["Qb0"], writes=["R_M"])
        mk.op("vector", lambda e: e.tensor_scalar_mul(R_VAR, Q[0][0:1, 512:1024], 1.0 / H),
              reads=["Qb1"], writes=["R_VAR"])
        mk.op("vector", lambda e: e.tensor_mul(R_T1, R_M, R_M),
              reads=["R_M"], writes=["R_T1"])
        mk.op("vector", lambda e: e.tensor_sub(R_VAR, R_VAR, R_T1),
              reads=["R_VAR", "R_T1"], writes=["R_VAR"])
        mk.op("scalar", lambda e: e.activation(R_T1, R_VAR, AF.Ln,
                                               bias=epscol[0:1, 0:1], scale=1.0),
              reads=["R_VAR"], writes=["R_T1"])
        mk.op("scalar", lambda e: e.activation(R_VAR, R_T1, AF.Exp,
                                               bias=zcol[0:1, 0:1], scale=-0.5),
              reads=["R_T1"], writes=["R_VAR"])
        mk.op("vector", lambda e: e.tensor_copy(RB_M, R_M),
              reads=["R_M"], writes=["RB_M"])
        mk.op("vector", lambda e: e.tensor_copy(RB_RST, R_VAR),
              reads=["R_VAR"], writes=["RB_RST"])
        yield
        def rcol_mm(e):
            last = None
            for ss in range(4):
                last = e.matmul(Q[0][:, ss:ss + 1],
                                RB_RST[:, 128 * ss:128 * ss + 128],
                                one1b[:, :], start=True, stop=True)
            return last
        mk.op("tensor", rcol_mm, reads=["RB_RST", "consts"], writes=["Qb0"])
        mk.op("vector", lambda e: e.tensor_copy(rcolv[:, :], Q[0][:, 0:4]),
              reads=["Qb0"], writes=["rcolv"])
        mk.op("tensor", lambda e: e.matmul(Q[0][:, 512:1024], ones_rb[:, :],
                                           RB_RST, start=True, stop=True),
              reads=["RB_RST", "consts"], writes=["Qb1"])
        mk.op("vector",
              lambda e, par=par: e.tensor_mul(rc_raw[par][:, :],
                                              rc_raw[par][:, :],
                                              Q[0][:, 512:1024]),
              reads=[f"rcraw{par}", "Qb1"], writes=[f"rcraw{par}"])
        mk.op("vector",
              lambda e, par=par: e.tensor_mul(rs_raw[par][:, :],
                                              rs_raw[par][:, :],
                                              Q[0][:, 512:1024]),
              reads=[f"rsraw{par}", "Qb1"], writes=[f"rsraw{par}"])
        yield
        for ss in (0, 1):
            mk.op("tensor",
                  lambda e, ss=ss: e.matmul(qb(2 + ss),
                                            RB_M[:, 128 * ss:128 * ss + 128],
                                            negws[:, 1024:1536],
                                            start=False, stop=True),
                  reads=["RB_M", "negws2"], writes=[bkey(2 + ss)])
            gs = 4 * t + ss
            mk.op("vector",
                  lambda e, ss=ss, gs=gs: e.tensor_scalar(
                      v_sb[:, 512 * gs:512 * gs + 512], qb(2 + ss),
                      rcolv[:, ss:ss + 1], None, ALU.mult),
                  reads=[bkey(2 + ss), "rcolv"], writes=[f"v_s{gs}"])
            yield
        for ss in (2, 3):
            def v_mm2(e, xb=xb, ss=ss):
                for k in range(NKC):
                    e.matmul(qb(ss),
                             xb[:, 512 * k + 128 * ss:512 * k + 128 * ss + 128],
                             wv_sb[:, 512 * k:512 * k + 512],
                             start=(k == 0), stop=False)
                return e.matmul(qb(ss), RB_M[:, 128 * ss:128 * ss + 128],
                                negws[:, 1024:1536], start=False, stop=True)
            mk.op("tensor", v_mm2, reads=[xkey, "wv", "RB_M", "negws2"],
                  writes=[bkey(ss)])
            gs = 4 * t + ss
            mk.op("vector",
                  lambda e, ss=ss, gs=gs: e.tensor_scalar(
                      v_sb[:, 512 * gs:512 * gs + 512], qb(ss),
                      rcolv[:, ss:ss + 1], None, ALU.mult),
                  reads=[bkey(ss), "rcolv"], writes=[f"v_s{gs}"])
            yield
        for h in range(NHC):
            for iq in range(2):
                is_q = (iq == 0)
                bank = 2 * (h % 2) + iq
                wsb, wkey, nwj = ((wq_sb, "wq", 0) if is_q else (wk_sb, "wk", 1))

                def qk_mm(e, xb=xb, h=h, wsb=wsb, bank=bank, nwj=nwj):
                    for k in range(NKC):
                        e.matmul(qb(bank),
                                 wsb[:, 512 * k + 128 * h:512 * k + 128 * h + 128],
                                 xb[:, 512 * k:512 * k + 512],
                                 start=(k == 0), stop=False)
                    return e.matmul(
                        qb(bank),
                        negws[:, 512 * nwj + 128 * h:512 * nwj + 128 * h + 128],
                        RB_M, start=False, stop=True)
                mk.op("tensor", qk_mm, reads=[xkey, wkey, "RB_M", f"negws{nwj}"],
                      writes=[bkey(bank)])
                mk.op("vector",
                      lambda e, bank=bank: e.tensor_copy(tmpsw[0:64, :],
                                                         qb(bank)[64:128, :]),
                      reads=[bkey(bank)], writes=["tmpsw_a"])
                mk.op("vector",
                      lambda e, bank=bank: e.tensor_copy(tmpsw[64:128, :],
                                                         qb(bank)[0:64, :]),
                      reads=[bkey(bank)], writes=["tmpsw_b"])
                if is_q:
                    dst = qsb[:, 512 * h:512 * h + 512]
                    dkey = f"q_h{h}_p{par}"
                else:
                    dst = k_res[:, S * h + TTK * t: S * h + TTK * t + 512]
                    dkey = f"k_h{h}_t{t}"
                mk.op("vector",
                      lambda e, bank=bank, par=par: e.tensor_mul(
                          tmpA[:, :], qb(bank), rc_raw[par][:, :]),
                      reads=[bkey(bank), f"rcraw{par}"], writes=["tmpA"])
                mk.op("vector",
                      lambda e, par=par: e.tensor_mul(tmpsw[:, :], tmpsw[:, :],
                                                      rs_raw[par][:, :]),
                      reads=["tmpsw_a", "tmpsw_b", f"rsraw{par}"],
                      writes=["tmpsw_a", "tmpsw_b"])
                mk.op("vector",
                      lambda e, dst=dst: e.tensor_add(dst, tmpA[:, :],
                                                      tmpsw[:, :]),
                      reads=["tmpsw_a", "tmpsw_b", "tmpA"],
                      writes=[dkey])
                yield
            cb = 2 * (h % 2)
            mk.op("vector",
                  lambda e, h=h, qsb=qsb: e.tensor_mul(
                      sqs[:, :], qsb[:, 512 * h:512 * h + 512],
                      qsb[:, 512 * h:512 * h + 512]),
                  reads=[f"q_h{h}_p{par}"], writes=["tmpsw_a", "tmpsw_b"])
            mk.op("tensor", lambda e, cb=cb: e.matmul(
                      qrow(cb), ones_cb[:, :], sqs[:, :], start=True, stop=True),
                  reads=["tmpsw_a", "tmpsw_b", "consts"], writes=[bkey(cb)])
            mk.op("vector", lambda e, cb=cb: e.reduce_max(R_QMX, qrow(cb),
                                                          axis=AXX),
                  reads=[bkey(cb)], writes=["R_QMX"])
            mk.op("vector",
                  lambda e, h=h, t=t: e.tensor_mul(
                      sqs[:, :], k_res[:, S * h + TTK * t:S * h + TTK * t + 512],
                      k_res[:, S * h + TTK * t:S * h + TTK * t + 512]),
                  reads=[f"k_h{h}_t{t}"], writes=["tmpsw_a", "tmpsw_b"])
            mk.op("tensor", lambda e, cb=cb: e.matmul(
                      qrow(cb), ones_cb[:, :], sqs[:, :], start=True, stop=True),
                  reads=["tmpsw_a", "tmpsw_b", "consts"], writes=[bkey(cb)])
            mk.op("vector", lambda e, cb=cb: e.reduce_max(R_KMX, qrow(cb),
                                                          axis=AXX),
                  reads=[bkey(cb)], writes=["R_KMX"])
            mk.op("vector",
                  lambda e, h=h: e.tensor_max(R_KAP[:, h:h + 1],
                                              R_KAP[:, h:h + 1], R_KMX),
                  reads=["R_KMX", "kap"], writes=["kap"])
            mk.op("vector",
                  lambda e, h=h: e.tensor_mul(R_PRD, R_QMX, R_KAP[:, h:h + 1]),
                  reads=["R_QMX", "kap"], writes=["R_PRD"])
            mk.op("scalar", lambda e: e.activation(R_SQP, R_PRD, AF.Ln,
                                                   bias=epscol[0:1, 0:1],
                                                   scale=1.0),
                  reads=["R_PRD"], writes=["R_SQP"])
            mk.op("scalar", lambda e: e.activation(R_SQT, R_SQP, AF.Exp,
                                                   bias=zcol[0:1, 0:1],
                                                   scale=0.5),
                  reads=["R_SQP"], writes=["R_SQT"])
            mk.op("vector", lambda e: e.tensor_scalar_mul(RB_BV, R_SQT, -CSC),
                  reads=["R_SQT"], writes=["RB_BV"])
            mk.op("tensor", lambda e, cb=cb: e.matmul(
                      Q[cb // 2][:, 512 * (cb % 2):512 * (cb % 2) + 1],
                      ones_rb[:, :], RB_BV, start=True, stop=True),
                  reads=["RB_BV", "consts"], writes=[bkey(cb)])
            bcn = 4 * par + h
            mk.op("vector",
                  lambda e, bcn=bcn, cb=cb: e.tensor_copy(
                      biascol[:, bcn:bcn + 1],
                      Q[cb // 2][:, 512 * (cb % 2):512 * (cb % 2) + 1]),
                  reads=[bkey(cb)], writes=[f"bias_h{h}_p{par}"])
            yield

    def gen_att(ta):
        par = ta % 2
        qsb = q_sb[par]
        pending = []
        for h in range(NHC):
            kcn = 4 * ta + 4
            npair = kcn // 2

            def emit_scores(j, hh=h):
                for half in range(2):
                    kc = 2 * j + half
                    mk.op("tensor",
                          lambda e, half=half, hh=hh, kc=kc: e.matmul(
                              Q[2][:, 512 * half:512 * half + 512],
                              k_res[:, S * hh + 128 * kc:S * hh + 128 * kc + 128],
                              qsb[:, 512 * hh:512 * hh + 512],
                              start=True, stop=True),
                          reads=[f"k_h{hh}_t{kc // 4}", f"q_h{hh}_p{par}"],
                          writes=[bkey(4 + half)])
            if h == 0:
                emit_scores(0)
            for j in range(npair):
                pb = probs[j % 2]
                pkey = f"probs{j % 2}"
                bcn = 4 * par + h
                for half in range(2):
                    mk.op("scalar",
                          lambda e, pb=pb, half=half, bcn=bcn: e.activation(
                              pb[:, 512 * half:512 * half + 512],
                              Q[2][:, 512 * half:512 * half + 512], AF.Exp,
                              bias=biascol[:, bcn:bcn + 1], scale=CSC),
                          reads=[bkey(4 + half), f"bias_h{h}_p{par}"],
                          writes=[pkey])
                for half in range(2):
                    kc = 2 * j + half
                    a = kc - 4 * ta
                    if a >= 0:
                        mk.op("vector",
                              lambda e, pb=pb, half=half, a=a: e.tensor_mul(
                                  pb[:, 512 * half:512 * half + 512],
                                  pb[:, 512 * half:512 * half + 512],
                                  masks_sb[:, 512 * a:512 * a + 512]),
                              reads=[pkey, "masks"], writes=[pkey])
                if j + 1 < npair:
                    emit_scores(j + 1)
                if j == 0 and pending:
                    pending.pop(0)()
                for half in range(2):
                    kc = 2 * j + half
                    mk.op("tensor",
                          lambda e, pb=pb, half=half, st0=(kc == 0),
                          sp=(kc == kcn - 1): e.matmul(
                              qrow(7), ones_cb[:, :],
                              pb[:, 512 * half:512 * half + 512],
                              start=st0, stop=sp),
                          reads=[pkey, "consts"], writes=["Qb7"])
                    mk.op("tensor",
                          lambda e, pb=pb, half=half, kc=kc, h=h,
                          st0=(kc == 0), sp=(kc == kcn - 1): e.matmul(
                              qb(6),
                              v_sb[:, 512 * kc + 128 * h:512 * kc + 128 * h + 128],
                              pb[:, 512 * half:512 * half + 512],
                              start=st0, stop=sp),
                          reads=[pkey, f"v_s{kc}"], writes=["Qb6"])
                yield
            if h + 1 < NHC:
                emit_scores(0, h + 1)
            def tail(h=h):
                mk.op("scalar", lambda e: e.activation(
                          R_T1, qrow(7), AF.Ln, bias=zcol[0:1, 0:1], scale=1.0),
                      reads=["Qb7"], writes=["R_T1"])
                mk.op("scalar", lambda e: e.activation(
                          RB_DEN, R_T1, AF.Exp, bias=zcol[0:1, 0:1], scale=-1.0),
                      reads=["R_T1"], writes=["RB_DEN"])
                mk.op("tensor", lambda e: e.matmul(
                          qb(7), ones_rb[:, :], RB_DEN, start=True, stop=True),
                      reads=["RB_DEN", "consts"], writes=["Qb7"])
                mk.op("vector", lambda e: e.tensor_copy(rdenb_sb[0][:, :], qb(7)),
                      reads=["Qb7"], writes=["rdenb0"])
                cp = (ta % 2) * 4 + h
                cw = []
                if ta >= 2:
                    cw.append((f"dC{(ta - 2) // 2}", 128))
                mk.op("vector",
                      lambda e, cp=cp: e.tensor_mul(
                          ctxev[cp][:, :], qb(6), rdenb_sb[0][:, :]),
                      reads=["Qb6", "rdenb0"], writes=[f"ctxev{cp}"],
                      waits=cw)
                dma("sync",
                    cag_in[ta // 2].ap()[128 * h:128 * h + 128,
                                         512 * (ta % 2):512 * (ta % 2) + 512],
                    ctxev[cp][:, :],
                    reads=[f"ctxev{cp}"], writes=[f"cagin{ta // 2}"],
                    sem=f"dC{ta // 2}")
                for fn in bg.get((ta, h), ()):
                    fn()
            pending.append(tail)
            yield
        while pending:
            pending.pop(0)()

    def drain(g):
        if g is None:
            return
        for _ in g:
            pass

    for t in range(min(NT, MKTILES)):
        gq = gen_qkv(t)
        ga = gen_att(t - 1) if (t >= 1 and MKATT) else None
        rnd = 0
        while True:
            rnd += 1
            qdone = adone = False
            try:
                next(gq)
            except StopIteration:
                qdone = True
            if ga is not None:
                try:
                    next(ga)
                except StopIteration:
                    adone = True
            else:
                adone = True
            if t <= 2 and gam_q:
                gamma_piece(gam_q.pop(0))
            elif ((t == 3 and rnd >= 9) or t in (4, 5)) and qnt_q:
                quant_piece(qnt_q.pop(0))
            if qdone:
                if not adone and ga is not None:
                    drain(ga)
                break
        ta = t - 1
        if t + 2 < min(NT, MKTILES):
            emit_xcast(t + 2)
        if ta >= 1 and ta % 2 == 1 and MKATT and MKAG:
            kq = ta // 2
            mk.op("gpsimd",
                  lambda e, kq=kq: e.collective_compute(
                      "AllGather", ALU.bypass,
                      replica_groups=[[0, 1, 2, 3], [4, 5, 6, 7]],
                      ins=[cag_in[kq].ap().opt()],
                      outs=[cag_out.ap()[2048 * kq:2048 * kq + 2048, :]]),
                  reads=[f"cagin{kq}"], writes=[f"cagout{kq}"],
                  sem=f"cX{kq}", inc=1)
        if ta == 1:
            while gam_q:
                gamma_piece(gam_q.pop(0))
            emit_ar()
        if t == 5:
            while qnt_q:
                quant_piece(qnt_q.pop(0))
            mk.op("gpsimd",
                  lambda e: e.collective_compute(
                      "AllGather", ALU.bypass, replica_groups=[list(range(8))],
                      ins=[w1q_in.ap().opt()], outs=[w1q_all.ap().opt()]),
                  reads=["wq_in0"], writes=["w1q_all"], sem="cW1", inc=1,
                  waits=[("dQ", mk.cnt["dQ"]), ("dQ1", mk.cnt["dQ1"])])
        if t == 6:
            mk.op("gpsimd",
                  lambda e: e.collective_compute(
                      "AllGather", ALU.bypass, replica_groups=[list(range(8))],
                      ins=[w2q_in.ap().opt()], outs=[w2q_all.ap().opt()]),
                  reads=["wq_in1"], writes=["w2q_all"], sem="cW2", inc=1,
                  waits=[("dQ", mk.cnt["dQ"]), ("dQ1", mk.cnt["dQ1"])])
        if t == 7:
            mk.op("gpsimd",
                  lambda e: e.collective_compute(
                      "AllGather", ALU.bypass, replica_groups=[list(range(8))],
                      ins=[w3q_in.ap().opt()], outs=[w3q_all.ap().opt()]),
                  reads=["wq_in2"], writes=["w3q_all"], sem="cW3", inc=1,
                  waits=[("dQ", mk.cnt["dQ"]), ("dQ1", mk.cnt["dQ1"])])

    qkv_done = [("T", mk.cnt["T"]), ("V", mk.cnt["V"]), ("S", mk.cnt["S"]),
                ("dQ", mk.cnt["dQ"]), ("dQ1", mk.cnt["dQ1"])]
    S1.close()
    Sr = ExitStack()
    res1 = Sr.enter_context(sb("res1", [128, NKC * 1024], F32))
    Sb1a = ExitStack()
    wo_sb = [Sb1a.enter_context(sb(f"wo_sb{i}", [128, 2048], BF16))
             for i in range(2)]
    xp2_sb = [Sb1a.enter_context(sb(f"xp2_sb{i}", [128, 1024], F32))
              for i in range(2)]
    r1bf = [probs[0][:, :], probs[1][:, :]]
    r1sq = [q_sb[0][:, 0:1024], q_sb[0][:, 1024:2048]]

    def emit_wo(m):
        slot = m % 2
        dma("gpsimd", wo_sb[slot][:, :], wo_t[m],
            writes=[f"wo{slot}"], waits=qkv_done, sem=f"dO{slot}")

    def emit_xp2(m):
        par = m % 2
        dma("sync", xp2_sb[par][:, :], xp2_fm[128 * m:128 * m + 128, :],
            writes=[f"xp2{par}"], waits=qkv_done, sem=f"dP{par}")

    emit_wo(0)
    emit_wo(1)
    emit_xp2(0)
    emit_xp2(1)

    if MKATT and min(NT, MKTILES) == NT:
        drain(gen_att(NT - 1))
        mk.op("gpsimd",
              lambda e: e.collective_compute(
                  "AllGather", ALU.bypass,
                  replica_groups=[[0, 1, 2, 3], [4, 5, 6, 7]],
                  ins=[cag_in[3].ap().opt()],
                  outs=[cag_out.ap()[6144:8192, :]]),
              reads=["cagin3"], writes=["cagout3"], sem="cX3", inc=1)

    ph1_done = [("T", mk.cnt["T"]), ("V", mk.cnt["V"]), ("S", mk.cnt["S"]),
                ("dS", mk.cnt["dS"])]

    ctx_fm = k_res
    h2 = v_sb

    for k in range(NKC):
        mk.op("gpsimd",
              lambda e, k=k: e.indirect_dma_start(
                  out=ctx_fm[:, 1024 * k:1024 * k + 1024], out_offset=None,
                  in_=cag_out.ap(),
                  in_offset=IndirectOffsetOnAxis(gidx_sb[:, k:k + 1], axis=0)),
              reads=["cagout0", "cagout1", "cagout2", "cagout3", "gidx_sb"],
              writes=[f"ctxfm{k}"], waits=ph1_done, sem="dGt", inc=16)

    if os.environ.get("MKDBG"):
        dbg_ctx = nc.dram_tensor("dbg_ctx", [H, TOK2], BF16, kind="Internal")
        for k in range(NKC):
            dma("sync", dbg_ctx.ap()[128 * k:128 * k + 128, :],
                ctx_fm[:, 1024 * k:1024 * k + 1024],
                reads=[f"ctxfm{kk}" for kk in range(NKC)],
                writes=[f"dbgctx{k}"], sem="dS")

    if STOP <= 3:
        quiesce()
        return finish()

    for m in range(NKC):
        par = m % 2
        slot = m % 2
        bk0 = 2 * par
        for n in range(2):
            def ao_mm(e, n=n, slot=slot, bk0=bk0):
                last = None
                for k in range(NKC):
                    last = e.matmul(
                        qb(bk0 + n), wo_sb[slot][:, 128 * k:128 * k + 128],
                        ctx_fm[:, 1024 * k + 512 * n:1024 * k + 512 * n + 512],
                        start=(k == 0), stop=(k == NKC - 1))
                return last
            mk.op("tensor", ao_mm,
                  reads=[f"wo{slot}"] + [f"ctxfm{k}" for k in range(NKC)],
                  writes=[bkey(bk0 + n)])
        if m + 2 < NKC:
            emit_wo(m + 2)
        mk.op("vector",
              lambda e, m=m, par=par, bk0=bk0: e.tensor_add(
                  res1[:, 1024 * m:1024 * m + 1024], Q[bk0 // 2][:, :],
                  xp2_sb[par][:, :]),
              reads=[bkey(bk0), bkey(bk0 + 1), f"xp2{par}"],
              writes=[f"res1_{m}"])
        if m + 2 < NKC:
            emit_xp2(m + 2)
        mk.op("vector",
              lambda e, m=m, par=par: e.tensor_copy(
                  r1bf[par][:, :], res1[:, 1024 * m:1024 * m + 1024]),
              reads=[f"res1_{m}"], writes=[f"r1bf{par}"])
        mk.op("vector",
              lambda e, par=par: e.tensor_mul(r1sq[par][:, :], r1bf[par][:, :],
                                              r1bf[par][:, :]),
              reads=[f"r1bf{par}"], writes=[f"r1sq{par}"])
        for n in range(2):
            mk.op("tensor",
                  lambda e, par=par, n=n, m=m: e.matmul(
                      qrow(4 + n), ones_cb[:, :],
                      r1bf[par][:, 512 * n:512 * n + 512],
                      start=(m == 0), stop=(m == NKC - 1)),
                  reads=[f"r1bf{par}", "consts"], writes=[bkey(4 + n)])
            mk.op("tensor",
                  lambda e, par=par, n=n, m=m: e.matmul(
                      qrow(6 + n), ones_cb[:, :],
                      r1sq[par][:, 512 * n:512 * n + 512],
                      start=(m == 0), stop=(m == NKC - 1)),
                  reads=[f"r1sq{par}", "consts"], writes=[bkey(6 + n)])

    mk.op("vector", lambda e: e.tensor_scalar_mul(R_M2A, Q[2][0:1, :], 1.0 / H),
          reads=["Qb4", "Qb5"], writes=["R_M2"])
    mk.op("vector", lambda e: e.tensor_scalar_mul(R_V2A, Q[3][0:1, :], 1.0 / H),
          reads=["Qb6", "Qb7"], writes=["R_V2"])
    mk.op("vector", lambda e: e.tensor_mul(Q[2][0:1, :], R_M2A, R_M2A),
          reads=["R_M2"], writes=["Qb4", "Qb5"])
    mk.op("vector", lambda e: e.tensor_sub(R_V2A, R_V2A, Q[2][0:1, :]),
          reads=["R_V2", "Qb4"], writes=["R_V2"])
    mk.op("scalar", lambda e: e.activation(Q[2][0:1, :], R_V2A, AF.Ln,
                                           bias=epscol[0:1, 0:1], scale=1.0),
          reads=["R_V2"], writes=["Qb4", "Qb5"])
    mk.op("scalar", lambda e: e.activation(R_V2A, Q[2][0:1, :], AF.Exp,
                                           bias=zcol[0:1, 0:1], scale=-0.5),
          reads=["Qb4"], writes=["R_V2"])
    mk.op("vector", lambda e: e.tensor_copy(RB_R2, R_V2A),
          reads=["R_V2"], writes=["RB_R2"])
    mk.op("vector", lambda e: e.tensor_mul(R_M2A, R_M2A, R_V2A),
          reads=["R_M2", "R_V2"], writes=["R_M2"])
    mk.op("vector", lambda e: e.tensor_copy(RB_MR2, R_M2A),
          reads=["R_M2"], writes=["RB_MR2"])
    for n in range(2):
        mk.op("tensor", lambda e, n=n: e.matmul(
            qb(0 + n), ones_rb[:, :], RB_R2[:, 512 * n:512 * n + 512],
            start=True, stop=True),
              reads=["RB_R2", "consts"], writes=[bkey(n)])
        mk.op("tensor", lambda e, n=n: e.matmul(
            qb(2 + n), ones_rb[:, :], RB_MR2[:, 512 * n:512 * n + 512],
            start=True, stop=True),
              reads=["RB_MR2", "consts"], writes=[bkey(2 + n)])

    SLn2 = ExitStack()
    r2b_sb = SLn2.enter_context(sb("r2b_sb", [128, 1024], F32))
    mr2b_sb = SLn2.enter_context(sb("mr2b_sb", [128, 1024], F32))
    h2tmp = SLn2.enter_context(sb("h2tmp", [128, 1024], F32))
    mk.op("vector", lambda e: e.tensor_copy(r2b_sb[:, :], Q[0][:, :]),
          reads=["Qb0", "Qb1"], writes=["r2b"])
    mk.op("vector", lambda e: e.tensor_copy(mr2b_sb[:, :], Q[1][:, :]),
          reads=["Qb2", "Qb3"], writes=["mr2b"])
    for m in range(NKC):
        mk.op("vector",
              lambda e, m=m: e.tensor_mul(h2tmp[:, :],
                                          res1[:, 1024 * m:1024 * m + 1024],
                                          r2b_sb[:, :]),
              reads=[f"res1_{m}", "r2b"], writes=["h2tmp"])
        mk.op("vector",
              lambda e, m=m: e.tensor_sub(h2[:, 1024 * m:1024 * m + 1024],
                                          h2tmp[:, :], mr2b_sb[:, :]),
              reads=["h2tmp", "mr2b"], writes=[f"h2_{m}"])
        dma("sync", res1_dram.ap()[128 * m:128 * m + 128, :],
            res1[:, 1024 * m:1024 * m + 1024],
            reads=[f"res1_{m}"], writes=[f"r1d{m}"], sem="dR")

    for m in range(NKC):
        mk.setmark(f"r1d{m}", "dR")

    b1free = [("dR", mk.cnt["dR"]), ("T", mk.cnt["T"]), ("V", mk.cnt["V"])]

    SLn2.close()
    Sb1a.close()
    Sr.close()
    S1c.close()

    if STOP <= 4:
        quiesce()
        return finish()

    Shid = ExitStack()
    hidden = Shid.enter_context(sb("hidden", [128, 44 * 1024], BF16))
    w12 = [ctx_fm[:, 0:8192], ctx_fm[:, 8192:16384]]

    NJ = 11
    h2keys = [f"h2_{k}" for k in range(NKC)]
    for j in range(NJ):
        dma("sync", w12[0][:, :].rearrange("p (k c) -> p k c", k=NKC),
            w1q_all.ap().rearrange("(k p) h -> p k h", p=128)[
                :, :, 512 * j:512 * j + 512],
            writes=["w12b0"], sem="dF0",
            waits=([("cW1", 1)] + (b1free if j == 0 else [])))
        dma("sync", w12[1][:, :].rearrange("p (k c) -> p k c", k=NKC),
            w2q_all.ap().rearrange("(k p) h -> p k h", p=128)[
                :, :, 512 * j:512 * j + 512],
            writes=["w12b1"], sem="dF1",
            waits=([("cW2", 1)] + (b1free if j == 0 else [])))
        for cc in range(4):
            c = 4 * j + cc
            b1 = (cc % 2) * 4
            for n in range(2):
                def x1_mm(e, cc=cc, n=n, b1=b1):
                    last = None
                    for k in range(NKC):
                        last = e.matmul(
                            qb(b1 + n),
                            w12[0][:, 512 * k + 128 * cc:512 * k + 128 * cc + 128],
                            h2[:, 1024 * k + 512 * n:1024 * k + 512 * n + 512],
                            start=(k == 0), stop=(k == NKC - 1))
                    return last
                mk.op("tensor", x1_mm, reads=["w12b0"] + h2keys,
                      writes=[bkey(b1 + n)])
            mk.op("scalar",
                  lambda e, c=c, b1=b1: e.activation(
                      hidden[:, 1024 * c:1024 * c + 1024], Q[b1 // 2][:, :],
                      AF.Sigmoid, bias=zcol[:, 0:1], scale=0.5),
                  reads=[bkey(b1), bkey(b1 + 1)], writes=[f"hid_{c}"])
        for cc in range(4):
            c = 4 * j + cc
            b1 = (cc % 2) * 4
            for n in range(2):
                def x2_mm(e, cc=cc, n=n, b1=b1):
                    last = None
                    for k in range(NKC):
                        last = e.matmul(
                            qb(b1 + 2 + n),
                            w12[1][:, 512 * k + 128 * cc:512 * k + 128 * cc + 128],
                            h2[:, 1024 * k + 512 * n:1024 * k + 512 * n + 512],
                            start=(k == 0), stop=(k == NKC - 1))
                    return last
                mk.op("tensor", x2_mm, reads=["w12b1"] + h2keys,
                      writes=[bkey(b1 + 2 + n)])
            mk.op("vector",
                  lambda e, c=c, b1=b1: e.tensor_mul(
                      hidden[:, 1024 * c:1024 * c + 1024],
                      hidden[:, 1024 * c:1024 * c + 1024],
                      Q[b1 // 2 + 1][:, :]),
                  reads=[bkey(b1 + 2), bkey(b1 + 3), f"hid_{c}"],
                  writes=[f"hid_{c}"])

    w3free = [("T", mk.cnt["T"]), ("V", mk.cnt["V"]), ("S", mk.cnt["S"])]

    if STOP <= 5:
        quiesce()
        return finish()

    Sw3 = ExitStack()
    w3s = [v_sb[:, 0:44 * 256], k_res[:, 0:44 * 256]]
    r1s = [Sw3.enter_context(sb(f"r1s{i}", [128, 1024], F32)) for i in range(2)]
    outsb = [Sw3.enter_context(sb(f"outs{i}", [128, 1024], F32)) for i in range(2)]
    hidkeys = [f"hid_{k}" for k in range(44)]

    for mp in range(8):
        par = mp % 2
        for piece in range(2):
            dma("sync",
                w3s[par][:, :].rearrange("p (k c) -> p k c", k=44)[
                    :, 22 * piece:22 * piece + 22, :],
                w3q_all.ap().rearrange("(k p) d -> p k d", p=128)[
                    :, 22 * piece:22 * piece + 22, 256 * mp:256 * mp + 256],
                writes=[f"w3p{piece}_{par}"], sem=("dW3a" if par == 0 else "dW3b"),
                waits=([("cW3", 1)] + (w3free if mp < 2 else [])))
        mk.setmark(f"w3p0_{par}", "dW3a" if par == 0 else "dW3b")
        for mi in range(2):
            m = 2 * mp + mi
            b1 = (m % 2) * 4
            dma("sync", r1s[par][:, :], res1_dram.ap()[128 * m:128 * m + 128, :],
                reads=[f"r1d{m}"], writes=[f"r1s{par}"], sem=f"dL{par}",
                waits=(w3free if m < 2 else []))
            for n in range(2):
                def x3_mm(e, par=par, mi=mi, n=n, b1=b1):
                    last = None
                    for k in range(44):
                        last = e.matmul(
                            qb(b1 + n),
                            w3s[par][:, 256 * k + 128 * mi:256 * k + 128 * mi + 128],
                            hidden[:, 1024 * k + 512 * n:1024 * k + 512 * n + 512],
                            start=(k == 0), stop=(k == 43))
                    return last
                mk.op("tensor", x3_mm,
                      reads=[f"w3p0_{par}", f"w3p1_{par}"] + hidkeys,
                      writes=[bkey(b1 + n)])
            mk.op("vector",
                  lambda e, par=par, b1=b1: e.scalar_tensor_tensor(
                      outsb[par][:, :], Q[b1 // 2][:, :], 0.25, r1s[par][:, :],
                      ALU.mult, ALU.add),
                  reads=[bkey(b1), bkey(b1 + 1), f"r1s{par}"],
                  writes=[f"outs{par}"])
            dma("scalar", out_fm[128 * m:128 * m + 128, :], outsb[par][:, :],
                reads=[f"outs{par}"], writes=[f"outd{m}"], sem=f"dA{par}")

    mk.wait_only("scalar", waits=[("dA0", mk.cnt["dA0"]), ("dA1", mk.cnt["dA1"])])

    Sw3.close()
    Shid.close()
    S1b.close()
    S0.close()

    finish()
    st.close()
    return nc


_NC_CACHE = None


def _rope_tables():
    inv_freq = (1.0 / (np.float32(10000.0) **
                       (np.arange(0, HD, 2, dtype=np.float32) / np.float32(HD)))
                ).astype(np.float32)
    t = np.arange(S, dtype=np.float32)
    freqs = np.einsum("i,j->ij", t, inv_freq).astype(np.float32)
    cos = np.cos(freqs).astype(np.float32).T
    sin = np.sin(freqs).astype(np.float32).T
    ropec = np.concatenate([cos, cos], axis=0)
    ropes = np.concatenate([-sin, sin], axis=0)
    return np.ascontiguousarray(ropec), np.ascontiguousarray(ropes)


def make_in_maps(L0_attn_hidden_states, L0_attn_Wq, L0_attn_Wk, L0_attn_Wv,
                 L0_attn_Wo, L0_ffn_w1, L0_ffn_w2, L0_ffn_w3):
    x = np.asarray(L0_attn_hidden_states, dtype=np.float32)
    Wq = np.asarray(L0_attn_Wq, dtype=np.float32)
    Wk = np.asarray(L0_attn_Wk, dtype=np.float32)
    Wv = np.asarray(L0_attn_Wv, dtype=np.float32)
    Wo = np.asarray(L0_attn_Wo, dtype=np.float32)
    w1 = np.asarray(L0_ffn_w1, dtype=np.float32)
    w2 = np.asarray(L0_ffn_w2, dtype=np.float32)
    w3 = np.asarray(L0_ffn_w3, dtype=np.float32)

    ropec, ropes = _rope_tables()
    rope2b = np.ascontiguousarray(
        np.stack([ropec, ropes]).astype(ml_dtypes.bfloat16))

    masks = np.zeros((4, 128, 512), dtype=np.float32)
    qf = np.arange(512)
    kp = np.arange(128)
    for a in range(4):
        masks[a] = (qf[None, :] >= (128 * a + kp)[:, None]).astype(np.float32)
    masks_bf = masks.astype(ml_dtypes.bfloat16)

    WqT = np.ascontiguousarray(Wq.T)
    WkT = np.ascontiguousarray(Wk.T)
    WvT = np.ascontiguousarray(Wv.T)
    wo_tiled = np.ascontiguousarray(
        Wo.T.reshape(16, 128, 16, 128).transpose(2, 1, 0, 3).reshape(16, 128, 2048))
    w1T = np.ascontiguousarray(w1.T)
    w2T = np.ascontiguousarray(w2.T)
    w3T = np.ascontiguousarray(w3.T)

    xT = [np.ascontiguousarray(x[b].T) for b in range(B)]
    x_tl = [np.ascontiguousarray(
        x[b].reshape(NT, TTK, NKC, 128).transpose(0, 3, 2, 1).reshape(
            NT, 128, NKC * TTK)) for b in range(B)]

    def wtile(WT, g):
        return np.ascontiguousarray(
            WT[:, 512 * g:512 * g + 512].reshape(NKC, 128, 512)
            .transpose(1, 0, 2).reshape(128, NKC * 512))

    in_maps = []
    for c in range(8):
        b, g = c // 4, c % 4
        w3sl = np.zeros((768, H), dtype=np.float32)
        w3sl[:704] = w3T[704 * c:704 * c + 704]
        gidx = np.empty((128, 16), dtype=np.uint32)
        for k in range(16):
            gidx[:, k] = 2048 * g + 128 * k + np.arange(128)
        in_maps.append({
            "x_t": x_tl[b],
            "xp2_fm": np.ascontiguousarray(xT[b][:, 1024 * g:1024 * g + 1024]),
            "wq_t": wtile(WqT, g),
            "wk_t": wtile(WkT, g),
            "wv_t": wtile(WvT, g),
            "wo_t": wo_tiled,
            "w1_sl": np.ascontiguousarray(w1T[256 * c:256 * c + 256]),
            "w2_sl": np.ascontiguousarray(w2T[256 * c:256 * c + 256]),
            "w3_sl": w3sl,
            "rope2": rope2b,
            "masks": masks_bf,
            "gidx": gidx,
        })
    return in_maps


def kernel(L0_attn_hidden_states, L0_attn_Wq, L0_attn_Wk, L0_attn_Wv,
           L0_attn_Wo, L0_ffn_w1, L0_ffn_w2, L0_ffn_w3):
    global _NC_CACHE
    in_maps = make_in_maps(L0_attn_hidden_states, L0_attn_Wq, L0_attn_Wk,
                           L0_attn_Wv, L0_attn_Wo, L0_ffn_w1, L0_ffn_w2,
                           L0_ffn_w3)
    if _NC_CACHE is None:
        _NC_CACHE = build_program()
    nc = _NC_CACHE
    res = run_bass_kernel_spmd(nc, in_maps, core_ids=list(range(8)))

    out = np.empty((B, S, H), dtype=np.float32)
    for c in range(8):
        b, g = c // 4, c % 4
        out[b, 1024 * g:1024 * g + 1024, :] = res.results[c]["out_fm"].T
    return out


# version 26
# speedup vs baseline: 1.1130x; 1.0116x over previous
import os
import numpy as np
import ml_dtypes
from contextlib import ExitStack

import concourse.bass as bass
import concourse.mybir as mybir
from concourse.bass import IndirectOffsetOnAxis
from concourse.bass_utils import run_bass_kernel_spmd

F32 = mybir.dt.float32
BF16 = mybir.dt.bfloat16
U32 = mybir.dt.uint32
AF = mybir.ActivationFunctionType
ALU = mybir.AluOpType
AXX = mybir.AxisListType.X

B, S, H, NH, HD = 2, 4096, 2048, 16, 128
FH = 5632
TTK = 512
NT = S // TTK
NKC = H // 128
NHC = 4
TOK2 = 1024
CSC = float(1.0 / np.sqrt(np.float32(HD)))
EPS_LN = 1e-5
NW = 2048 * 5632

ENGS = ("tensor", "vector", "scalar", "sync", "gpsimd")


class MK:

    def __init__(self, nc, sems):
        self.nc = nc
        self.sems = sems
        self.cnt = {k: 0 for k in sems}
        self.plan = {e: [] for e in ENGS}
        self.last_w = {}
        self.readers = {}
        self.hw = {e: {} for e in ENGS}
        self.used = {}
        self.own = {"tensor": "T", "vector": "V", "scalar": "S",
                    "sync": None, "gpsimd": None}

    def op(self, q, fn, reads=(), writes=(), sem=None, inc=None, waits=()):
        deps = {}

        def add(s, v):
            if deps.get(s, 0) < v:
                deps[s] = v
        for b in reads:
            lw = self.last_w.get(b)
            if lw is not None:
                add(*lw)
        for b in writes:
            lw = self.last_w.get(b)
            if lw is not None:
                add(*lw)
            for s, v in self.readers.get(b, {}).items():
                add(s, v)
        for s, v in waits:
            add(s, v)
        wl = []
        for s, v in deps.items():
            if v <= self.hw[q].get(s, 0):
                continue
            wl.append((s, v))
            self.hw[q][s] = v
        if sem is None:
            sem = {"tensor": "T", "vector": "V", "scalar": "S",
                   "sync": "dS", "gpsimd": "dG"}[q]
            if inc is None:
                inc = 16 if q in ("sync", "gpsimd") else 1
        if inc is None:
            inc = 1
        self.cnt[sem] += inc
        val = self.cnt[sem]
        rec = {"q": q, "fn": fn, "wl": tuple(wl), "sem": sem, "val": val,
               "inc": inc}
        self.plan[q].append(rec)
        for s, v in wl:
            self.used.setdefault(s, set()).add(v)

        mark = (sem, val)
        for b in writes:
            self.last_w[b] = mark
            self.readers.pop(b, None)
        for b in reads:
            d = self.readers.setdefault(b, {})
            if d.get(sem, 0) < val:
                d[sem] = val
        return mark

    def setmark(self, buf, sem, val=None):
        if val is None:
            val = self.cnt[sem]
        self.last_w[buf] = (sem, val)
        self.readers.pop(buf, None)

    def wait_only(self, q, waits=()):
        self.plan[q].append({"q": q, "fn": None, "wl": tuple(waits),
                             "sem": None, "val": 0, "inc": 0})
        for s, v in waits:
            self.used.setdefault(s, set()).add(v)

    def finalize_plan(self):
        prunable = {"T", "V", "S"}
        remap = {}
        for s in prunable:
            used = self.used.get(s, set())
            kept = 0
            m = {0: 0}
            recs = []
            for qn in ENGS:
                for r in self.plan[qn]:
                    if r["sem"] == s:
                        recs.append(r)
            recs.sort(key=lambda r: r["val"])
            for r in recs:
                if r["val"] in used:
                    kept += 1
                    r["newinc"] = r["val"] - (r["val"] - kept)
                    r["keep"] = True
                else:
                    r["keep"] = False
                m[r["val"]] = kept
            remap[s] = m
        sems = self.sems
        out = {}
        for qn in ENGS:
            fns = []
            for r in self.plan[qn]:
                wl = []
                for s, v in r["wl"]:
                    if s in remap:
                        v = remap[s][v]
                        if v == 0:
                            continue
                    wl.append((s, v))
                fn, s_, inc = r["fn"], r["sem"], r["inc"]
                keep = r.get("keep", True) if s_ in ("T", "V", "S") else True

                def run(eng, wl=tuple(wl), fn=fn, s_=s_, inc=inc, keep=keep):
                    for s, v in wl:
                        eng.wait_ge(sems[s], v)
                    if fn is None:
                        return
                    inst = fn(eng)
                    if keep and s_ is not None:
                        inst.then_inc(sems[s_], inc)
                fns.append(run)
            out[qn] = fns
        self.plan = out


def build_program():
    STOP = int(os.environ.get("MKSTOP", "99"))
    MKTILES = int(os.environ.get("MKTILES", "8"))
    MKATT = int(os.environ.get("MKATT", "1"))
    MKAG = int(os.environ.get("MKAG", "1"))
    MKSUB = int(os.environ.get("MKSUB", "7"))
    MKS2 = int(os.environ.get("MKS2", "31"))
    MKS3 = int(os.environ.get("MKS3", "7"))
    nc = bass.Bass("TRN2", target_bir_lowering=False, debug=False)

    x_t = nc.dram_tensor("x_t", [NT, 128, NKC * 512], F32, kind="ExternalInput").ap()
    xp2_fm = nc.dram_tensor("xp2_fm", [H, TOK2], F32, kind="ExternalInput").ap()
    wq_t = nc.dram_tensor("wq_t", [128, NKC * 512], F32, kind="ExternalInput").ap()
    wk_t = nc.dram_tensor("wk_t", [128, NKC * 512], F32, kind="ExternalInput").ap()
    wv_t = nc.dram_tensor("wv_t", [128, NKC * 512], F32, kind="ExternalInput").ap()
    wo_t = nc.dram_tensor("wo_t", [16, 128, 2048], F32, kind="ExternalInput").ap()
    w1_sl = nc.dram_tensor("w1_sl", [256, FH], F32, kind="ExternalInput").ap()
    w2_sl = nc.dram_tensor("w2_sl", [256, FH], F32, kind="ExternalInput").ap()
    w3_sl = nc.dram_tensor("w3_sl", [768, H], F32, kind="ExternalInput").ap()
    rope2 = nc.dram_tensor("rope2", [2, 128, S], BF16, kind="ExternalInput").ap()
    masks = nc.dram_tensor("masks", [4, 128, 512], BF16, kind="ExternalInput").ap()
    gidx = nc.dram_tensor("gidx", [128, 16], U32, kind="ExternalInput").ap()
    out_fm = nc.dram_tensor("out_fm", [H, TOK2], F32, kind="ExternalOutput").ap()

    g_in = nc.dram_tensor("g_in", [1, 4], F32, kind="Internal")
    g_out = nc.dram_tensor("g_out", [1, 4], F32, kind="Internal")
    w1q_in = nc.dram_tensor("w1q_in", [256, FH], BF16, kind="Internal")
    w2q_in = nc.dram_tensor("w2q_in", [256, FH], BF16, kind="Internal")
    w3q_in = nc.dram_tensor("w3q_in", [704, H], BF16, kind="Internal")
    w1q_all = nc.dram_tensor("w1q_all", [H, FH], BF16, kind="Internal")
    w2q_all = nc.dram_tensor("w2q_all", [H, FH], BF16, kind="Internal")
    w3q_all = nc.dram_tensor("w3q_all", [FH, H], BF16, kind="Internal")
    cag_in = [nc.dram_tensor(f"cag_in{k}", [512, 1024], BF16, kind="Internal")
              for k in range(4)]
    cag_out = nc.dram_tensor("cag_out", [8192, 1024], BF16, kind="Internal")
    res1_dram = nc.dram_tensor("res1_dram", [H, TOK2], F32, kind="Internal")

    sem_names = ["T", "V", "S", "dS", "dG", "dA",
                 "cG", "cW1", "cW2", "cW3", "cX0", "cX1", "cX2", "cX3",
                 "dI", "dIg", "dGi", "dGo", "dX0", "dX1", "dW0", "dW1", "dQ",
                 "dV0", "dV1", "dV2", "dV3", "dV4", "dV5", "dV6", "dV7",
                 "dRc1", "dRs1",
                 "dC0", "dC1", "dC2", "dC3", "dGt", "dO0", "dO1",
                 "dP0", "dP1", "dR", "dF0", "dF1", "dW3a", "dW3b",
                 "dL0", "dL1", "dA0", "dA1", "dRc", "dRs", "dQ1", "dW2", "dW3"]

    st = ExitStack()
    sems = {n: st.enter_context(nc.semaphore(n)) for n in sem_names}

    Q = [st.enter_context(nc.psum_tensor(f"Q{i}", [128, 1024], F32))
         for i in range(4)]

    def qb(i):
        return Q[i // 2][:, 512 * (i % 2):512 * (i % 2) + 512]

    def qrow(i):
        return Q[i // 2][0:1, 512 * (i % 2):512 * (i % 2) + 512]

    def bkey(i):
        return f"Qb{i}"

    mk = MK(nc, sems)
    sb = nc.sbuf_tensor

    S0 = ExitStack()
    ones_cb = S0.enter_context(sb("ones_cb", [128, 1], BF16))
    ones_cf = S0.enter_context(sb("ones_cf", [128, 1], F32))
    ones_rb = S0.enter_context(sb("ones_rb", [1, 128], BF16))
    ones_rf = S0.enter_context(sb("ones_rf", [1, 128], F32))
    one1b = S0.enter_context(sb("one1b", [1, 1], BF16))
    zcol = S0.enter_context(sb("zcol", [128, 1], F32))
    epscol = S0.enter_context(sb("epscol", [128, 1], F32))
    thr = S0.enter_context(sb("thr", [128, 8], F32))
    gpart = S0.enter_context(sb("gpart", [128, 4], F32))
    gred = S0.enter_context(sb("gred", [128, 1], F32))
    biascol = S0.enter_context(sb("biascol", [128, 8], F32))
    gidx_sb = S0.enter_context(sb("gidx_sb", [128, 16], U32))
    rows_f = S0.enter_context(sb("rows_f", [1, 2048], F32))
    rows_b = S0.enter_context(sb("rows_b", [1, 2048], BF16))
    negws = S0.enter_context(sb("negws", [1, 1536], BF16))
    rdenb_sb = [S0.enter_context(sb(f"rdenb_sb{i}", [128, 512], BF16))
                for i in range(2)]
    rcolv = S0.enter_context(sb("rcolv", [128, 4], F32))

    R_M = rows_f[:, 0:512]
    R_VAR = rows_f[:, 512:1024]
    R_T1 = rows_f[:, 1024:1536]
    R_G = rows_f[:, 1536:1540]
    R_TT = rows_f[:, 1540:1544]
    R_NT = rows_f[:, 1544:1548]
    R_KAP = rows_f[:, 1548:1552]
    R_QMX = rows_f[:, 1552:1553]
    R_KMX = rows_f[:, 1553:1554]
    R_PRD = rows_f[:, 1554:1555]
    R_SQT = rows_f[:, 1555:1556]
    R_SQP = rows_f[:, 1556:1557]
    R_M2A = rows_f[:, 0:1024]
    R_V2A = rows_f[:, 1024:2048]

    RB_M = rows_b[:, 0:512]
    RB_RST = rows_b[:, 512:1024]
    RB_DEN = rows_b[:, 1024:1536]
    RB_BV = rows_b[:, 1536:1537]
    RB_R2 = rows_b[:, 0:1024]
    RB_MR2 = rows_b[:, 1024:2048]

    def dma(q, out_ap, in_ap, reads=(), writes=(), sem=None, waits=()):
        return mk.op(q, lambda e, o=out_ap, i=in_ap: e.dma_start(out=o, in_=i),
                     reads=reads, writes=writes, sem=sem,
                     inc=16 if sem else None, waits=waits)

    for buf, valv in ((ones_cb, 1.0), (ones_cf, 1.0), (ones_rb, 1.0),
                      (ones_rf, 1.0), (one1b, 1.0), (zcol, 0.0),
                      (epscol, EPS_LN)):
        mk.op("vector", lambda e, b=buf, v=valv: e.memset(b[:, :], v),
              writes=["consts"])
    mk.op("vector", lambda e: e.memset(gpart[:, :], 0.0), writes=["gpart"])
    mk.op("vector", lambda e: e.memset(R_KAP, 0.0), writes=["kap"])
    dma("sync", gidx_sb[:, :], gidx, writes=["gidx_sb"], sem="dI")

    S1b = ExitStack()
    k_res = S1b.enter_context(sb("k_res", [128, NHC * S], BF16))
    v_sb = S1b.enter_context(sb("v_sb", [128, 32 * 512], BF16))
    S1c = ExitStack()
    q_sb = [S1c.enter_context(sb(f"q_sb{i}", [128, NHC * 512], BF16))
            for i in range(2)]
    masks_sb = S1c.enter_context(sb("masks_sb", [128, 4 * 512], BF16))
    dma("sync", masks_sb[:, :].rearrange("p (m s) -> p m s", m=4),
        masks.rearrange("m p s -> p m s"), writes=["masks"], sem="dI")
    probs = [S1c.enter_context(sb(f"probs{i}", [128, 1024], BF16)) for i in range(2)]
    tmpsw = S1c.enter_context(sb("tmpsw", [128, 512], BF16))
    tmpA = S1c.enter_context(sb("tmpA", [128, 512], BF16))
    sqs = tmpsw
    xsq = [S1c.enter_context(sb(f"xsq{i}", [128, 512], BF16)) for i in range(2)]
    ctxev = [S1c.enter_context(sb(f"ctxev{i}", [128, 512], BF16)) for i in range(8)]

    S1 = ExitStack()
    x_bf = [S1.enter_context(sb(f"x_bf{i}", [128, NKC * 512], BF16)) for i in range(2)]
    wq_sb = S1.enter_context(sb("wq_sb", [128, NKC * 512], BF16))
    wk_sb = S1.enter_context(sb("wk_sb", [128, NKC * 512], BF16))
    wv_sb = S1.enter_context(sb("wv_sb", [128, NKC * 512], BF16))
    rc_raw = [S1.enter_context(sb(f"rc_raw{i}", [128, TTK], BF16))
              for i in range(2)]
    rs_raw = [S1.enter_context(sb(f"rs_raw{i}", [128, TTK], BF16))
              for i in range(2)]

    for nm, dst, srcw in (("wq", wq_sb, wq_t), ("wk", wk_sb, wk_t),
                          ("wv", wv_sb, wv_t)):
        dma("gpsimd", dst[:, :], srcw, writes=[nm], sem="dIg")

    dma("sync", rc_raw[0][:, :], rope2[0, :, 0:TTK], writes=["rcraw0"],
        sem="dRc")
    dma("sync", rs_raw[0][:, :], rope2[1, :, 0:TTK], writes=["rsraw0"],
        sem="dRs")
    for _b in ("gidx_sb", "masks"):
        mk.setmark(_b, "dI")
    for _b in ("wq", "wk", "wv", "ropec_t", "ropes_t"):
        mk.setmark(_b, "dIg")

    def emit_xcast(t):
        par = t % 2
        dma("gpsimd", x_bf[par][:, :], x_t[t],
            writes=[f"xbf{par}"], sem=f"dX{par}")

    NWSL = 5
    wsl = [S1.enter_context(sb(f"wsl{i}", [128, 512], F32)) for i in range(NWSL)]
    qA = [S1.enter_context(sb(f"qA{i}", [128, 512], BF16)) for i in range(2)]
    qB = S1.enter_context(sb("qB", [128, 512], BF16))

    qpieces = []
    for a in range(2):
        for cidx in range(11):
            qpieces.append((0, w1_sl[128 * a:128 * a + 128,
                                     512 * cidx:512 * cidx + 512], 512,
                            w1q_in.ap()[128 * a:128 * a + 128,
                                        512 * cidx:512 * cidx + 512], 128))
    for a in range(2):
        for cidx in range(11):
            qpieces.append((1, w2_sl[128 * a:128 * a + 128,
                                     512 * cidx:512 * cidx + 512], 512,
                            w2q_in.ap()[128 * a:128 * a + 128,
                                        512 * cidx:512 * cidx + 512], 128))
    for a in range(6):
        rows = 64 if a == 5 else 128
        for cidx in range(4):
            qpieces.append((2, w3_sl[128 * a:128 * a + 128,
                                     512 * cidx:512 * cidx + 512], 512,
                            w3q_in.ap()[128 * a:128 * a + rows,
                                        512 * cidx:512 * cidx + 512], rows))
    NPIECE = len(qpieces)

    def gamma_piece(i):
        m, srcp, width, _, _ = qpieces[i]
        par = i % NWSL
        dma("sync", wsl[par][:, 0:width], srcp, writes=[f"wsl{par}"],
            sem=f"dV{par}")
        mk.op("vector",
              lambda e, par=par, w=width: e.reduce_sum(
                  gred[:, 0:1], wsl[par][:, 0:w], axis=AXX,
                  apply_absolute_value=True),
              reads=[f"wsl{par}"], writes=["gred"])
        mk.op("vector",
              lambda e, m=m: e.tensor_add(gpart[:, m:m + 1], gpart[:, m:m + 1],
                                          gred[:, 0:1]),
              reads=["gred", "gpart"], writes=["gpart"])

    def emit_ar():
        mk.op("tensor", lambda e: e.matmul(Q[0][0:1, 0:4], ones_cf[:, :],
                                           gpart[:, :], start=True, stop=True),
              reads=["gpart", "consts"], writes=["Qb0"])
        mk.op("vector", lambda e: e.tensor_copy(R_G, Q[0][0:1, 0:4]),
              reads=["Qb0"], writes=["R_G"])
        dma("sync", g_in.ap(), R_G, reads=["R_G"], writes=["g_in"], sem="dGi")
        mk.op("gpsimd",
              lambda e: e.collective_compute(
                  "AllReduce", ALU.add, replica_groups=[list(range(8))],
                  ins=[g_in.ap().opt()], outs=[g_out.ap().opt()]),
              reads=["g_in"], writes=["g_outb"], sem="cG", inc=1)
        dma("gpsimd", R_G, g_out.ap(), reads=["g_outb", "R_G"],
            writes=["R_G"], sem="dGo")
        mk.op("gpsimd", lambda e: e.tensor_scalar(R_TT, R_G, 0.5 / NW, 0.5e-8,
                                                  ALU.mult, ALU.add),
              reads=["R_G"], writes=["R_TT"])
        mk.op("gpsimd", lambda e: e.tensor_scalar_mul(R_NT, R_TT, -1.0),
              reads=["R_TT"], writes=["R_NT"])

    def emit_thr_bcast():
        mk.op("tensor", lambda e: e.matmul(Q[0][:, 0:4], ones_rf[:, :], R_TT,
                                           start=True, stop=True),
              reads=["R_TT", "consts"], writes=["Qb0"])
        mk.op("tensor", lambda e: e.matmul(Q[0][:, 4:8], ones_rf[:, :], R_NT,
                                           start=True, stop=True),
              reads=["R_NT", "consts"], writes=["Qb0"])
        mk.op("vector", lambda e: e.tensor_copy(thr[:, 0:8], Q[0][:, 0:8]),
              reads=["Qb0"], writes=["gthr"])

    def quant_piece(i):
        m, srcp, width, dst, rows = qpieces[i]
        par = i % NWSL
        qp = i % 2
        dma("sync", wsl[par][:, 0:width], srcp, writes=[f"wsl{par}"],
            sem=f"dV{par}")
        mk.op("vector",
              lambda e, par=par, qp=qp, w=width, m=m: e.tensor_scalar(
                  qA[qp][:, 0:w], wsl[par][:, 0:w], thr[:, m:m + 1], 2.0,
                  ALU.is_gt, ALU.mult),
              reads=[f"wsl{par}", "gthr"], writes=[f"qA{qp}"])
        mk.op("vector",
              lambda e, par=par, w=width, m=m: e.tensor_scalar(
                  qB[:, 0:w], wsl[par][:, 0:w], thr[:, 4 + m:5 + m], 2.0,
                  ALU.is_gt, ALU.mult),
              reads=[f"wsl{par}", "gthr"], writes=["qB"])
        mk.op("vector",
              lambda e, qp=qp, w=width: e.scalar_tensor_tensor(
                  qA[qp][:, 0:w], qA[qp][:, 0:w], -2.0, qB[:, 0:w],
                  ALU.add, ALU.add),
              reads=[f"qA{qp}", "qB"], writes=[f"qA{qp}"])
        dma("sync", dst, qA[qp][0:rows, 0:width],
            reads=[f"qA{qp}"], writes=[f"wq_in{m}"],
            sem=("dQ" if qp == 0 else "dQ1"))

    emit_xcast(0)
    emit_xcast(1)
    qfree = []

    bg = {}

    def bg_add(t, h, fn):
        bg.setdefault((t, h), []).append(fn)

    bg_add(2, 0, emit_thr_bcast)
    gam_q = list(range(NPIECE))
    qnt_q = list(range(NPIECE))


    for j, (nm, wsb) in enumerate((("wq", wq_sb), ("wk", wk_sb), ("wv", wv_sb))):
        def wsum_mm(e, wsb=wsb):
            last = None
            for k in range(NKC):
                last = e.matmul(Q[1][0:1, 0:512], ones_cb[:, :],
                                wsb[:, 512 * k:512 * k + 512],
                                start=(k == 0), stop=(k == NKC - 1))
            return last
        mk.op("tensor", wsum_mm, reads=[nm, "consts"], writes=["Qb2"])
        mk.op("vector",
              lambda e, j=j: e.tensor_scalar_mul(negws[:, 512 * j:512 * j + 512],
                                                 Q[1][0:1, 0:512], -1.0),
              reads=["Qb2"], writes=[f"negws{j}"])

    def quiesce():
        allw = [(s, mk.cnt[s]) for s in sem_names if mk.cnt[s] > 0]
        for qn in ENGS:
            mk.wait_only(qn, waits=allw)

    def finish(nc=nc):
        mk.finalize_plan()
        with nc.Block() as blk:
            @blk.tensor
            def _(eng):
                for f in mk.plan["tensor"]:
                    f(eng)

            @blk.vector
            def _(eng):
                for f in mk.plan["vector"]:
                    f(eng)

            @blk.scalar
            def _(eng):
                for f in mk.plan["scalar"]:
                    f(eng)

            @blk.sync
            def _(eng):
                for f in mk.plan["sync"]:
                    f(eng)

            @blk.gpsimd
            def _(eng):
                for f in mk.plan["gpsimd"]:
                    f(eng)
        return nc

    if STOP <= 1:
        quiesce()
        return finish()


    def gen_qkv(t):
        par = t % 2
        xb = x_bf[par]
        xkey = f"xbf{par}"
        qsb = q_sb[par]

        if t + 1 < NT:
            nxp = (t + 1) % 2
            dma("sync", rc_raw[nxp][:, :], rope2[0, :, TTK * (t + 1):TTK * (t + 2)],
                writes=[f"rcraw{nxp}"], sem=("dRc" if nxp == 0 else "dRc1"))
            dma("sync", rs_raw[nxp][:, :], rope2[1, :, TTK * (t + 1):TTK * (t + 2)],
                writes=[f"rsraw{nxp}"], sem=("dRs" if nxp == 0 else "dRs1"))
        def sum_mm(e, xb=xb):
            last = None
            for k in range(NKC):
                last = e.matmul(Q[0][0:1, 0:512], ones_cb[:, :],
                                xb[:, 512 * k:512 * k + 512],
                                start=(k == 0), stop=(k == NKC - 1))
            return last
        mk.op("tensor", sum_mm, reads=[xkey, "consts"], writes=["Qb0"])
        yield
        for k in range(NKC):
            xp = k % 2
            mk.op("vector",
                  lambda e, xb=xb, k=k, xp=xp: e.tensor_mul(
                      xsq[xp][:, :], xb[:, 512 * k:512 * k + 512],
                      xb[:, 512 * k:512 * k + 512]),
                  reads=[xkey], writes=[f"xsq{xp}"])
            mk.op("tensor",
                  lambda e, k=k, xp=xp: e.matmul(
                      Q[0][0:1, 512:1024], ones_cb[:, :], xsq[xp][:, :],
                      start=(k == 0), stop=(k == NKC - 1)),
                  reads=[f"xsq{xp}", "consts"], writes=["Qb1"])
            if k % 4 == 3:
                yield
        for ss in (0, 1):
            def v_main(e, xb=xb, ss=ss):
                last = None
                for k in range(NKC):
                    last = e.matmul(qb(2 + ss),
                                    xb[:, 512 * k + 128 * ss:512 * k + 128 * ss + 128],
                                    wv_sb[:, 512 * k:512 * k + 512],
                                    start=(k == 0), stop=False)
                return last
            mk.op("tensor", v_main, reads=[xkey, "wv"], writes=[bkey(2 + ss)])
            yield
        mk.op("vector", lambda e: e.tensor_scalar_mul(R_M, Q[0][0:1, 0:512], 1.0 / H),
              reads=["Qb0"], writes=["R_M"])
        mk.op("vector", lambda e: e.tensor_scalar_mul(R_VAR, Q[0][0:1, 512:1024], 1.0 / H),
              reads=["Qb1"], writes=["R_VAR"])
        mk.op("vector", lambda e: e.tensor_mul(R_T1, R_M, R_M),
              reads=["R_M"], writes=["R_T1"])
        mk.op("vector", lambda e: e.tensor_sub(R_VAR, R_VAR, R_T1),
              reads=["R_VAR", "R_T1"], writes=["R_VAR"])
        mk.op("scalar", lambda e: e.activation(R_T1, R_VAR, AF.Ln,
                                               bias=epscol[0:1, 0:1], scale=1.0),
              reads=["R_VAR"], writes=["R_T1"])
        mk.op("scalar", lambda e: e.activation(R_VAR, R_T1, AF.Exp,
                                               bias=zcol[0:1, 0:1], scale=-0.5),
              reads=["R_T1"], writes=["R_VAR"])
        mk.op("vector", lambda e: e.tensor_copy(RB_M, R_M),
              reads=["R_M"], writes=["RB_M"])
        mk.op("vector", lambda e: e.tensor_copy(RB_RST, R_VAR),
              reads=["R_VAR"], writes=["RB_RST"])
        yield
        def rcol_mm(e):
            last = None
            for ss in range(4):
                last = e.matmul(Q[0][:, ss:ss + 1],
                                RB_RST[:, 128 * ss:128 * ss + 128],
                                one1b[:, :], start=True, stop=True)
            return last
        mk.op("tensor", rcol_mm, reads=["RB_RST", "consts"], writes=["Qb0"])
        mk.op("vector", lambda e: e.tensor_copy(rcolv[:, :], Q[0][:, 0:4]),
              reads=["Qb0"], writes=["rcolv"])
        mk.op("tensor", lambda e: e.matmul(Q[0][:, 512:1024], ones_rb[:, :],
                                           RB_RST, start=True, stop=True),
              reads=["RB_RST", "consts"], writes=["Qb1"])
        mk.op("vector",
              lambda e, par=par: e.tensor_mul(rc_raw[par][:, :],
                                              rc_raw[par][:, :],
                                              Q[0][:, 512:1024]),
              reads=[f"rcraw{par}", "Qb1"], writes=[f"rcraw{par}"])
        mk.op("vector",
              lambda e, par=par: e.tensor_mul(rs_raw[par][:, :],
                                              rs_raw[par][:, :],
                                              Q[0][:, 512:1024]),
              reads=[f"rsraw{par}", "Qb1"], writes=[f"rsraw{par}"])
        yield
        for ss in (0, 1):
            mk.op("tensor",
                  lambda e, ss=ss: e.matmul(qb(2 + ss),
                                            RB_M[:, 128 * ss:128 * ss + 128],
                                            negws[:, 1024:1536],
                                            start=False, stop=True),
                  reads=["RB_M", "negws2"], writes=[bkey(2 + ss)])
            gs = 4 * t + ss
            mk.op("vector",
                  lambda e, ss=ss, gs=gs: e.tensor_scalar(
                      v_sb[:, 512 * gs:512 * gs + 512], qb(2 + ss),
                      rcolv[:, ss:ss + 1], None, ALU.mult),
                  reads=[bkey(2 + ss), "rcolv"], writes=[f"v_s{gs}"])
            yield
        for ss in (2, 3):
            def v_mm2(e, xb=xb, ss=ss):
                for k in range(NKC):
                    e.matmul(qb(ss),
                             xb[:, 512 * k + 128 * ss:512 * k + 128 * ss + 128],
                             wv_sb[:, 512 * k:512 * k + 512],
                             start=(k == 0), stop=False)
                return e.matmul(qb(ss), RB_M[:, 128 * ss:128 * ss + 128],
                                negws[:, 1024:1536], start=False, stop=True)
            mk.op("tensor", v_mm2, reads=[xkey, "wv", "RB_M", "negws2"],
                  writes=[bkey(ss)])
            gs = 4 * t + ss
            mk.op("vector",
                  lambda e, ss=ss, gs=gs: e.tensor_scalar(
                      v_sb[:, 512 * gs:512 * gs + 512], qb(ss),
                      rcolv[:, ss:ss + 1], None, ALU.mult),
                  reads=[bkey(ss), "rcolv"], writes=[f"v_s{gs}"])
            yield
        for h in range(NHC):
            for iq in range(2):
                is_q = (iq == 0)
                bank = 2 * (h % 2) + iq
                wsb, wkey, nwj = ((wq_sb, "wq", 0) if is_q else (wk_sb, "wk", 1))

                def qk_mm(e, xb=xb, h=h, wsb=wsb, bank=bank, nwj=nwj):
                    for k in range(NKC):
                        e.matmul(qb(bank),
                                 wsb[:, 512 * k + 128 * h:512 * k + 128 * h + 128],
                                 xb[:, 512 * k:512 * k + 512],
                                 start=(k == 0), stop=False)
                    return e.matmul(
                        qb(bank),
                        negws[:, 512 * nwj + 128 * h:512 * nwj + 128 * h + 128],
                        RB_M, start=False, stop=True)
                mk.op("tensor", qk_mm, reads=[xkey, wkey, "RB_M", f"negws{nwj}"],
                      writes=[bkey(bank)])
                mk.op("vector",
                      lambda e, bank=bank: e.tensor_copy(tmpsw[0:64, :],
                                                         qb(bank)[64:128, :]),
                      reads=[bkey(bank)], writes=["tmpsw_a"])
                mk.op("vector",
                      lambda e, bank=bank: e.tensor_copy(tmpsw[64:128, :],
                                                         qb(bank)[0:64, :]),
                      reads=[bkey(bank)], writes=["tmpsw_b"])
                if is_q:
                    dst = qsb[:, 512 * h:512 * h + 512]
                    dkey = f"q_h{h}_p{par}"
                else:
                    dst = k_res[:, S * h + TTK * t: S * h + TTK * t + 512]
                    dkey = f"k_h{h}_t{t}"
                mk.op("vector",
                      lambda e, bank=bank, par=par: e.tensor_mul(
                          tmpA[:, :], qb(bank), rc_raw[par][:, :]),
                      reads=[bkey(bank), f"rcraw{par}"], writes=["tmpA"])
                mk.op("vector",
                      lambda e, par=par: e.tensor_mul(tmpsw[:, :], tmpsw[:, :],
                                                      rs_raw[par][:, :]),
                      reads=["tmpsw_a", "tmpsw_b", f"rsraw{par}"],
                      writes=["tmpsw_a", "tmpsw_b"])
                mk.op("vector",
                      lambda e, dst=dst: e.tensor_add(dst, tmpA[:, :],
                                                      tmpsw[:, :]),
                      reads=["tmpsw_a", "tmpsw_b", "tmpA"],
                      writes=[dkey])
                yield
            cb = 2 * (h % 2)
            mk.op("vector",
                  lambda e, h=h, qsb=qsb: e.tensor_mul(
                      sqs[:, :], qsb[:, 512 * h:512 * h + 512],
                      qsb[:, 512 * h:512 * h + 512]),
                  reads=[f"q_h{h}_p{par}"], writes=["tmpsw_a", "tmpsw_b"])
            mk.op("tensor", lambda e, cb=cb: e.matmul(
                      qrow(cb), ones_cb[:, :], sqs[:, :], start=True, stop=True),
                  reads=["tmpsw_a", "tmpsw_b", "consts"], writes=[bkey(cb)])
            mk.op("vector", lambda e, cb=cb: e.reduce_max(R_QMX, qrow(cb),
                                                          axis=AXX),
                  reads=[bkey(cb)], writes=["R_QMX"])
            mk.op("vector",
                  lambda e, h=h, t=t: e.tensor_mul(
                      sqs[:, :], k_res[:, S * h + TTK * t:S * h + TTK * t + 512],
                      k_res[:, S * h + TTK * t:S * h + TTK * t + 512]),
                  reads=[f"k_h{h}_t{t}"], writes=["tmpsw_a", "tmpsw_b"])
            mk.op("tensor", lambda e, cb=cb: e.matmul(
                      qrow(cb), ones_cb[:, :], sqs[:, :], start=True, stop=True),
                  reads=["tmpsw_a", "tmpsw_b", "consts"], writes=[bkey(cb)])
            mk.op("vector", lambda e, cb=cb: e.reduce_max(R_KMX, qrow(cb),
                                                          axis=AXX),
                  reads=[bkey(cb)], writes=["R_KMX"])
            mk.op("vector",
                  lambda e, h=h: e.tensor_max(R_KAP[:, h:h + 1],
                                              R_KAP[:, h:h + 1], R_KMX),
                  reads=["R_KMX", "kap"], writes=["kap"])
            mk.op("vector",
                  lambda e, h=h: e.tensor_mul(R_PRD, R_QMX, R_KAP[:, h:h + 1]),
                  reads=["R_QMX", "kap"], writes=["R_PRD"])
            mk.op("scalar", lambda e: e.activation(R_SQP, R_PRD, AF.Ln,
                                                   bias=epscol[0:1, 0:1],
                                                   scale=1.0),
                  reads=["R_PRD"], writes=["R_SQP"])
            mk.op("scalar", lambda e: e.activation(R_SQT, R_SQP, AF.Exp,
                                                   bias=zcol[0:1, 0:1],
                                                   scale=0.5),
                  reads=["R_SQP"], writes=["R_SQT"])
            mk.op("vector", lambda e: e.tensor_scalar_mul(RB_BV, R_SQT, -CSC),
                  reads=["R_SQT"], writes=["RB_BV"])
            mk.op("tensor", lambda e, cb=cb: e.matmul(
                      Q[cb // 2][:, 512 * (cb % 2):512 * (cb % 2) + 1],
                      ones_rb[:, :], RB_BV, start=True, stop=True),
                  reads=["RB_BV", "consts"], writes=[bkey(cb)])
            bcn = 4 * par + h
            mk.op("vector",
                  lambda e, bcn=bcn, cb=cb: e.tensor_copy(
                      biascol[:, bcn:bcn + 1],
                      Q[cb // 2][:, 512 * (cb % 2):512 * (cb % 2) + 1]),
                  reads=[bkey(cb)], writes=[f"bias_h{h}_p{par}"])
            yield

    def gen_att(ta):
        par = ta % 2
        qsb = q_sb[par]
        for h in range(NHC):
            kcn = 4 * ta + 4
            npair = kcn // 2

            def emit_scores(j, hh=h):
                for half in range(2):
                    kc = 2 * j + half
                    mk.op("tensor",
                          lambda e, half=half, hh=hh, kc=kc: e.matmul(
                              Q[2][:, 512 * half:512 * half + 512],
                              k_res[:, S * hh + 128 * kc:S * hh + 128 * kc + 128],
                              qsb[:, 512 * hh:512 * hh + 512],
                              start=True, stop=True),
                          reads=[f"k_h{hh}_t{kc // 4}", f"q_h{hh}_p{par}"],
                          writes=[bkey(4 + half)])
            if h == 0:
                emit_scores(0)
            for j in range(npair):
                pb = probs[j % 2]
                pkey = f"probs{j % 2}"
                bcn = 4 * par + h
                for half in range(2):
                    mk.op("scalar",
                          lambda e, pb=pb, half=half, bcn=bcn: e.activation(
                              pb[:, 512 * half:512 * half + 512],
                              Q[2][:, 512 * half:512 * half + 512], AF.Exp,
                              bias=biascol[:, bcn:bcn + 1], scale=CSC),
                          reads=[bkey(4 + half), f"bias_h{h}_p{par}"],
                          writes=[pkey])
                for half in range(2):
                    kc = 2 * j + half
                    a = kc - 4 * ta
                    if a >= 0:
                        mk.op("vector",
                              lambda e, pb=pb, half=half, a=a: e.tensor_mul(
                                  pb[:, 512 * half:512 * half + 512],
                                  pb[:, 512 * half:512 * half + 512],
                                  masks_sb[:, 512 * a:512 * a + 512]),
                              reads=[pkey, "masks"], writes=[pkey])
                if j + 1 < npair:
                    emit_scores(j + 1)
                for half in range(2):
                    kc = 2 * j + half
                    mk.op("tensor",
                          lambda e, pb=pb, half=half, st0=(kc == 0),
                          sp=(kc == kcn - 1): e.matmul(
                              qrow(7), ones_cb[:, :],
                              pb[:, 512 * half:512 * half + 512],
                              start=st0, stop=sp),
                          reads=[pkey, "consts"], writes=["Qb7"])
                    mk.op("tensor",
                          lambda e, pb=pb, half=half, kc=kc, h=h,
                          st0=(kc == 0), sp=(kc == kcn - 1): e.matmul(
                              qb(6),
                              v_sb[:, 512 * kc + 128 * h:512 * kc + 128 * h + 128],
                              pb[:, 512 * half:512 * half + 512],
                              start=st0, stop=sp),
                          reads=[pkey, f"v_s{kc}"], writes=["Qb6"])
                yield
            if h + 1 < NHC:
                emit_scores(0, h + 1)
            mk.op("scalar", lambda e: e.activation(
                      R_T1, qrow(7), AF.Ln, bias=zcol[0:1, 0:1], scale=1.0),
                  reads=["Qb7"], writes=["R_T1"])
            mk.op("scalar", lambda e: e.activation(
                      RB_DEN, R_T1, AF.Exp, bias=zcol[0:1, 0:1], scale=-1.0),
                  reads=["R_T1"], writes=["RB_DEN"])
            mk.op("tensor", lambda e: e.matmul(
                      qb(7), ones_rb[:, :], RB_DEN, start=True, stop=True),
                  reads=["RB_DEN", "consts"], writes=["Qb7"])
            mk.op("vector", lambda e: e.tensor_copy(rdenb_sb[0][:, :], qb(7)),
                  reads=["Qb7"], writes=["rdenb0"])
            cp = (ta % 2) * 4 + h
            cw = []
            if ta >= 2:
                cw.append((f"dC{(ta - 2) // 2}", 128))
            mk.op("vector",
                  lambda e, cp=cp: e.tensor_mul(
                      ctxev[cp][:, :], qb(6), rdenb_sb[0][:, :]),
                  reads=["Qb6", "rdenb0"], writes=[f"ctxev{cp}"],
                  waits=cw)
            dma("sync",
                cag_in[ta // 2].ap()[128 * h:128 * h + 128,
                                     512 * (ta % 2):512 * (ta % 2) + 512],
                ctxev[cp][:, :],
                reads=[f"ctxev{cp}"], writes=[f"cagin{ta // 2}"],
                sem=f"dC{ta // 2}")
            for fn in bg.get((ta, h), ()):
                fn()
            yield

    def drain(g):
        if g is None:
            return
        for _ in g:
            pass

    for t in range(min(NT, MKTILES)):
        gq = gen_qkv(t)
        ga = gen_att(t - 1) if (t >= 1 and MKATT) else None
        rnd = 0
        while True:
            rnd += 1
            qdone = adone = False
            try:
                next(gq)
            except StopIteration:
                qdone = True
            if ga is not None:
                try:
                    next(ga)
                except StopIteration:
                    adone = True
            else:
                adone = True
            if ga is not None and not adone and t >= 5:
                try:
                    next(ga)
                except StopIteration:
                    adone = True
            if t <= 2 and gam_q:
                gamma_piece(gam_q.pop(0))
            elif ((t == 3 and rnd >= 9) or t in (4, 5)) and qnt_q:
                quant_piece(qnt_q.pop(0))
            if qdone:
                if not adone and ga is not None:
                    drain(ga)
                break
        ta = t - 1
        if t + 2 < min(NT, MKTILES):
            emit_xcast(t + 2)
        if ta >= 1 and ta % 2 == 1 and MKATT and MKAG:
            kq = ta // 2
            mk.op("gpsimd",
                  lambda e, kq=kq: e.collective_compute(
                      "AllGather", ALU.bypass,
                      replica_groups=[[0, 1, 2, 3], [4, 5, 6, 7]],
                      ins=[cag_in[kq].ap().opt()],
                      outs=[cag_out.ap()[2048 * kq:2048 * kq + 2048, :]]),
                  reads=[f"cagin{kq}"], writes=[f"cagout{kq}"],
                  sem=f"cX{kq}", inc=1)
        if ta == 1:
            while gam_q:
                gamma_piece(gam_q.pop(0))
            emit_ar()
        if t == 5:
            while qnt_q:
                quant_piece(qnt_q.pop(0))
            mk.op("gpsimd",
                  lambda e: e.collective_compute(
                      "AllGather", ALU.bypass, replica_groups=[list(range(8))],
                      ins=[w1q_in.ap().opt()], outs=[w1q_all.ap().opt()]),
                  reads=["wq_in0"], writes=["w1q_all"], sem="cW1", inc=1,
                  waits=[("dQ", mk.cnt["dQ"]), ("dQ1", mk.cnt["dQ1"])])
        if t == 6:
            mk.op("gpsimd",
                  lambda e: e.collective_compute(
                      "AllGather", ALU.bypass, replica_groups=[list(range(8))],
                      ins=[w2q_in.ap().opt()], outs=[w2q_all.ap().opt()]),
                  reads=["wq_in1"], writes=["w2q_all"], sem="cW2", inc=1,
                  waits=[("dQ", mk.cnt["dQ"]), ("dQ1", mk.cnt["dQ1"])])
        if t == 7:
            mk.op("gpsimd",
                  lambda e: e.collective_compute(
                      "AllGather", ALU.bypass, replica_groups=[list(range(8))],
                      ins=[w3q_in.ap().opt()], outs=[w3q_all.ap().opt()]),
                  reads=["wq_in2"], writes=["w3q_all"], sem="cW3", inc=1,
                  waits=[("dQ", mk.cnt["dQ"]), ("dQ1", mk.cnt["dQ1"])])

    qkv_done = [("T", mk.cnt["T"]), ("V", mk.cnt["V"]), ("S", mk.cnt["S"]),
                ("dQ", mk.cnt["dQ"]), ("dQ1", mk.cnt["dQ1"])]
    S1.close()
    Sr = ExitStack()
    res1 = Sr.enter_context(sb("res1", [128, NKC * 1024], F32))
    Sb1a = ExitStack()
    wo_sb = [Sb1a.enter_context(sb(f"wo_sb{i}", [128, 2048], BF16))
             for i in range(2)]
    xp2_sb = [Sb1a.enter_context(sb(f"xp2_sb{i}", [128, 1024], F32))
              for i in range(2)]
    r1bf = [probs[0][:, :], probs[1][:, :]]
    r1sq = [q_sb[0][:, 0:1024], q_sb[0][:, 1024:2048]]

    def emit_wo(m):
        slot = m % 2
        dma("gpsimd", wo_sb[slot][:, :], wo_t[m],
            writes=[f"wo{slot}"], waits=qkv_done, sem=f"dO{slot}")

    def emit_xp2(m):
        par = m % 2
        dma("sync", xp2_sb[par][:, :], xp2_fm[128 * m:128 * m + 128, :],
            writes=[f"xp2{par}"], waits=qkv_done, sem=f"dP{par}")

    emit_wo(0)
    emit_wo(1)
    emit_xp2(0)
    emit_xp2(1)

    if MKATT and min(NT, MKTILES) == NT:
        drain(gen_att(NT - 1))
        mk.op("gpsimd",
              lambda e: e.collective_compute(
                  "AllGather", ALU.bypass,
                  replica_groups=[[0, 1, 2, 3], [4, 5, 6, 7]],
                  ins=[cag_in[3].ap().opt()],
                  outs=[cag_out.ap()[6144:8192, :]]),
              reads=["cagin3"], writes=["cagout3"], sem="cX3", inc=1)

    ph1_done = [("T", mk.cnt["T"]), ("V", mk.cnt["V"]), ("S", mk.cnt["S"]),
                ("dS", mk.cnt["dS"])]

    ctx_fm = k_res
    h2 = v_sb

    for k in range(NKC):
        mk.op("gpsimd",
              lambda e, k=k: e.indirect_dma_start(
                  out=ctx_fm[:, 1024 * k:1024 * k + 1024], out_offset=None,
                  in_=cag_out.ap(),
                  in_offset=IndirectOffsetOnAxis(gidx_sb[:, k:k + 1], axis=0)),
              reads=["cagout0", "cagout1", "cagout2", "cagout3", "gidx_sb"],
              writes=[f"ctxfm{k}"], waits=ph1_done, sem="dGt", inc=16)

    if os.environ.get("MKDBG"):
        dbg_ctx = nc.dram_tensor("dbg_ctx", [H, TOK2], BF16, kind="Internal")
        for k in range(NKC):
            dma("sync", dbg_ctx.ap()[128 * k:128 * k + 128, :],
                ctx_fm[:, 1024 * k:1024 * k + 1024],
                reads=[f"ctxfm{kk}" for kk in range(NKC)],
                writes=[f"dbgctx{k}"], sem="dS")

    if STOP <= 3:
        quiesce()
        return finish()

    for m in range(NKC):
        par = m % 2
        slot = m % 2
        bk0 = 2 * par
        for n in range(2):
            def ao_mm(e, n=n, slot=slot, bk0=bk0):
                last = None
                for k in range(NKC):
                    last = e.matmul(
                        qb(bk0 + n), wo_sb[slot][:, 128 * k:128 * k + 128],
                        ctx_fm[:, 1024 * k + 512 * n:1024 * k + 512 * n + 512],
                        start=(k == 0), stop=(k == NKC - 1))
                return last
            mk.op("tensor", ao_mm,
                  reads=[f"wo{slot}"] + [f"ctxfm{k}" for k in range(NKC)],
                  writes=[bkey(bk0 + n)])
        if m + 2 < NKC:
            emit_wo(m + 2)
        mk.op("vector",
              lambda e, m=m, par=par, bk0=bk0: e.tensor_add(
                  res1[:, 1024 * m:1024 * m + 1024], Q[bk0 // 2][:, :],
                  xp2_sb[par][:, :]),
              reads=[bkey(bk0), bkey(bk0 + 1), f"xp2{par}"],
              writes=[f"res1_{m}"])
        if m + 2 < NKC:
            emit_xp2(m + 2)
        mk.op("vector",
              lambda e, m=m, par=par: e.tensor_copy(
                  r1bf[par][:, :], res1[:, 1024 * m:1024 * m + 1024]),
              reads=[f"res1_{m}"], writes=[f"r1bf{par}"])
        mk.op("vector",
              lambda e, par=par: e.tensor_mul(r1sq[par][:, :], r1bf[par][:, :],
                                              r1bf[par][:, :]),
              reads=[f"r1bf{par}"], writes=[f"r1sq{par}"])
        for n in range(2):
            mk.op("tensor",
                  lambda e, par=par, n=n, m=m: e.matmul(
                      qrow(4 + n), ones_cb[:, :],
                      r1bf[par][:, 512 * n:512 * n + 512],
                      start=(m == 0), stop=(m == NKC - 1)),
                  reads=[f"r1bf{par}", "consts"], writes=[bkey(4 + n)])
            mk.op("tensor",
                  lambda e, par=par, n=n, m=m: e.matmul(
                      qrow(6 + n), ones_cb[:, :],
                      r1sq[par][:, 512 * n:512 * n + 512],
                      start=(m == 0), stop=(m == NKC - 1)),
                  reads=[f"r1sq{par}", "consts"], writes=[bkey(6 + n)])

    mk.op("vector", lambda e: e.tensor_scalar_mul(R_M2A, Q[2][0:1, :], 1.0 / H),
          reads=["Qb4", "Qb5"], writes=["R_M2"])
    mk.op("vector", lambda e: e.tensor_scalar_mul(R_V2A, Q[3][0:1, :], 1.0 / H),
          reads=["Qb6", "Qb7"], writes=["R_V2"])
    mk.op("vector", lambda e: e.tensor_mul(Q[2][0:1, :], R_M2A, R_M2A),
          reads=["R_M2"], writes=["Qb4", "Qb5"])
    mk.op("vector", lambda e: e.tensor_sub(R_V2A, R_V2A, Q[2][0:1, :]),
          reads=["R_V2", "Qb4"], writes=["R_V2"])
    mk.op("scalar", lambda e: e.activation(Q[2][0:1, :], R_V2A, AF.Ln,
                                           bias=epscol[0:1, 0:1], scale=1.0),
          reads=["R_V2"], writes=["Qb4", "Qb5"])
    mk.op("scalar", lambda e: e.activation(R_V2A, Q[2][0:1, :], AF.Exp,
                                           bias=zcol[0:1, 0:1], scale=-0.5),
          reads=["Qb4"], writes=["R_V2"])
    mk.op("vector", lambda e: e.tensor_copy(RB_R2, R_V2A),
          reads=["R_V2"], writes=["RB_R2"])
    mk.op("vector", lambda e: e.tensor_mul(R_M2A, R_M2A, R_V2A),
          reads=["R_M2", "R_V2"], writes=["R_M2"])
    mk.op("vector", lambda e: e.tensor_copy(RB_MR2, R_M2A),
          reads=["R_M2"], writes=["RB_MR2"])
    for n in range(2):
        mk.op("tensor", lambda e, n=n: e.matmul(
            qb(0 + n), ones_rb[:, :], RB_R2[:, 512 * n:512 * n + 512],
            start=True, stop=True),
              reads=["RB_R2", "consts"], writes=[bkey(n)])
        mk.op("tensor", lambda e, n=n: e.matmul(
            qb(2 + n), ones_rb[:, :], RB_MR2[:, 512 * n:512 * n + 512],
            start=True, stop=True),
              reads=["RB_MR2", "consts"], writes=[bkey(2 + n)])

    SLn2 = ExitStack()
    r2b_sb = SLn2.enter_context(sb("r2b_sb", [128, 1024], F32))
    mr2b_sb = SLn2.enter_context(sb("mr2b_sb", [128, 1024], F32))
    h2tmp = SLn2.enter_context(sb("h2tmp", [128, 1024], F32))
    mk.op("vector", lambda e: e.tensor_copy(r2b_sb[:, :], Q[0][:, :]),
          reads=["Qb0", "Qb1"], writes=["r2b"])
    mk.op("vector", lambda e: e.tensor_copy(mr2b_sb[:, :], Q[1][:, :]),
          reads=["Qb2", "Qb3"], writes=["mr2b"])
    for m in range(NKC):
        mk.op("vector",
              lambda e, m=m: e.tensor_mul(h2tmp[:, :],
                                          res1[:, 1024 * m:1024 * m + 1024],
                                          r2b_sb[:, :]),
              reads=[f"res1_{m}", "r2b"], writes=["h2tmp"])
        mk.op("vector",
              lambda e, m=m: e.tensor_sub(h2[:, 1024 * m:1024 * m + 1024],
                                          h2tmp[:, :], mr2b_sb[:, :]),
              reads=["h2tmp", "mr2b"], writes=[f"h2_{m}"])
        dma("sync", res1_dram.ap()[128 * m:128 * m + 128, :],
            res1[:, 1024 * m:1024 * m + 1024],
            reads=[f"res1_{m}"], writes=[f"r1d{m}"], sem="dR")

    for m in range(NKC):
        mk.setmark(f"r1d{m}", "dR")

    b1free = [("dR", mk.cnt["dR"]), ("T", mk.cnt["T"]), ("V", mk.cnt["V"])]

    SLn2.close()
    Sb1a.close()
    Sr.close()
    S1c.close()

    if STOP <= 4:
        quiesce()
        return finish()

    Shid = ExitStack()
    hidden = Shid.enter_context(sb("hidden", [128, 44 * 1024], BF16))
    w12 = [ctx_fm[:, 0:8192], ctx_fm[:, 8192:16384]]

    NJ = 11
    h2keys = [f"h2_{k}" for k in range(NKC)]
    for j in range(NJ):
        dma("sync", w12[0][:, :].rearrange("p (k c) -> p k c", k=NKC),
            w1q_all.ap().rearrange("(k p) h -> p k h", p=128)[
                :, :, 512 * j:512 * j + 512],
            writes=["w12b0"], sem="dF0",
            waits=([("cW1", 1)] + (b1free if j == 0 else [])))
        dma("sync", w12[1][:, :].rearrange("p (k c) -> p k c", k=NKC),
            w2q_all.ap().rearrange("(k p) h -> p k h", p=128)[
                :, :, 512 * j:512 * j + 512],
            writes=["w12b1"], sem="dF1",
            waits=([("cW2", 1)] + (b1free if j == 0 else [])))
        for cc in range(4):
            c = 4 * j + cc
            b1 = (cc % 2) * 4
            for n in range(2):
                def x1_mm(e, cc=cc, n=n, b1=b1):
                    last = None
                    for k in range(NKC):
                        last = e.matmul(
                            qb(b1 + n),
                            w12[0][:, 512 * k + 128 * cc:512 * k + 128 * cc + 128],
                            h2[:, 1024 * k + 512 * n:1024 * k + 512 * n + 512],
                            start=(k == 0), stop=(k == NKC - 1))
                    return last
                mk.op("tensor", x1_mm, reads=["w12b0"] + h2keys,
                      writes=[bkey(b1 + n)])
            mk.op("scalar",
                  lambda e, c=c, b1=b1: e.activation(
                      hidden[:, 1024 * c:1024 * c + 1024], Q[b1 // 2][:, :],
                      AF.Sigmoid, bias=zcol[:, 0:1], scale=0.5),
                  reads=[bkey(b1), bkey(b1 + 1)], writes=[f"hid_{c}"])
        for cc in range(4):
            c = 4 * j + cc
            b1 = (cc % 2) * 4
            for n in range(2):
                def x2_mm(e, cc=cc, n=n, b1=b1):
                    last = None
                    for k in range(NKC):
                        last = e.matmul(
                            qb(b1 + 2 + n),
                            w12[1][:, 512 * k + 128 * cc:512 * k + 128 * cc + 128],
                            h2[:, 1024 * k + 512 * n:1024 * k + 512 * n + 512],
                            start=(k == 0), stop=(k == NKC - 1))
                    return last
                mk.op("tensor", x2_mm, reads=["w12b1"] + h2keys,
                      writes=[bkey(b1 + 2 + n)])
            mk.op("vector",
                  lambda e, c=c, b1=b1: e.tensor_mul(
                      hidden[:, 1024 * c:1024 * c + 1024],
                      hidden[:, 1024 * c:1024 * c + 1024],
                      Q[b1 // 2 + 1][:, :]),
                  reads=[bkey(b1 + 2), bkey(b1 + 3), f"hid_{c}"],
                  writes=[f"hid_{c}"])

    w3free = [("T", mk.cnt["T"]), ("V", mk.cnt["V"]), ("S", mk.cnt["S"])]

    if STOP <= 5:
        quiesce()
        return finish()

    Sw3 = ExitStack()
    w3s = [v_sb[:, 0:44 * 256], k_res[:, 0:44 * 256]]
    r1s = [Sw3.enter_context(sb(f"r1s{i}", [128, 1024], F32)) for i in range(2)]
    outsb = [Sw3.enter_context(sb(f"outs{i}", [128, 1024], F32)) for i in range(2)]
    hidkeys = [f"hid_{k}" for k in range(44)]

    for mp in range(8):
        par = mp % 2
        for piece in range(2):
            dma("sync",
                w3s[par][:, :].rearrange("p (k c) -> p k c", k=44)[
                    :, 22 * piece:22 * piece + 22, :],
                w3q_all.ap().rearrange("(k p) d -> p k d", p=128)[
                    :, 22 * piece:22 * piece + 22, 256 * mp:256 * mp + 256],
                writes=[f"w3p{piece}_{par}"], sem=("dW3a" if par == 0 else "dW3b"),
                waits=([("cW3", 1)] + (w3free if mp < 2 else [])))
        mk.setmark(f"w3p0_{par}", "dW3a" if par == 0 else "dW3b")
        for mi in range(2):
            m = 2 * mp + mi
            b1 = (m % 2) * 4
            dma("sync", r1s[par][:, :], res1_dram.ap()[128 * m:128 * m + 128, :],
                reads=[f"r1d{m}"], writes=[f"r1s{par}"], sem=f"dL{par}",
                waits=(w3free if m < 2 else []))
            for n in range(2):
                def x3_mm(e, par=par, mi=mi, n=n, b1=b1):
                    last = None
                    for k in range(44):
                        last = e.matmul(
                            qb(b1 + n),
                            w3s[par][:, 256 * k + 128 * mi:256 * k + 128 * mi + 128],
                            hidden[:, 1024 * k + 512 * n:1024 * k + 512 * n + 512],
                            start=(k == 0), stop=(k == 43))
                    return last
                mk.op("tensor", x3_mm,
                      reads=[f"w3p0_{par}", f"w3p1_{par}"] + hidkeys,
                      writes=[bkey(b1 + n)])
            mk.op("vector",
                  lambda e, par=par, b1=b1: e.scalar_tensor_tensor(
                      outsb[par][:, :], Q[b1 // 2][:, :], 0.25, r1s[par][:, :],
                      ALU.mult, ALU.add),
                  reads=[bkey(b1), bkey(b1 + 1), f"r1s{par}"],
                  writes=[f"outs{par}"])
            dma("scalar", out_fm[128 * m:128 * m + 128, :], outsb[par][:, :],
                reads=[f"outs{par}"], writes=[f"outd{m}"], sem=f"dA{par}")

    mk.wait_only("scalar", waits=[("dA0", mk.cnt["dA0"]), ("dA1", mk.cnt["dA1"])])

    Sw3.close()
    Shid.close()
    S1b.close()
    S0.close()

    finish()
    st.close()
    return nc


_NC_CACHE = None


def _rope_tables():
    inv_freq = (1.0 / (np.float32(10000.0) **
                       (np.arange(0, HD, 2, dtype=np.float32) / np.float32(HD)))
                ).astype(np.float32)
    t = np.arange(S, dtype=np.float32)
    freqs = np.einsum("i,j->ij", t, inv_freq).astype(np.float32)
    cos = np.cos(freqs).astype(np.float32).T
    sin = np.sin(freqs).astype(np.float32).T
    ropec = np.concatenate([cos, cos], axis=0)
    ropes = np.concatenate([-sin, sin], axis=0)
    return np.ascontiguousarray(ropec), np.ascontiguousarray(ropes)


def make_in_maps(L0_attn_hidden_states, L0_attn_Wq, L0_attn_Wk, L0_attn_Wv,
                 L0_attn_Wo, L0_ffn_w1, L0_ffn_w2, L0_ffn_w3):
    x = np.asarray(L0_attn_hidden_states, dtype=np.float32)
    Wq = np.asarray(L0_attn_Wq, dtype=np.float32)
    Wk = np.asarray(L0_attn_Wk, dtype=np.float32)
    Wv = np.asarray(L0_attn_Wv, dtype=np.float32)
    Wo = np.asarray(L0_attn_Wo, dtype=np.float32)
    w1 = np.asarray(L0_ffn_w1, dtype=np.float32)
    w2 = np.asarray(L0_ffn_w2, dtype=np.float32)
    w3 = np.asarray(L0_ffn_w3, dtype=np.float32)

    ropec, ropes = _rope_tables()
    rope2b = np.ascontiguousarray(
        np.stack([ropec, ropes]).astype(ml_dtypes.bfloat16))

    masks = np.zeros((4, 128, 512), dtype=np.float32)
    qf = np.arange(512)
    kp = np.arange(128)
    for a in range(4):
        masks[a] = (qf[None, :] >= (128 * a + kp)[:, None]).astype(np.float32)
    masks_bf = masks.astype(ml_dtypes.bfloat16)

    WqT = np.ascontiguousarray(Wq.T)
    WkT = np.ascontiguousarray(Wk.T)
    WvT = np.ascontiguousarray(Wv.T)
    wo_tiled = np.ascontiguousarray(
        Wo.T.reshape(16, 128, 16, 128).transpose(2, 1, 0, 3).reshape(16, 128, 2048))
    w1T = np.ascontiguousarray(w1.T)
    w2T = np.ascontiguousarray(w2.T)
    w3T = np.ascontiguousarray(w3.T)

    xT = [np.ascontiguousarray(x[b].T) for b in range(B)]
    x_tl = [np.ascontiguousarray(
        x[b].reshape(NT, TTK, NKC, 128).transpose(0, 3, 2, 1).reshape(
            NT, 128, NKC * TTK)) for b in range(B)]

    def wtile(WT, g):
        return np.ascontiguousarray(
            WT[:, 512 * g:512 * g + 512].reshape(NKC, 128, 512)
            .transpose(1, 0, 2).reshape(128, NKC * 512))

    in_maps = []
    for c in range(8):
        b, g = c // 4, c % 4
        w3sl = np.zeros((768, H), dtype=np.float32)
        w3sl[:704] = w3T[704 * c:704 * c + 704]
        gidx = np.empty((128, 16), dtype=np.uint32)
        for k in range(16):
            gidx[:, k] = 2048 * g + 128 * k + np.arange(128)
        in_maps.append({
            "x_t": x_tl[b],
            "xp2_fm": np.ascontiguousarray(xT[b][:, 1024 * g:1024 * g + 1024]),
            "wq_t": wtile(WqT, g),
            "wk_t": wtile(WkT, g),
            "wv_t": wtile(WvT, g),
            "wo_t": wo_tiled,
            "w1_sl": np.ascontiguousarray(w1T[256 * c:256 * c + 256]),
            "w2_sl": np.ascontiguousarray(w2T[256 * c:256 * c + 256]),
            "w3_sl": w3sl,
            "rope2": rope2b,
            "masks": masks_bf,
            "gidx": gidx,
        })
    return in_maps


def kernel(L0_attn_hidden_states, L0_attn_Wq, L0_attn_Wk, L0_attn_Wv,
           L0_attn_Wo, L0_ffn_w1, L0_ffn_w2, L0_ffn_w3):
    global _NC_CACHE
    in_maps = make_in_maps(L0_attn_hidden_states, L0_attn_Wq, L0_attn_Wk,
                           L0_attn_Wv, L0_attn_Wo, L0_ffn_w1, L0_ffn_w2,
                           L0_ffn_w3)
    if _NC_CACHE is None:
        _NC_CACHE = build_program()
    nc = _NC_CACHE
    res = run_bass_kernel_spmd(nc, in_maps, core_ids=list(range(8)))

    out = np.empty((B, S, H), dtype=np.float32)
    for c in range(8):
        b, g = c // 4, c % 4
        out[b, 1024 * g:1024 * g + 1024, :] = res.results[c]["out_fm"].T
    return out
